# version 24
# baseline (speedup 1.0000x reference)
import numpy as np
import ml_dtypes
from contextlib import ExitStack

import concourse.bass as bass
import concourse.mybir as mybir
from concourse.bass_utils import run_bass_kernel_spmd

F32 = mybir.dt.float32
BF16 = mybir.dt.bfloat16
AX = mybir.AxisListType.X
AF = mybir.ActivationFunctionType
OP = mybir.AluOpType

D, H, HD, F = 1024, 16, 64, 4096
KD = D // 128
NF = F // 128
M1 = float(3 * 2**22)
M2 = float(3 * 2**51)
EPS_LN = 1e-5
EPS_Q = 1e-5
EPS_DEN = 1e-6
RINV127 = float(np.float32(1.0) / np.float32(127.0))
RINV127_LO = float(np.float32(1.0 / 127.0 - np.float64(np.float32(1.0) / np.float32(127.0))))
INV1024 = 1.0 / 1024.0
INV2P22 = 1.0 / float(2**22)


class Sem:
    def __init__(self, h):
        self.h = h
        self.n = 0

    def tick(self, k=1):
        self.n += k
        return self.n


class Prog:
    ENGINES = ("sync", "vector", "scalar", "tensor", "gpsimd")

    def __init__(self, block):
        self.block = block
        self.items = {e: [] for e in self.ENGINES}
        self.all_sems = []

    def phase_barrier(self, waits=None):
        bar = tuple(waits) if waits is not None else tuple(
            (s, s.n) for s in self.all_sems if s.n > 0)
        for eng in self.ENGINES:
            self.items[eng].append((lambda e: e.nop(), bar, None, 0))

    DRAIN_ENGINES = ("vector", "scalar", "gpsimd")

    def op(self, eng, fn, wait=(), inc=None, incn=1, drain=True):
        tick = inc.tick(incn) if inc is not None else None
        if drain and eng in self.DRAIN_ENGINES:
            def fn2(e, fn=fn):
                r = fn(e)
                e.drain()
                return r
            self.items[eng].append((fn2, tuple(wait), inc, incn))
        else:
            self.items[eng].append((fn, tuple(wait), inc, incn))
        return tick

    def dma(self, eng, out, in_, wait=(), inc=None, transpose=False):
        tick = inc.tick(16) if inc is not None else None

        def fn(e, out=out, in_=in_, tr=transpose):
            return e.dma_start(out=out, in_=in_, transpose=tr)

        self.items[eng].append((fn, tuple(wait), inc, 16))
        return tick

    def flush(self):
        for eng in self.ENGINES:
            items = self.items[eng]
            if not items:
                continue
            self.items[eng] = []
            section = getattr(self.block, eng)

            def emit(e, items=items):
                for fn, waits, inc, incn in items:
                    for (s, v) in waits:
                        e.wait_ge(s.h, v)
                    r = fn(e)
                    if inc is not None:
                        r.then_inc(inc.h, incn)

            section(emit)


def host_tables(nt=16):
    T = nt * 128
    j = np.arange(0, HD, 2, dtype=np.float32) / np.float32(HD)
    inv_freq = np.float32(1.0) / (10000.0 ** j.astype(np.float64)).astype(np.float32)
    t = np.arange(T, dtype=np.float32)
    freqs = (t[:, None] * inv_freq[None, :]).astype(np.float32)
    cos = np.cos(freqs.astype(np.float64)).astype(np.float32)
    sin = np.sin(freqs.astype(np.float64)).astype(np.float32)
    cos_t = cos.reshape(nt, 128, 32).transpose(1, 0, 2)
    sin_t = sin.reshape(nt, 128, 32).transpose(1, 0, 2)
    cos64 = np.concatenate([cos_t, cos_t], axis=2).reshape(128, nt * 64)
    sinpat = np.concatenate([-sin_t, sin_t], axis=2).reshape(128, nt * 64)
    sel2 = np.zeros((2, 128), dtype=np.float32)
    sel2[0, 0:64] = 1.0
    sel2[1, 64:128] = 1.0
    constc = np.zeros((128, 8), dtype=np.float32)
    constc[:, 0] = M1
    constc[:, 1] = -M1
    constc[:, 2] = 0.5
    constc[:, 3] = -0.5
    constc[:, 4] = 1000.0
    constc[:, 5] = M2
    constc[:, 6] = -M2
    return dict(constc=constc,
        cos64=np.ascontiguousarray(cos64), sinpat=np.ascontiguousarray(sinpat),
        ident=np.eye(128, dtype=ml_dtypes.bfloat16),
        ident_f=np.eye(128, dtype=np.float32),
        onesrow_f=np.ones((1, 128), dtype=np.float32),
        ones_col_b=np.ones((128, 1), dtype=ml_dtypes.bfloat16),
        sel2=sel2,
    )


def build_nc(nt=16, tap=False):
    T = nt * 128
    NS = max(T // 512, 1)
    SW = min(T, 512)
    nc = bass.Bass("TRN2", target_bir_lowering=False)

    x_h = nc.dram_tensor("x", [T, D], F32, kind="ExternalInput").ap()
    wqkv_h = nc.dram_tensor("w_qkv", [3 * D, D], F32, kind="ExternalInput").ap()
    wout_h = nc.dram_tensor("w_out", [D, D], F32, kind="ExternalInput").ap()
    wg_h = nc.dram_tensor("w_g", [F, D], F32, kind="ExternalInput").ap()
    wu_h = nc.dram_tensor("w_u", [F, D], F32, kind="ExternalInput").ap()
    wd_h = nc.dram_tensor("w_d", [D, F], F32, kind="ExternalInput").ap()
    tspecs = (("cos64", [128, nt * 64], F32), ("sinpat", [128, nt * 64], F32),
              ("ident", [128, 128], BF16), ("ident_f", [128, 128], F32),
              ("onesrow_f", [1, 128], F32),
              ("ones_col_b", [128, 1], BF16), ("sel2", [2, 128], F32),
              ("constc", [128, 8], F32))
    tabs = {nm: nc.dram_tensor(nm, shp, dt, kind="ExternalInput").ap()
            for nm, shp, dt in tspecs}

    out_h = nc.dram_tensor("out", [T, D], F32, kind="ExternalOutput").ap()
    skind = "ExternalOutput" if tap else "Internal"
    q1_scr = nc.dram_tensor("q1_scr", [T, D], BF16, kind=skind).ap()
    attn_scr = nc.dram_tensor("attn_scr", [T, D], BF16, kind=skind).ap()
    h_scr = nc.dram_tensor("h_scr", [T, F], F32, kind=skind).ap()
    wdt_scr = nc.dram_tensor("wdt_scr", [D, F], BF16, kind="Internal").ap()

    es = ExitStack()
    with es:
        block = es.enter_context(nc.Block())
        P = Prog(block)
        def mksem(name):
            s = Sem(es.enter_context(nc.semaphore(name)))
            P.all_sems.append(s)
            return s

        def mksems(name, n):
            return [mksem(f"{name}{k}") for k in range(n)]

        s_tab, s_q1l = mksem("s_tab"), mksem("s_q1l")
        s_x = mksems("s_x", 3)
        s_wsl = mksems("s_wsl", 4)
        s_msl = mksems("s_msl", 4)
        s_wo = mksems("s_wo", 2)
        s_q1s = mksems("s_q1s", 3)
        s_gu = mksems("s_gu", 4)
        s_wd = mksems("s_wd", 2)
        s_hw = mksems("s_hw", 3)
        s_hl = mksems("s_hl", 8)
        s_at = mksems("s_at", 2)
        s_atl = mksems("s_atl", 2)
        s_xr = mksems("s_xr", 2)
        s_out = mksems("s_out", 2)
        s_wts = mksems("s_wts", 2)
        s_wdl = mksems("s_wdl", 1)[0]
        c_v, c_s, c_t, c_g = mksem("c_v"), mksem("c_s"), mksem("c_t"), mksem("c_g")

        per = ExitStack()
        es.enter_context(per)
        tb = {nm: per.enter_context(nc.sbuf_tensor("t_" + nm, shp, dt))
              for nm, shp, dt in tspecs}
        invsc127 = per.enter_context(nc.sbuf_tensor("invsc127", [128, nt], F32))
        scx_sb = per.enter_context(nc.sbuf_tensor("scx_sb", [128, nt], F32))
        hmax_sb = per.enter_context(nc.sbuf_tensor("hmax_sb", [128, nt], F32))
        wbias_n = per.enter_context(nc.sbuf_tensor("wbias_n", [128, 3], F32))
        wbias_p = per.enter_context(nc.sbuf_tensor("wbias_p", [128, 3], F32))
        wmean_b = per.enter_context(nc.sbuf_tensor("wmean_b", [128, 3], F32))
        ag_sb = per.enter_context(nc.sbuf_tensor("ag_sb", [128, nt], F32))
        au_sb = per.enter_context(nc.sbuf_tensor("au_sb", [128, nt], F32))
        ad_sb = per.enter_context(nc.sbuf_tensor("ad_sb", [128, nt], F32))
        sch_sb = per.enter_context(nc.sbuf_tensor("sch_sb", [128, nt], F32))
        stat1 = per.enter_context(nc.sbuf_tensor("stat1", [1, 16], F32))

        for nm in tb:
            P.dma("sync", out=tb[nm][:], in_=tabs[nm][:, :], inc=s_tab)
        tTAB = s_tab.n
        ident = tb["ident"]

        def transpose8(psbuf, src_ap_fn, nblk, wfirst):
            for c in range(nblk):
                P.op("tensor", lambda t, c=c: t.transpose(
                    out=psbuf[:, c * 128:(c + 1) * 128], in_=src_ap_fn(c),
                    identity=ident[:]),
                    wait=wfirst if c == 0 else (), inc=c_t if c == nblk - 1 else None)
            return c_t.n

        right = ExitStack()
        es.enter_context(right)
        xn_fm = right.enter_context(nc.sbuf_tensor("xn_fm", [128, KD * T], BF16, side="right"))
        wqkvT_q = right.enter_context(nc.sbuf_tensor("wqkvT_q", [128, KD * D], BF16, side="right"))
        rkv = ExitStack()
        es.enter_context(rkv)
        wqkvT_kv = rkv.enter_context(nc.sbuf_tensor("wqkvT_kv", [128, KD * 2 * D], BF16, side="right"))

        pm = ExitStack()
        m_wsl = [pm.enter_context(nc.sbuf_tensor(f"m_wsl{k}", [128, 512], F32)) for k in range(4)]
        mcols = [pm.enter_context(nc.sbuf_tensor(f"mcols{k}", [128, 64], F32)) for k in range(3)]
        rscol = pm.enter_context(nc.sbuf_tensor("rscol", [128, 3], F32))
        mrow = pm.enter_context(nc.sbuf_tensor("mrow", [1, 384], F32))

        MG = float(1.5 * 2**24 * 2**-6)
        mci = {"g": 0, "u": 1, "d": 2}
        mean_ready = {}
        _mp = dict(prev_ld=[0, 0, 0, 0], tick=0, done=0)
        _mp_jobs = []
        for mi, (nm, hsrc) in enumerate((("g", wg_h), ("u", wu_h), ("d", wd_h))):
            for j in range(64):
                _mp_jobs.append((mci[nm], j, hsrc, nm))

        def emit_mean_jobs(njobs):
            while njobs > 0 and _mp["done"] < len(_mp_jobs):
                ci, j, hsrc, nm = _mp_jobs[_mp["done"]]
                bs = _mp["done"] % 4
                if nm == "d":
                    r0, c0 = (j // 8) * 128, (j % 8) * 512
                    src_ap = wd_h[r0:r0 + 128, c0:c0 + 512]
                else:
                    src_ap = hsrc[(j // 2) * 128:(j // 2 + 1) * 128,
                                  (j % 2) * 512:(j % 2 + 1) * 512]
                wld = [(c_s, _mp["prev_ld"][bs])] if _mp["prev_ld"][bs] else []
                tl = P.dma("gpsimd", out=m_wsl[bs][:], in_=src_ap, wait=wld, inc=s_msl[bs])
                ab = P.op("scalar", lambda s, bs=bs, ci=ci, j=j: s.activation(
                    out=m_wsl[bs][:], in_=m_wsl[bs][:], func=AF.Abs,
                    accum_out=mcols[ci][:, j:j + 1]), wait=[(s_msl[bs], tl)], inc=c_s)
                _mp["prev_ld"][bs] = ab
                _mp["done"] += 1
                njobs -= 1
                if _mp["done"] % 64 == 0:
                    ci_done = _mp_jobs[_mp["done"] - 1][0]
                    P.op("vector", lambda v, ci=ci_done: v.reduce_sum(
                        out=rscol[:, ci:ci + 1], in_=mcols[ci][:, :], axis=AX),
                        wait=[(c_s, ab)], inc=c_v)
                    _mp["tick"] = c_v.n


        pa = ExitStack()
        xtile = [pa.enter_context(nc.sbuf_tensor(f"xt{k}", [128, D], F32)) for k in range(3)]
        xnf = [pa.enter_context(nc.sbuf_tensor(f"xnf{k}", [128, D], F32)) for k in range(3)]
        xnb = [pa.enter_context(nc.sbuf_tensor(f"xnb{k}", [128, D], BF16)) for k in range(3)]
        q1t = [pa.enter_context(nc.sbuf_tensor(f"q1t{k}", [128, D], BF16)) for k in range(3)]
        y0 = [pa.enter_context(nc.sbuf_tensor(f"y0{k}", [128, D], F32)) for k in range(3)]
        y1 = [pa.enter_context(nc.sbuf_tensor(f"y1{k}", [128, D], F32)) for k in range(3)]
        y2 = [pa.enter_context(nc.sbuf_tensor(f"y2{k}", [128, D], F32)) for k in range(3)]
        colA = [pa.enter_context(nc.sbuf_tensor(f"colA{k}", [128, 8], F32)) for k in range(3)]
        wsl = [pa.enter_context(nc.sbuf_tensor(f"wsl{k}", [128, D], F32)) for k in range(4)]
        wslb = [pa.enter_context(nc.sbuf_tensor(f"wslb{k}", [128, D], BF16)) for k in range(2)]

        psTw = [pa.enter_context(nc.psum_tensor(f"psTw{k}", [128, 1024], BF16)) for k in range(2)]
        psTx = [pa.enter_context(nc.psum_tensor(f"psTx{k}", [128, 1024], BF16)) for k in range(3)]


        wq_cast = [0] * 24
        wq_tr = [0] * 24
        wq_ev = [0] * 24
        for j in range(24):
            w = [(c_s, wq_cast[j - 4])] if j >= 4 else ()
            tl = P.dma("sync", out=wsl[j % 4][:], in_=wqkv_h[j * 128:(j + 1) * 128, :],
                       wait=w, inc=s_wsl[j % 4])
            w2 = [(s_wsl[j % 4], tl)]
            if j >= 2:
                w2.append((c_t, wq_tr[j - 2]))
            wq_cast[j] = P.op("scalar",
                              lambda s, j=j: s.copy(out=wslb[j % 2][:], in_=wsl[j % 4][:]),
                              wait=w2, inc=c_s)
            w3 = [(c_s, wq_cast[j]), (s_tab, tTAB)]
            if j >= 2:
                w3.append((c_v, wq_ev[j - 2]))
            wq_tr[j] = transpose8(psTw[j % 2],
                                  lambda c, j=j: wslb[j % 2][:, c * 128:(c + 1) * 128], 8, w3)
            if j < 8:
                dst, W, off = wqkvT_q, D, j * 128
            else:
                dst, W, off = wqkvT_kv, 2 * D, (j - 8) * 128
            wq_ev[j] = P.op(
                "vector",
                lambda v, j=j, dst=dst, W=W, off=off: v.tensor_copy(
                    out=dst[:].rearrange("p (c r) -> p c r", c=8, r=W)[:, :, off:off + 128],
                    in_=psTw[j % 2][:]),
                wait=[(c_t, wq_tr[j])], inc=c_v)
        tWQ_Q = wq_ev[7]
        tWQ_KV = wq_ev[23]

        xn_ready = [0] * nt
        q1_done = [0] * nt
        ev_x = [0] * nt
        xnT = [0] * nt
        yq_t = [0] * nt
        r2_t = [0] * nt
        st_q1 = [0] * nt
        xb_t = [0] * nt
        for i in range(nt):
            k = i % 3
            ca = colA[k]
            w = [(c_s, xn_ready[i - 3])] if i >= 3 else ()
            tl = P.dma("sync", out=xtile[k][:], in_=x_h[i * 128:(i + 1) * 128, :],
                       wait=w, inc=s_x[k])
            wsq = [(s_x[k], tl)]
            if i >= 3:
                wsq.append((c_v, r2_t[i - 3]))
            s2 = P.op("scalar",
                      lambda s, k=k, ca=ca: s.activation(out=y2[k][:], in_=xtile[k][:],
                                                         func=AF.Square, accum_out=ca[:, 1:2]),
                      wait=wsq, inc=c_s)
            P.op("vector", lambda v, k=k, ca=ca: v.reduce_sum(out=ca[:, 0:1], in_=xtile[k][:], axis=AX),
                 wait=[(c_s, s2)])
            P.op("vector", lambda v, ca=ca: v.tensor_scalar(out=ca[:, 2:3], in0=ca[:, 0:1],
                                                            scalar1=INV1024, scalar2=None, op0=OP.mult))
            P.op("vector", lambda v, ca=ca: v.tensor_scalar(out=ca[:, 3:4], in0=ca[:, 1:2],
                                                            scalar1=INV1024, scalar2=None, op0=OP.mult))
            P.op("vector", lambda v, ca=ca: v.scalar_tensor_tensor(
                out=ca[:, 4:5], in0=ca[:, 2:3], scalar=ca[:, 2:3], in1=ca[:, 3:4],
                op0=OP.mult, op1=OP.subtract))
            P.op("vector", lambda v, ca=ca: v.tensor_scalar(
                out=ca[:, 4:5], in0=ca[:, 4:5], scalar1=-1.0, scalar2=EPS_LN,
                op0=OP.mult, op1=OP.add))
            rc = P.op("vector", lambda v, ca=ca: v.reciprocal(out=ca[:, 5:6], in_=ca[:, 4:5]),
                      inc=c_v)
            sq = P.op("scalar", lambda s, ca=ca: s.activation(out=ca[:, 6:7], in_=ca[:, 5:6],
                                                              func=AF.Sqrt),
                      wait=[(c_v, rc)], inc=c_s)
            nb = P.op("vector", lambda v, ca=ca: v.scalar_tensor_tensor(
                out=ca[:, 7:8], in0=ca[:, 2:3], scalar=-1.0, in1=ca[:, 6:7],
                op0=OP.mult, op1=OP.mult), wait=[(c_s, sq)], inc=c_v)
            wxn = [(c_v, nb)]
            if i >= 3:
                wxn.append((c_v, yq_t[i - 3]))
            xn_ready[i] = P.op("scalar", lambda s, k=k, ca=ca: s.activation(
                out=xnf[k][:], in_=xtile[k][:], func=AF.Identity,
                bias=ca[:, 7:8], scale=ca[:, 6:7]), wait=wxn, inc=c_s)
            P.op("vector", lambda v, k=k, ca=ca: v.reduce_max(
                out=ca[:, 0:1], in_=xnf[k][:], axis=AX, apply_absolute_value=True),
                 wait=[(c_s, xn_ready[i])])
            P.op("vector", lambda v, ca=ca: v.tensor_scalar(
                out=ca[:, 1:2], in0=ca[:, 0:1], scalar1=EPS_Q, scalar2=None, op0=OP.max))
            P.op("vector", lambda v, ca=ca: v.reciprocal(out=ca[:, 2:3], in_=ca[:, 1:2]))
            P.op("vector", lambda v, ca=ca, i=i: v.tensor_scalar(
                out=scx_sb[:, i:i + 1], in0=ca[:, 2:3], scalar1=127.0, scalar2=None, op0=OP.mult))
            P.op("vector", lambda v, ca=ca, i=i: v.tensor_scalar(
                out=invsc127[:, i:i + 1], in0=ca[:, 1:2], scalar1=RINV127, scalar2=None, op0=OP.mult))
            P.op("vector", lambda v, ca=ca, i=i: v.scalar_tensor_tensor(
                out=invsc127[:, i:i + 1], in0=ca[:, 1:2], scalar=RINV127_LO,
                in1=invsc127[:, i:i + 1], op0=OP.mult, op1=OP.add))
            wxb = [(c_s, xn_ready[i])]
            if i >= 3:
                wxb.append((c_t, xnT[i - 3]))
            xb_t[i] = P.op("scalar", lambda s, k=k: s.copy(out=xnb[k][:], in_=xnf[k][:]),
                           wait=wxb, inc=c_s)
            wy = [(c_v, q1_done[i - 3])] if i >= 3 else ()
            yq_t[i] = P.op("vector", lambda v, k=k, i=i: v.tensor_scalar(
                out=y0[k][:], in0=xnf[k][:], scalar1=scx_sb[:, i:i + 1], scalar2=1000.0,
                op0=OP.mult, op1=OP.add), wait=wy, inc=c_v)
            P.op("vector", lambda v, k=k: v.tensor_scalar(
                out=y1[k][:], in0=y0[k][:], scalar1=M1, scalar2=M1, op0=OP.add, op1=OP.subtract))
            r2_t[i] = P.op("vector", lambda v, k=k: v.tensor_scalar(
                out=y2[k][:], in0=y0[k][:], scalar1=M2, scalar2=M2, op0=OP.add, op1=OP.subtract),
                inc=c_v)
            P.op("vector", lambda v, k=k: v.scalar_tensor_tensor(
                out=y0[k][:], in0=y2[k][:], scalar=1.0, in1=y1[k][:], op0=OP.is_lt, op1=OP.mult))
            wq1 = [(s_q1s[k], st_q1[i - 3])] if i >= 3 else ()
            q1_done[i] = P.op("vector", lambda v, k=k: v.scalar_tensor_tensor(
                out=q1t[k][:], in0=y0[k][:], scalar=1000.0, in1=y2[k][:],
                op0=OP.subtract, op1=OP.add), wait=wq1, inc=c_v)
            st_q1[i] = P.dma("gpsimd", out=q1_scr[i * 128:(i + 1) * 128, :], in_=q1t[k][:],
                             wait=[(c_v, q1_done[i])], inc=s_q1s[k])
            wtr = [(c_s, xb_t[i]), (s_tab, tTAB)]
            if i >= 3:
                wtr.append((c_v, ev_x[i - 3]))
            xnT[i] = transpose8(psTx[k], lambda c, k=k: xnb[k][:, c * 128:(c + 1) * 128], 8, wtr)
            ev_x[i] = P.op("vector", lambda v, k=k, i=i: v.tensor_copy(
                out=xn_fm[:].rearrange("p (c r) -> p c r", c=8, r=T)[:, :, i * 128:(i + 1) * 128],
                in_=psTx[k][:]), wait=[(c_t, xnT[i])], inc=c_v)

        P.flush()
        pa.close()

        P.phase_barrier()

        pbL = ExitStack()
        kf_sb = pbL.enter_context(nc.sbuf_tensor("kf_sb", [128, nt * D], BF16))
        v_sb = pbL.enter_context(nc.sbuf_tensor("v_sb", [128, nt * D], BF16))
        qf_sb = pbL.enter_context(nc.sbuf_tensor("qf_sb", [128, KD * T], BF16))

        pbM = ExitStack()
        rs1 = pbM.enter_context(nc.sbuf_tensor("rs1", [128, D], F32))
        rs2 = pbM.enter_context(nc.sbuf_tensor("rs2", [128, D], F32))
        qs1, qs2 = rs1, rs2
        qfb0 = pbM.enter_context(nc.sbuf_tensor("qfb0", [128, D], BF16))
        qfb = [qfb0, qfb0]
        psK = pbM.enter_context(nc.psum_tensor("psK", [128, 1024], F32))
        psV = pbM.enter_context(nc.psum_tensor("psV", [128, 1024], F32))
        psQ = pbM.enter_context(nc.psum_tensor("psQ", [128, 1024], F32))
        psTq = pbM.enter_context(nc.psum_tensor("psTq", [128, 1024], BF16))

        def swap_ap(buf):
            return buf[:].rearrange("p (h two j) -> p h two j", two=2, j=32)[:, :, ::-1, :]

        def full_ap(buf):
            return buf[:].rearrange("p (h two j) -> p h two j", two=2, j=32)

        def rtab64(tbl, i):
            return tbl[:, i * 64:(i + 1) * 64].rearrange("p (o two j) -> p o two j", o=1, two=2) \
                      .broadcast_to([128, H, 2, 32])

        kft_t = [0] * nt
        qf_t = [0] * nt
        vev_t = [0] * nt
        krd_t = [0] * nt
        qrd_t = [0] * nt
        qT_t = [0] * nt
        qev_t = [0] * nt

        def emit_qtr(i):
            wtr = [(c_v, qf_t[i])]
            if i >= 1:
                wtr.append((c_s, qev_t[i - 1]))
            qT_t[i] = transpose8(psTq, lambda c, i=i: qfb[i % 2][:, c * 128:(c + 1) * 128],
                                 8, wtr)
            qev_t[i] = P.op("scalar", lambda s, i=i: s.copy(
                out=qf_sb[:].rearrange("p (c r) -> p c r", c=8, r=T)[:, :, i * 128:(i + 1) * 128],
                in_=psTq[:]), wait=[(c_t, qT_t[i])], inc=c_s)

        def mm_group(ps, rhs_buf, rhs_stride, rhs_off, wfirst, i):
            for sh in range(2):
                for c in range(KD):
                    P.op("tensor", lambda t, i=i, sh=sh, c=c, ps=ps, rb=rhs_buf,
                         rs=rhs_stride, ro=rhs_off: t.matmul(
                        out=ps[:, sh * 512:(sh + 1) * 512],
                        lhsT=xn_fm[:, c * T + i * 128: c * T + (i + 1) * 128],
                        rhs=rb[:, c * rs + ro + sh * 512: c * rs + ro + (sh + 1) * 512],
                        start=(c == 0), stop=(c == KD - 1)),
                        wait=wfirst if (sh == 0 and c == 0) else (),
                        inc=c_t if (sh == 1 and c == KD - 1) else None)
            return c_t.n

        def rope_elu(ps, s1, s2, i, dst_fn, qscale, wfirst):
            P.op("vector", lambda v, ps=ps, i=i, s1=s1: v.tensor_tensor(
                out=full_ap(s1), in0=full_ap(ps), in1=rtab64(tb["cos64"], i), op=OP.mult),
                wait=wfirst)
            rd = P.op("vector", lambda v, ps=ps, i=i, s2=s2: v.tensor_tensor(
                out=full_ap(s2), in0=swap_ap(ps), in1=rtab64(tb["sinpat"], i), op=OP.mult),
                inc=c_v)
            P.op("vector", lambda v, s1=s1, s2=s2: v.tensor_tensor(
                out=s1[:], in0=s1[:], in1=s2[:], op=OP.add))
            mx = P.op("vector", lambda v, s1=s1, s2=s2: v.tensor_scalar(
                out=s2[:], in0=s1[:], scalar1=0.0, scalar2=None, op0=OP.min), inc=c_v)
            if qscale:
                eq = P.op("scalar", lambda s, s2=s2: s.activation(
                    out=s2[:], in_=s2[:], func=AF.Exp, scale=qscale),
                    wait=[(c_v, mx)], inc=c_s)
                P.op("vector", lambda v, s1=s1: v.tensor_scalar(
                    out=s1[:], in0=s1[:], scalar1=0.0, scalar2=qscale, op0=OP.max, op1=OP.mult))
            else:
                eq = P.op("scalar", lambda s, s2=s2: s.activation(
                    out=s2[:], in_=s2[:], func=AF.Exp), wait=[(c_v, mx)], inc=c_s)
                P.op("vector", lambda v, s1=s1: v.tensor_scalar(
                    out=s1[:], in0=s1[:], scalar1=0.0, scalar2=None, op0=OP.max))
            ft = P.op("vector", lambda v, s1=s1, s2=s2: v.tensor_tensor(
                out=dst_fn(), in0=s1[:], in1=s2[:], op=OP.add),
                wait=[(c_s, eq)], inc=c_v)
            return rd, ft

        for i in range(nt):
            wk = [(c_v, ev_x[i]), (c_v, tWQ_KV)]
            if i >= 1:
                wk.append((c_v, krd_t[i - 1]))
            kMMi = mm_group(psK, wqkvT_kv, 2 * D, 0, wk, i)
            wv = [(c_s, vev_t[i - 1])] if i >= 1 else ()
            vMMi = mm_group(psV, wqkvT_kv, 2 * D, 1024, wv, i)
            wq = [(c_v, tWQ_Q)]
            if i >= 1:
                wq.append((c_v, qrd_t[i - 1]))
            qMMi = mm_group(psQ, wqkvT_q, D, 0, wq, i)
            if i >= 1:
                emit_qtr(i - 1)
            wkr = [(c_t, kMMi)]
            if i >= 1:
                wkr.append((c_v, kft_t[i - 1]))
            krd_t[i], kft_t[i] = rope_elu(
                psK, rs1, rs2, i, lambda i=i: kf_sb[:, i * D:(i + 1) * D], None, wkr)
            vev_t[i] = P.op("scalar", lambda s, i=i: s.copy(
                out=v_sb[:, i * D:(i + 1) * D], in_=psV[:, :]),
                wait=[(c_t, vMMi)], inc=c_s)
            wqr = [(c_t, qMMi)]
            if i >= 1:
                wqr.append((c_v, qf_t[i - 1]))
                wqr.append((c_t, qT_t[i - 1]))
            qrd_t[i], qf_t[i] = rope_elu(
                psQ, qs1, qs2, i, lambda i=i: qfb[i % 2][:], 0.125, wqr)

            if i == nt - 1:
                emit_qtr(i)
            emit_mean_jobs(192 // nt + 1)
        emit_mean_jobs(len(_mp_jobs))
        tQF = qev_t[nt - 1]

        P.flush()
        pbM.close()
        P.phase_barrier(waits=[(c_v, kft_t[nt - 1]), (c_s, qev_t[nt - 1]),
                               (c_t, qT_t[nt - 1])])
        rkv.close()
        right.close()


        pt = ExitStack()
        psB = pt.enter_context(nc.psum_tensor("psB", [128, 512], F32))
        for nm in ("g", "u", "d"):
            ci = mci[nm]
            tp = P.op("tensor", lambda t, ci=ci: t.transpose(
                out=psB[0:1, 128:256], in_=rscol[:, ci:ci + 1], identity=tb["ident_f"][:]),
                wait=[(c_v, _mp["tick"])], inc=c_t)
            ev = P.op("vector", lambda v: v.tensor_copy(out=mrow[0:1, 0:128], in_=psB[0:1, 128:256]),
                      wait=[(c_t, tp)], inc=c_v)
            P.op("vector", lambda v: v.tensor_scalar(
                out=mrow[0:1, 128:256], in0=mrow[0:1, 0:128], scalar1=MG, scalar2=MG,
                op0=OP.add, op1=OP.subtract))
            P.op("vector", lambda v: v.tensor_tensor(
                out=mrow[0:1, 256:384], in0=mrow[0:1, 0:128], in1=mrow[0:1, 128:256],
                op=OP.subtract))
            P.op("vector", lambda v: v.reduce_sum(out=stat1[0:1, 0:1], in_=mrow[0:1, 128:256], axis=AX))
            P.op("vector", lambda v: v.reduce_sum(out=stat1[0:1, 1:2], in_=mrow[0:1, 256:384], axis=AX))
            P.op("vector", lambda v: v.tensor_tensor(
                out=stat1[0:1, 2:3], in0=stat1[0:1, 0:1], in1=stat1[0:1, 1:2], op=OP.add))
            P.op("vector", lambda v: v.tensor_scalar(out=stat1[0:1, 3:4], in0=stat1[0:1, 2:3],
                                                     scalar1=INV2P22, scalar2=EPS_Q,
                                                     op0=OP.mult, op1=OP.max))
            P.op("vector", lambda v: v.tensor_scalar(out=stat1[0:1, 4:5], in0=stat1[0:1, 3:4],
                                                     scalar1=-0.5, scalar2=None, op0=OP.mult))
            bc0 = P.op("vector", lambda v: v.tensor_scalar(out=stat1[0:1, 5:6], in0=stat1[0:1, 3:4],
                                                           scalar1=0.5, scalar2=None, op0=OP.mult),
                       inc=c_v)
            last = 0
            for (srcc, dstb) in ((3, wmean_b), (4, wbias_n), (5, wbias_p)):
                bcm = P.op("tensor", lambda t, srcc=srcc: t.matmul(
                    out=psB[:, 0:1], lhsT=tb["onesrow_f"][:], rhs=stat1[0:1, srcc:srcc + 1],
                    start=True, stop=True), wait=[(c_v, bc0), (c_v, last)] if last else [(c_v, bc0)],
                    inc=c_t)
                last = P.op("vector", lambda v, ci=ci, dstb=dstb: v.tensor_copy(
                    out=dstb[:, ci:ci + 1], in_=psB[:, 0:1]), wait=[(c_t, bcm)], inc=c_v)
            mean_ready[nm] = last
        tMEAN = mean_ready["d"]
        tmp = 0
        for nm, dstc in (("g", ag_sb), ("u", au_sb)):
            tmp = P.op("vector", lambda v, dstc=dstc, ci=mci[nm]: v.tensor_scalar(
                out=dstc[:, :], in0=invsc127[:, :], scalar1=wmean_b[:, ci:ci + 1],
                scalar2=0.5, op0=OP.mult, op1=OP.mult),
                wait=[(c_v, tMEAN), (c_v, q1_done[nt - 1])], inc=c_v)
        tAGU = tmp
        P.flush()
        pt.close()

        pb3 = ExitStack()
        kv_sb = pb3.enter_context(nc.sbuf_tensor("kv_sb", [128, 8 * 128], BF16))
        ks_sb = pb3.enter_context(nc.sbuf_tensor("ks_sb", [128, 16], BF16))
        rnb0 = [pb3.enter_context(nc.sbuf_tensor(f"rnb0{k}", [2, 512], F32)) for k in range(2)]
        rnbs = [pb3.enter_context(nc.sbuf_tensor(f"rnbs{k}", [128, 512], F32)) for k in range(2)]
        attn_fm = pb3.enter_context(nc.sbuf_tensor("attn_fm", [128, KD * T], BF16))
        woutT = pb3.enter_context(nc.sbuf_tensor("woutT", [128, KD * D], BF16))
        wosl = [pb3.enter_context(nc.sbuf_tensor(f"wosl{k}", [128, D], F32)) for k in range(2)]
        woslb = [pb3.enter_context(nc.sbuf_tensor(f"woslb{k}", [128, D], BF16)) for k in range(2)]
        ao = [pb3.enter_context(nc.sbuf_tensor(f"ao{k}", [128, D], BF16)) for k in range(2)]
        psKV = pb3.enter_context(nc.psum_tensor("psKV", [128, 512], F32))
        psKS = pb3.enter_context(nc.psum_tensor("psKS", [128, 512], F32))
        wdsg = [pb3.enter_context(nc.sbuf_tensor(f"wdsg{k}", [128, 512], BF16)) for k in range(2)]
        wdtr = [pb3.enter_context(nc.sbuf_tensor(f"wdtr{k}", [128, 512], BF16)) for k in range(2)]
        pbo = ExitStack()
        psTo = [pbo.enter_context(nc.psum_tensor(f"psTo{k}", [128, 1024], BF16)) for k in range(2)]

        _wd = dict(ld=[0, 0, 0, 0], tl=[0, 0, 0, 0], sgn=[0, 0], st=[0, 0],
                   li=0, pi=0)

        def emit_wd_load(n):
            for _ in range(n):
                ix = _wd["li"]
                if ix >= 64 or ix >= _wd["pi"] + 4:
                    return
                _wd["li"] = ix + 1
                d8, cc = ix // 8, ix % 8
                slot = ix % 4
                wld = [(c_v, _wd["ld"][slot])] if _wd["ld"][slot] else [(c_v, tMEAN)]
                _wd["tl"][slot] = P.dma("gpsimd", out=m_wsl[slot][:],
                                        in_=wd_h[d8 * 128:(d8 + 1) * 128,
                                                 cc * 512:(cc + 1) * 512],
                                        wait=wld, inc=s_msl[slot])

        def emit_wd_proc(n):
            for _ in range(n):
                ix = _wd["pi"]
                if ix >= 64:
                    return
                if ix >= _wd["li"]:
                    emit_wd_load(1)
                _wd["pi"] = ix + 1
                d8, cc = ix // 8, ix % 8
                slot = ix % 4
                par = ix % 2
                wsg = [(s_msl[slot], _wd["tl"][slot]), (c_v, tMEAN)]
                if _wd["sgn"][par]:
                    wsg.append((c_v, _wd["sgn"][par]))
                P.op("scalar", lambda s, par=par, slot=slot: s.activation(
                    out=wdsg[par][:], in_=m_wsl[slot][:], func=AF.Sign,
                    bias=wbias_n[:, 2:3]), wait=wsg, inc=c_s)
                P.op("scalar", lambda s, slot=slot: s.activation(
                    out=m_wsl[slot][:], in_=m_wsl[slot][:], func=AF.Sign,
                    bias=wbias_p[:, 2:3]), inc=c_s)
                s2t = c_s.n
                wtt = [(c_s, s2t)]
                if _wd["st"][par]:
                    wtt.append((s_wts[par], _wd["st"][par]))
                tt = P.op("vector", lambda v, par=par, slot=slot: v.tensor_tensor(
                    out=wdtr[par][:], in0=wdsg[par][:], in1=m_wsl[slot][:], op=OP.add),
                    wait=wtt, inc=c_v)
                _wd["sgn"][par] = tt
                _wd["ld"][slot] = tt
                _wd["st"][par] = P.dma("gpsimd",
                                       out=wdt_scr[d8 * 128:(d8 + 1) * 128,
                                                   cc * 512:(cc + 1) * 512],
                                       in_=wdtr[par][:], wait=[(c_v, tt)], inc=s_wts[par])

        emit_wd_load(4)

        nq4 = T // 512 if T >= 512 else 1
        qw = min(T, 512)
        wo_cast = [0] * 8
        wo_tr = [0] * 8
        wo_ev = [0] * 8
        for j in range(8):
            w = [(c_s, wo_cast[j - 2])] if j >= 2 else ()
            tl = P.dma("sync", out=wosl[j % 2][:], in_=wout_h[j * 128:(j + 1) * 128, :],
                       wait=w, inc=s_wo[j % 2])
            wcst = [(s_wo[j % 2], tl)]
            if j >= 2:
                wcst.append((c_t, wo_tr[j - 2]))
            wo_cast[j] = P.op("scalar", lambda s, j=j: s.copy(
                out=woslb[j % 2][:], in_=wosl[j % 2][:]), wait=wcst, inc=c_s)
            wtr = [(c_s, wo_cast[j])]
            if j >= 2:
                wtr.append((c_v, wo_ev[j - 2]))
            wo_tr[j] = transpose8(psTo[j % 2],
                                  lambda c, j=j: woslb[j % 2][:, c * 128:(c + 1) * 128], 8, wtr)
            wo_ev[j] = P.op("vector", lambda v, j=j: v.tensor_copy(
                out=woutT[:].rearrange("p (c r) -> p c r", c=8, r=D)[:, :, j * 128:(j + 1) * 128],
                in_=psTo[j % 2][:]), wait=[(c_t, wo_tr[j])], inc=c_v)
            emit_wd_load(1)
            emit_wd_proc(1)
        tWOUT = wo_ev[7]
        pbo.close()
        pp2 = ExitStack()
        psD = pp2.enter_context(nc.psum_tensor("psD", [2, 512], F32))
        psR = [pp2.enter_context(nc.psum_tensor(f"psR{k}", [128, 512], F32)) for k in range(2)]
        psN = [pp2.enter_context(nc.psum_tensor(f"psN{k}", [128, 512], F32)) for k in range(2)]

        kvev = [0] * KD

        def emit_kv_pair(qc):
            wkv = [(c_v, kft_t[nt - 1]), (c_s, vev_t[nt - 1])]
            if qc >= 1:
                wkv.append((c_v, kvev[qc - 1]))
            for i2 in range(nt):
                P.op("tensor", lambda t, qc=qc, i2=i2: t.matmul(
                    out=psKV[:, 0:128],
                    lhsT=kf_sb[:, i2 * D + qc * 128: i2 * D + (qc + 1) * 128],
                    rhs=v_sb[:, i2 * D + qc * 128: i2 * D + (qc + 1) * 128],
                    start=(i2 == 0), stop=(i2 == nt - 1)),
                    wait=wkv if i2 == 0 else ())
                P.op("tensor", lambda t, qc=qc, i2=i2: t.matmul(
                    out=psKS[:, 0:1],
                    lhsT=kf_sb[:, i2 * D + qc * 128: i2 * D + (qc + 1) * 128],
                    rhs=tb["ones_col_b"][:],
                    start=(i2 == 0), stop=(i2 == nt - 1)),
                    inc=c_t if i2 == nt - 1 else None)
            kvmm = c_t.n
            P.op("vector", lambda v, qc=qc: v.tensor_copy(
                out=kv_sb[:, qc * 128:(qc + 1) * 128], in_=psKV[:, 0:128]),
                wait=[(c_t, kvmm)])
            P.op("vector", lambda v, qc=qc: v.memset(ks_sb[:, 2 * qc:2 * qc + 2], 0.0))
            P.op("vector", lambda v, qc=qc: v.tensor_copy(
                out=ks_sb[0:64, 2 * qc:2 * qc + 1], in_=psKS[0:64, 0:1]))
            kvev[qc] = P.op("vector", lambda v, qc=qc: v.tensor_copy(
                out=ks_sb[64:128, 2 * qc + 1:2 * qc + 2], in_=psKS[64:128, 0:1]), inc=c_v)

        emit_kv_pair(0)
        att_t = {}
        mx_t = {}
        rev_t = {}
        rn_t = {}
        it = 0
        for p in range(KD):
            if p + 1 < KD:
                emit_kv_pair(p + 1)
            for st4 in range(nq4):
                kd = it % 2
                wdn = [(c_v, kvev[p]), (c_s, tQF)]
                if it >= 1:
                    wdn.append((c_v, mx_t[it - 1]))
                dnMM = P.op("tensor", lambda t, p=p, st4=st4, kd=kd: t.matmul(
                    out=psD[:, 0:qw], lhsT=ks_sb[:, 2 * p:2 * p + 2],
                    rhs=qf_sb[:, p * T + st4 * qw: p * T + (st4 + 1) * qw],
                    start=True, stop=True), wait=wdn, inc=c_t)
                wmx = [(c_t, dnMM)]
                if it >= 2:
                    wmx.append((c_t, rn_t[it - 2]))
                mx_t[it] = P.op("vector", lambda v, kd=kd: v.tensor_scalar(
                    out=rnb0[kd][:, 0:qw], in0=psD[:, 0:qw], scalar1=EPS_DEN,
                    scalar2=None, op0=OP.max), wait=wmx, inc=c_v)
                rrc = P.op("vector", lambda v, kd=kd: v.reciprocal(
                    out=rnb0[kd][:, 0:qw], in_=rnb0[kd][:, 0:qw]), inc=c_v)
                wrn = [(c_v, rrc)]
                if it >= 2:
                    wrn.append((c_v, rev_t[it - 2]))
                rn_t[it] = P.op("tensor", lambda t, kd=kd: t.matmul(
                    out=psR[kd][:, 0:qw], lhsT=tb["sel2"][:], rhs=rnb0[kd][:, 0:qw],
                    start=True, stop=True), wait=wrn, inc=c_t)
                rev_t[it] = P.op("vector", lambda v, kd=kd: v.tensor_copy(
                    out=rnbs[kd][:, 0:qw], in_=psR[kd][:, 0:qw]),
                    wait=[(c_t, rn_t[it])], inc=c_v)
                wn0 = []
                if it >= 2:
                    wn0.append((c_v, att_t[it - 2]))
                P.op("tensor", lambda t, p=p, st4=st4, kd=kd: t.matmul(
                    out=psN[kd][0:64, 0:qw], lhsT=kv_sb[0:64, p * 128: p * 128 + 64],
                    rhs=qf_sb[0:64, p * T + st4 * qw: p * T + (st4 + 1) * qw],
                    start=True, stop=True, tile_position=(0, 0)), wait=wn0)
                nmm = P.op("tensor", lambda t, p=p, st4=st4, kd=kd: t.matmul(
                    out=psN[kd][64:128, 0:qw], lhsT=kv_sb[64:128, p * 128 + 64: p * 128 + 128],
                    rhs=qf_sb[64:128, p * T + st4 * qw: p * T + (st4 + 1) * qw],
                    start=True, stop=True, tile_position=(64, 64)), inc=c_t)
                att_t[it] = P.op("vector", lambda v, p=p, st4=st4, kd=kd: v.tensor_tensor(
                    out=attn_fm[:, p * T + st4 * qw: p * T + (st4 + 1) * qw],
                    in0=psN[kd][:, 0:qw], in1=rnbs[kd][:, 0:qw], op=OP.mult),
                    wait=[(c_v, rev_t[it]), (c_t, nmm)], inc=c_v)
                emit_wd_load(1)
                emit_wd_proc(1)
                it += 1
        last_att = att_t[it - 1]
        tATTN = last_att
        pp2.close()
        psW = [pb3.enter_context(nc.psum_tensor(f"psW{k}", [128, 512], F32)) for k in range(2)]




        st_at = [0] * nt
        aev_t = [0] * nt
        for i in range(nt):
            for sh in range(2):
                w0 = [(c_v, tATTN), (c_v, tWOUT), (c_v, tAGU)]
                if i >= 1:
                    w0.append((c_s, aev_t[i - 1]))
                for c in range(KD):
                    P.op("tensor", lambda t, i=i, sh=sh, c=c: t.matmul(
                        out=psW[sh][:, :],
                        lhsT=attn_fm[:, c * T + i * 128: c * T + (i + 1) * 128],
                        rhs=woutT[:, c * D + sh * 512: c * D + (sh + 1) * 512],
                        start=(c == 0), stop=(c == KD - 1)),
                        wait=w0 if c == 0 else (), inc=c_t if c == KD - 1 else None)
                mmW = c_t.n
                wev = [(c_t, mmW)]
                if sh == 0 and i >= 2:
                    wev.append((s_at[i % 2], st_at[i - 2]))
                P.op("scalar", lambda s, i=i, sh=sh: s.copy(
                    out=ao[i % 2][:, sh * 512:(sh + 1) * 512], in_=psW[sh][:, :]),
                    wait=wev, inc=c_s)
            aev_t[i] = c_s.n
            st_at[i] = P.dma("gpsimd", out=attn_scr[i * 128:(i + 1) * 128, :], in_=ao[i % 2][:],
                             wait=[(c_s, aev_t[i])], inc=s_at[i % 2])
            emit_wd_load(1)
            emit_wd_proc(1)
        while _wd["pi"] < 64:
            emit_wd_load(1)
            emit_wd_proc(1)
        tWTS = [(s_wts[0], s_wts[0].n), (s_wts[1], s_wts[1].n)]
        tATST = [(s_at[0], s_at[0].n), (s_at[1], s_at[1].n)]

        P.phase_barrier(waits=[(c_v, tAGU), (c_s, _mp["prev_ld"][0]),
                               (c_s, _mp["prev_ld"][1])])
        P.flush()
        pb3.close()
        pbL.close()
        pm.close()
        P.phase_barrier()

        pouter = ExitStack()
        wdT = pouter.enter_context(nc.sbuf_tensor("wdT", [128, NF * D], BF16))
        _wl = dict(idx=0)

        def emit_wdT(n):
            for _ in range(n):
                c = _wl["idx"]
                if c >= NF:
                    return
                _wl["idx"] = c + 1
                P.dma("sync", out=wdT[:, c * D:(c + 1) * D],
                      in_=wdt_scr[:, c * 128:(c + 1) * 128],
                      wait=list(tWTS) if c == 0 else (), inc=s_wdl, transpose=True)

        pc1 = ExitStack()
        q1fm = pc1.enter_context(nc.sbuf_tensor("q1fm", [128, KD * T], BF16))
        q1l = 0
        for c in range(KD):
            q1l = P.dma("sync", out=q1fm[:, c * T:(c + 1) * T],
                        in_=q1_scr[:, c * 128:(c + 1) * 128],
                        wait=[(s, s.n) for s in s_q1s] if c == 0 else (),
                        inc=s_q1l, transpose=True)
        tQ1L = q1l

        gsl = [pc1.enter_context(nc.sbuf_tensor(f"gsl{k}", [128, D], F32)) for k in range(4)]
        sgn = [pc1.enter_context(nc.sbuf_tensor(f"sgn{k}", [128, D], BF16)) for k in range(2)]
        trn = [pc1.enter_context(nc.sbuf_tensor(f"trn{k}", [128, D], BF16)) for k in range(2)]
        wgTb = [pc1.enter_context(nc.sbuf_tensor(f"wgTb{k}", [128, 4096], BF16)) for k in range(2)]
        wuTb = [pc1.enter_context(nc.sbuf_tensor(f"wuTb{k}", [128, 4096], BF16)) for k in range(2)]
        hrelu = [pc1.enter_context(nc.sbuf_tensor(f"hrelu{k}", [128, 512], F32)) for k in range(3)]
        hup = [pc1.enter_context(nc.sbuf_tensor(f"hup{k}", [128, 512], F32)) for k in range(3)]
        hsq = [pc1.enter_context(nc.sbuf_tensor(f"hsq{k}", [128, 512], F32)) for k in range(3)]
        htile = [pc1.enter_context(nc.sbuf_tensor(f"htile{k}", [128, 512], F32)) for k in range(3)]
        hcol = pc1.enter_context(nc.sbuf_tensor("hcol", [128, 2], F32))
        psG = [pc1.enter_context(nc.psum_tensor(f"psG{k}", [128, 512], F32)) for k in range(3)]
        psU = [pc1.enter_context(nc.psum_tensor(f"psU{k}", [128, 512], F32)) for k in range(3)]
        psTg = [pc1.enter_context(nc.psum_tensor(f"psTg{k}", [128, 1024], BF16)) for k in range(2)]

        zh = P.op("vector", lambda v: v.memset(hmax_sb[:, :], 0.0), inc=c_v)

        state = dict(ld=[0, 0, 0, 0], tl=[0, 0, 0, 0], sgn=[0, 0], trn=[0, 0],
                     ev=[0, 0], slot=0, par=0, lslot=0)

        def prep_load(mat_h, j):
            slot = state["lslot"] % 4
            state["lslot"] += 1
            wld = [(c_v, state["ld"][slot])] if state["ld"][slot] else []
            state["tl"][slot] = P.dma("sync", out=gsl[slot][:],
                                      in_=mat_h[j * 128:(j + 1) * 128, :],
                                      wait=wld, inc=s_gu[slot])

        def prep_proc(ci, j, dstbuf, bf):
            slot = state["slot"] % 4
            par = state["par"] % 2
            state["slot"] += 1
            state["par"] += 1
            wsg = [(s_gu[slot], state["tl"][slot]), (c_v, tMEAN)]
            if state["sgn"][par]:
                wsg.append((c_v, state["sgn"][par]))
            P.op("scalar", lambda s, slot=slot, par=par, ci=ci: s.activation(
                out=sgn[par][:], in_=gsl[slot][:], func=AF.Sign,
                bias=wbias_n[:, ci:ci + 1]), wait=wsg, inc=c_s)
            P.op("scalar", lambda s, slot=slot, ci=ci: s.activation(
                out=gsl[slot][:], in_=gsl[slot][:], func=AF.Sign,
                bias=wbias_p[:, ci:ci + 1]), inc=c_s)
            s2t = c_s.n
            wtt = [(c_s, s2t)]
            if state["trn"][par]:
                wtt.append((c_t, state["trn"][par]))
            tt = P.op("vector", lambda v, slot=slot, par=par: v.tensor_tensor(
                out=trn[par][:], in0=sgn[par][:], in1=gsl[slot][:], op=OP.add),
                wait=wtt, inc=c_v)
            state["sgn"][par] = tt
            state["ld"][slot] = tt
            wtr = [(c_v, tt)]
            if state["ev"][par]:
                wtr.append((c_v, state["ev"][par]))
            trt = transpose8(psTg[par], lambda c, par=par: trn[par][:, c * 128:(c + 1) * 128],
                             8, wtr)
            state["trn"][par] = trt
            jj = j % 4
            wev = [(c_t, trt)]
            if bf[par]:
                wev.append((c_t, bf[par]))
            ev = P.op("vector", lambda v, dstbuf=dstbuf, jj=jj, par=par: v.tensor_copy(
                out=dstbuf[:].rearrange("p (c r) -> p c r", c=8, r=512)[:, :, jj * 128:(jj + 1) * 128],
                in_=psTg[par][:]), wait=wev, inc=c_v)
            state["ev"][par] = ev
            return ev

        relu_t = {}
        uev_t = {}
        hdone = {}
        hst = {}
        hrow_t = {}
        mm_strip = [0] * 8
        prep_ql = []
        prep_qp = []

        def pop_load(n):
            for _ in range(n):
                if prep_ql:
                    m, j = prep_ql.pop(0)
                    prep_load(m, j)

        def pop_proc(n):
            for _ in range(n):
                if prep_qp:
                    ci, j, dst, bf = prep_qp.pop(0)
                    prep_proc(ci, j, dst, bf)

        for jj in range(4):
            prep_load(wg_h, jj)
            prep_load(wu_h, jj)
            prep_proc(0, jj, wgTb[0], [0, 0])
            prep_proc(1, jj, wuTb[0], [0, 0])
        for s8 in range(8):
            if s8 + 1 < 8:
                bf = [mm_strip[s8 - 1], mm_strip[s8 - 1]] if s8 >= 1 else [0, 0]
                for jj in range(4):
                    prep_ql.append((wg_h, (s8 + 1) * 4 + jj))
                    prep_ql.append((wu_h, (s8 + 1) * 4 + jj))
                    prep_qp.append((0, (s8 + 1) * 4 + jj, wgTb[(s8 + 1) % 2], bf))
                    prep_qp.append((1, (s8 + 1) * 4 + jj, wuTb[(s8 + 1) % 2], bf))
                pop_load(4)
                for _ in range(8):
                    pop_proc(1)
                    pop_load(1)
            last_ev = state["ev"]
            for i in range(nt):
                prev = (s8, i - 3) if i >= 3 else (s8 - 1, nt - 3 + i)
                gl = (s8 * nt + i) % 3
                wg0 = [(c_v, last_ev[0]), (c_v, last_ev[1]), (s_q1l, tQ1L), (c_v, tAGU)]
                if prev in uev_t:
                    wg0.append((c_s, uev_t[prev]))
                for c in range(KD):
                    P.op("tensor", lambda t, gl=gl, s8=s8, i=i, c=c: t.matmul(
                        out=psG[gl][:, :],
                        lhsT=q1fm[:, c * T + i * 128: c * T + (i + 1) * 128],
                        rhs=wgTb[s8 % 2][:, c * 512:(c + 1) * 512],
                        start=(c == 0), stop=(c == KD - 1)),
                        wait=wg0 if c == 0 else (), inc=c_t if c == KD - 1 else None)
                gmm = c_t.n
                for c in range(KD):
                    P.op("tensor", lambda t, gl=gl, s8=s8, i=i, c=c: t.matmul(
                        out=psU[gl][:, :],
                        lhsT=q1fm[:, c * T + i * 128: c * T + (i + 1) * 128],
                        rhs=wuTb[s8 % 2][:, c * 512:(c + 1) * 512],
                        start=(c == 0), stop=(c == KD - 1)),
                        inc=c_t if c == KD - 1 else None)
                umm = c_t.n
                wrl = [(c_t, gmm)]
                if prev in hdone:
                    wrl.append((c_v, hdone[prev]))
                relu_t[(s8, i)] = P.op("scalar", lambda s, gl=gl, i=i: s.activation(
                    out=hrelu[gl][:], in_=psG[gl][:, :], func=AF.Relu,
                    scale=ag_sb[:, i:i + 1]), wait=wrl, inc=c_s)
                uev_t[(s8, i)] = P.op("scalar", lambda s, gl=gl, i=i: s.activation(
                    out=hup[gl][:], in_=psU[gl][:, :], func=AF.Identity,
                    scale=au_sb[:, i:i + 1]), wait=[(c_t, umm)], inc=c_s)
                sqt = P.op("scalar", lambda s, gl=gl: s.activation(
                    out=hsq[gl][:], in_=hrelu[gl][:], func=AF.Square), inc=c_s)
                wht = [(c_s, sqt), (c_s, uev_t[(s8, i)])]
                if prev in hst:
                    wht.append((s_hw[gl], hst[prev]))
                ht = P.op("vector", lambda v, gl=gl: v.tensor_tensor(
                    out=htile[gl][:], in0=hsq[gl][:], in1=hup[gl][:], op=OP.mult),
                    wait=wht, inc=c_v)
                P.op("vector", lambda v, gl=gl: v.reduce_max(
                    out=hcol[:, 0:1], in_=htile[gl][:], axis=AX, apply_absolute_value=True),
                    wait=[(c_v, zh)])
                hdone[(s8, i)] = P.op("vector", lambda v, i=i: v.tensor_tensor(
                    out=hmax_sb[:, i:i + 1], in0=hmax_sb[:, i:i + 1], in1=hcol[:, 0:1],
                    op=OP.max), inc=c_v)
                hst[(s8, i)] = P.dma("gpsimd",
                                     out=h_scr[i * 128:(i + 1) * 128, s8 * 512:(s8 + 1) * 512],
                                     in_=htile[gl][:], wait=[(c_v, hdone[(s8, i)])],
                                     inc=s_hw[gl])
                if s8 == 7:
                    hrow_t[i] = [(s, s.n) for s in s_hw]
            mm_strip[s8] = umm
            if s8 >= 2:
                emit_wdT(6 if s8 < 6 else 4)
        emit_wdT(NF)
        tWDL = s_wdl.n
        tHST = [(s_hw[0], s_hw[0].n), (s_hw[1], s_hw[1].n)]
        tHMAX = hdone[(7, nt - 1)]

        P.flush()
        pc1.close()
        P.phase_barrier()

        pc2 = ExitStack()
        hbuf = [pc2.enter_context(nc.sbuf_tensor(f"hbuf{k}", [128, 1024], F32)) for k in range(8)]
        q2t = [pc2.enter_context(nc.sbuf_tensor(f"q2t{k}", [128, 4096], BF16)) for k in range(3)]
        q2fm = [pc2.enter_context(nc.sbuf_tensor(f"q2fm{k}", [128, 4096], BF16)) for k in range(2)]
        xre = [pc2.enter_context(nc.sbuf_tensor(f"xre{k}", [128, D], F32)) for k in range(2)]
        are = [pc2.enter_context(nc.sbuf_tensor(f"are{k}", [128, D], BF16)) for k in range(2)]
        zz = [[pc2.enter_context(nc.sbuf_tensor(f"z{a}{b}", [128, 1024], F32))
               for a in range(3)] for b in range(2)]

        psDn = [pc2.enter_context(nc.psum_tensor(f"psDn{k}", [128, 1024], F32)) for k in range(3)]
        psTd = [pc2.enter_context(nc.psum_tensor(f"psTd{k}", [128, 1024], BF16)) for k in range(2)]

        P.op("vector", lambda v: v.tensor_scalar(out=sch_sb[:, :], in0=hmax_sb[:, :],
                                                 scalar1=EPS_Q, scalar2=None, op0=OP.max),
             wait=[(c_v, tHMAX)])
        P.op("vector", lambda v: v.tensor_scalar(out=ad_sb[:, :], in0=sch_sb[:, :],
                                                 scalar1=RINV127, scalar2=None, op0=OP.mult))
        P.op("vector", lambda v: v.scalar_tensor_tensor(
            out=ad_sb[:, :], in0=sch_sb[:, :], scalar=RINV127_LO,
            in1=ad_sb[:, :], op0=OP.mult, op1=OP.add))
        P.op("vector", lambda v: v.tensor_scalar(out=ad_sb[:, :], in0=ad_sb[:, :],
                                                 scalar1=wmean_b[:, 2:3], scalar2=0.5,
                                                 op0=OP.mult, op1=OP.mult),
             wait=[(c_v, tMEAN)])
        P.op("vector", lambda v: v.reciprocal(out=sch_sb[:, :], in_=sch_sb[:, :]))
        tSCH = P.op("vector", lambda v: v.tensor_scalar(out=sch_sb[:, :], in0=sch_sb[:, :],
                                                        scalar1=127.0, scalar2=None, op0=OP.mult),
                    inc=c_v)

        st_out = [0] * nt
        base_t = [0] * nt
        dnev2 = {}
        q2done = {}
        q2T = {}
        q2ev = {}
        tail_ev = [0, 0]
        dn_last = [0] * nt
        hb_y = [0] * 8
        hload_t = {}

        def emit_hload(i):
            for qq in range(4):
                gidx = i * 4 + qq
                hs = gidx % 8
                wldh = [(c_v, hb_y[hs])] if hb_y[hs] else list(hrow_t[i])
                hload_t[(i, qq)] = P.dma("sync" if hs % 2 == 0 else "gpsimd",
                                         out=hbuf[hs][:],
                                         in_=h_scr[i * 128:(i + 1) * 128,
                                                   qq * 1024:(qq + 1) * 1024],
                                         wait=wldh, inc=s_hl[hs])

        def emit_qmath(i):
            k = i % 3
            for qq in range(4):
                gidx = i * 4 + qq
                hs = gidx % 8
                par = qq % 2
                z0, z1, z2 = zz[par]
                pq = (i, qq - 2) if qq >= 2 else (i - 1, qq + 2)
                wyq = [(s_hl[hs], hload_t[(i, qq)]), (c_v, tSCH)]
                yt = P.op("vector", lambda v, hs=hs, i=i, z0=z0: v.tensor_scalar(
                    out=z0[:], in0=hbuf[hs][:], scalar1=sch_sb[:, i:i + 1], scalar2=1000.0,
                    op0=OP.mult, op1=OP.add), wait=wyq, inc=c_v)
                hb_y[hs] = yt
                P.op("vector", lambda v, z0=z0, z1=z1: v.tensor_scalar(
                    out=z1[:], in0=z0[:], scalar1=M1, scalar2=M1, op0=OP.add, op1=OP.subtract))
                P.op("vector", lambda v, z0=z0, z2=z2: v.tensor_scalar(
                    out=z2[:], in0=z0[:], scalar1=M2, scalar2=M2, op0=OP.add, op1=OP.subtract))
                P.op("vector", lambda v, z0=z0, z1=z1, z2=z2: v.scalar_tensor_tensor(
                    out=z0[:], in0=z2[:], scalar=1.0, in1=z1[:], op0=OP.is_lt, op1=OP.mult))
                wq2 = [(c_t, q2T[(i - 3, qq)])] if (i - 3, qq) in q2T else []
                q2done[(i, qq)] = P.op("vector", lambda v, k=k, qq=qq, z0=z0, z2=z2: v.scalar_tensor_tensor(
                    out=q2t[k][:, qq * 1024:(qq + 1) * 1024], in0=z0[:], in1=z2[:],
                    scalar=1000.0, op0=OP.subtract, op1=OP.add), wait=wq2, inc=c_v)

        emit_hload(0)
        emit_hload(1)
        emit_qmath(0)

        def emit_base(i):
            k = i % 2
            wre = list(tATST)
            if i >= 2:
                wre.append((s_out[k], st_out[i - 2]))
            al = P.dma("sync", out=are[k][:], in_=attn_scr[i * 128:(i + 1) * 128, :],
                       wait=wre, inc=s_atl[k])
            xl = P.dma("sync", out=xre[k][:], in_=x_h[i * 128:(i + 1) * 128, :],
                       wait=[(s_out[k], st_out[i - 2])] if i >= 2 else (), inc=s_xr[k])
            base_t[i] = P.op("vector", lambda v, k=k: v.tensor_tensor(
                out=xre[k][:], in0=xre[k][:], in1=are[k][:], op=OP.add),
                wait=[(s_atl[k], al), (s_xr[k], xl)], inc=c_v)

        def emit_tail(i):
            k = i % 2
            kq = i % 3
            emit_base(i)
            for grp in range(4):
                par = grp % 2
                wtr = [(c_v, q2done[(i, grp)])]
                if tail_ev[par]:
                    wtr.append((c_s, tail_ev[par]))
                trt = transpose8(psTd[par],
                                 lambda c, kq=kq, grp=grp: q2t[kq][:, grp * 1024 + c * 128: grp * 1024 + (c + 1) * 128],
                                 8, wtr)
                q2T[(i, grp)] = trt
                wevq = [(c_t, trt)]
                if i >= 2:
                    wevq.append((c_t, dn_last[i - 2]))
                tail_ev[par] = P.op("scalar", lambda s, k=k, grp=grp, par=par: s.copy(
                    out=q2fm[k][:, grp * 1024:(grp + 1) * 1024], in_=psTd[par][:]),
                    wait=wevq, inc=c_s)
                q2ev[(i, grp)] = tail_ev[par]
            for sh in range(2):
                w0 = [(c_s, q2ev[(i, 3)]), (s_wdl, tWDL)]
                if i >= 3:
                    w0.append((c_v, dnev2[i - 3]))
                for c in range(NF):
                    P.op("tensor", lambda t, i=i, sh=sh, c=c, k=k: t.matmul(
                        out=psDn[i % 3][:, sh * 512:(sh + 1) * 512],
                        lhsT=q2fm[k][:, c * 128:(c + 1) * 128],
                        rhs=wdT[:, c * D + sh * 512:c * D + (sh + 1) * 512],
                        start=(c == 0), stop=(c == NF - 1)),
                        wait=w0 if c == 0 else (), inc=c_t if c == NF - 1 else None)
                dn_last[i] = c_t.n
            dnev2[i] = P.op("vector", lambda v, i=i, k=k: v.scalar_tensor_tensor(
                out=xre[k][:, :], in0=psDn[i % 3][:, :], scalar=ad_sb[:, i:i + 1],
                in1=xre[k][:, :], op0=OP.mult, op1=OP.add),
                wait=[(c_t, dn_last[i]), (c_v, base_t[i])], inc=c_v)
            st_out[i] = P.dma("gpsimd", out=out_h[i * 128:(i + 1) * 128, :], in_=xre[k][:],
                              wait=[(c_v, dnev2[i])], inc=s_out[k])

        for i in range(nt):
            if i + 2 < nt:
                emit_hload(i + 2)
            if i + 1 < nt:
                emit_qmath(i + 1)
            emit_tail(i)

        P.op("gpsimd", lambda e: e.nop(),
             wait=[(s_out[0], s_out[0].n), (s_out[1], s_out[1].n)], drain=False)

        P.flush()
        pc2.close()
        pouter.close()

    return nc


def kernel(L0_attn_x, L0_attn_w_qkv, L0_attn_w_out, L0_ffn_w_gate, L0_ffn_w_up,
           L0_ffn_w_down):
    x = np.ascontiguousarray(np.asarray(L0_attn_x, dtype=np.float32))
    tabs = host_tables(16)
    nc = build_nc(nt=16)
    in_maps = []
    wq = np.ascontiguousarray(np.asarray(L0_attn_w_qkv, dtype=np.float32))
    wo = np.ascontiguousarray(np.asarray(L0_attn_w_out, dtype=np.float32))
    wg = np.ascontiguousarray(np.asarray(L0_ffn_w_gate, dtype=np.float32))
    wu = np.ascontiguousarray(np.asarray(L0_ffn_w_up, dtype=np.float32))
    wd = np.ascontiguousarray(np.asarray(L0_ffn_w_down, dtype=np.float32))
    for b in range(8):
        m = dict(x=np.ascontiguousarray(x[b]), w_qkv=wq, w_out=wo, w_g=wg, w_u=wu, w_d=wd)
        m.update(tabs)
        in_maps.append(m)
    res = run_bass_kernel_spmd(nc, in_maps, core_ids=list(range(8)))
    out = np.stack([np.asarray(r["out"]) for r in res.results], axis=0)
    if out.dtype != np.float64:
        out = out.astype(np.float32)
    return out


# version 25
# speedup vs baseline: 1.0075x; 1.0075x over previous
import numpy as np
import ml_dtypes
from contextlib import ExitStack

import concourse.bass as bass
import concourse.mybir as mybir
from concourse.bass_utils import run_bass_kernel_spmd

F32 = mybir.dt.float32
BF16 = mybir.dt.bfloat16
AX = mybir.AxisListType.X
AF = mybir.ActivationFunctionType
OP = mybir.AluOpType

D, H, HD, F = 1024, 16, 64, 4096
KD = D // 128
NF = F // 128
M1 = float(3 * 2**22)
M2 = float(3 * 2**51)
EPS_LN = 1e-5
EPS_Q = 1e-5
EPS_DEN = 1e-6
RINV127 = float(np.float32(1.0) / np.float32(127.0))
RINV127_LO = float(np.float32(1.0 / 127.0 - np.float64(np.float32(1.0) / np.float32(127.0))))
INV1024 = 1.0 / 1024.0
INV2P22 = 1.0 / float(2**22)


class Sem:
    def __init__(self, h):
        self.h = h
        self.n = 0

    def tick(self, k=1):
        self.n += k
        return self.n


class Prog:
    ENGINES = ("sync", "vector", "scalar", "tensor", "gpsimd")

    def __init__(self, block):
        self.block = block
        self.items = {e: [] for e in self.ENGINES}
        self.all_sems = []

    def phase_barrier(self, waits=None):
        bar = tuple(waits) if waits is not None else tuple(
            (s, s.n) for s in self.all_sems if s.n > 0)
        for eng in self.ENGINES:
            self.items[eng].append((lambda e: e.nop(), bar, None, 0))

    DRAIN_ENGINES = ("vector", "scalar", "gpsimd")

    def op(self, eng, fn, wait=(), inc=None, incn=1, drain=True):
        tick = inc.tick(incn) if inc is not None else None
        if drain and eng in self.DRAIN_ENGINES:
            def fn2(e, fn=fn):
                r = fn(e)
                e.drain()
                return r
            self.items[eng].append((fn2, tuple(wait), inc, incn))
        else:
            self.items[eng].append((fn, tuple(wait), inc, incn))
        return tick

    def dma(self, eng, out, in_, wait=(), inc=None, transpose=False):
        tick = inc.tick(16) if inc is not None else None

        def fn(e, out=out, in_=in_, tr=transpose):
            return e.dma_start(out=out, in_=in_, transpose=tr)

        self.items[eng].append((fn, tuple(wait), inc, 16))
        return tick

    def flush(self):
        for eng in self.ENGINES:
            items = self.items[eng]
            if not items:
                continue
            self.items[eng] = []
            section = getattr(self.block, eng)

            def emit(e, items=items):
                for fn, waits, inc, incn in items:
                    for (s, v) in waits:
                        e.wait_ge(s.h, v)
                    r = fn(e)
                    if inc is not None:
                        r.then_inc(inc.h, incn)

            section(emit)


def host_tables(nt=16):
    T = nt * 128
    j = np.arange(0, HD, 2, dtype=np.float32) / np.float32(HD)
    inv_freq = np.float32(1.0) / (10000.0 ** j.astype(np.float64)).astype(np.float32)
    t = np.arange(T, dtype=np.float32)
    freqs = (t[:, None] * inv_freq[None, :]).astype(np.float32)
    cos = np.cos(freqs.astype(np.float64)).astype(np.float32)
    sin = np.sin(freqs.astype(np.float64)).astype(np.float32)
    cos_t = cos.reshape(nt, 128, 32).transpose(1, 0, 2)
    sin_t = sin.reshape(nt, 128, 32).transpose(1, 0, 2)
    cos64 = np.concatenate([cos_t, cos_t], axis=2).reshape(128, nt * 64)
    sinpat = np.concatenate([-sin_t, sin_t], axis=2).reshape(128, nt * 64)
    sel2 = np.zeros((2, 128), dtype=np.float32)
    sel2[0, 0:64] = 1.0
    sel2[1, 64:128] = 1.0
    constc = np.zeros((128, 8), dtype=np.float32)
    constc[:, 0] = M1
    constc[:, 1] = -M1
    constc[:, 2] = 0.5
    constc[:, 3] = -0.5
    constc[:, 4] = 1000.0
    constc[:, 5] = M2
    constc[:, 6] = -M2
    return dict(constc=constc,
        cos64=np.ascontiguousarray(cos64), sinpat=np.ascontiguousarray(sinpat),
        ident=np.eye(128, dtype=ml_dtypes.bfloat16),
        ident_f=np.eye(128, dtype=np.float32),
        onesrow_f=np.ones((1, 128), dtype=np.float32),
        ones_col_b=np.ones((128, 1), dtype=ml_dtypes.bfloat16),
        sel2=sel2,
    )


def build_nc(nt=16, tap=False):
    T = nt * 128
    NS = max(T // 512, 1)
    SW = min(T, 512)
    nc = bass.Bass("TRN2", target_bir_lowering=False)

    x_h = nc.dram_tensor("x", [T, D], F32, kind="ExternalInput").ap()
    wqkv_h = nc.dram_tensor("w_qkv", [3 * D, D], F32, kind="ExternalInput").ap()
    wout_h = nc.dram_tensor("w_out", [D, D], F32, kind="ExternalInput").ap()
    wg_h = nc.dram_tensor("w_g", [F, D], F32, kind="ExternalInput").ap()
    wu_h = nc.dram_tensor("w_u", [F, D], F32, kind="ExternalInput").ap()
    wd_h = nc.dram_tensor("w_d", [D, F], F32, kind="ExternalInput").ap()
    tspecs = (("cos64", [128, nt * 64], F32), ("sinpat", [128, nt * 64], F32),
              ("ident", [128, 128], BF16), ("ident_f", [128, 128], F32),
              ("onesrow_f", [1, 128], F32),
              ("ones_col_b", [128, 1], BF16), ("sel2", [2, 128], F32),
              ("constc", [128, 8], F32))
    tabs = {nm: nc.dram_tensor(nm, shp, dt, kind="ExternalInput").ap()
            for nm, shp, dt in tspecs}

    out_h = nc.dram_tensor("out", [T, D], F32, kind="ExternalOutput").ap()
    skind = "ExternalOutput" if tap else "Internal"
    q1_scr = nc.dram_tensor("q1_scr", [T, D], BF16, kind=skind).ap()
    attn_scr = nc.dram_tensor("attn_scr", [T, D], BF16, kind=skind).ap()
    h_scr = nc.dram_tensor("h_scr", [T, F], F32, kind=skind).ap()
    wdt_scr = nc.dram_tensor("wdt_scr", [D, F], BF16, kind="Internal").ap()

    es = ExitStack()
    with es:
        block = es.enter_context(nc.Block())
        P = Prog(block)
        def mksem(name):
            s = Sem(es.enter_context(nc.semaphore(name)))
            P.all_sems.append(s)
            return s

        def mksems(name, n):
            return [mksem(f"{name}{k}") for k in range(n)]

        s_tab, s_q1l = mksem("s_tab"), mksem("s_q1l")
        s_x = mksems("s_x", 3)
        s_wsl = mksems("s_wsl", 4)
        s_msl = mksems("s_msl", 4)
        s_wo = mksems("s_wo", 2)
        s_q1s = mksems("s_q1s", 3)
        s_gu = mksems("s_gu", 4)
        s_wd = mksems("s_wd", 2)
        s_hw = mksems("s_hw", 3)
        s_hl = mksems("s_hl", 8)
        s_at = mksems("s_at", 2)
        s_atl = mksems("s_atl", 2)
        s_xr = mksems("s_xr", 2)
        s_out = mksems("s_out", 2)
        s_wts = mksems("s_wts", 2)
        s_wdl = mksems("s_wdl", 1)[0]
        c_v, c_s, c_t, c_g = mksem("c_v"), mksem("c_s"), mksem("c_t"), mksem("c_g")

        per = ExitStack()
        es.enter_context(per)
        tb = {nm: per.enter_context(nc.sbuf_tensor("t_" + nm, shp, dt))
              for nm, shp, dt in tspecs}
        invsc127 = per.enter_context(nc.sbuf_tensor("invsc127", [128, nt], F32))
        scx_sb = per.enter_context(nc.sbuf_tensor("scx_sb", [128, nt], F32))
        hmax_sb = per.enter_context(nc.sbuf_tensor("hmax_sb", [128, nt], F32))
        wbias_n = per.enter_context(nc.sbuf_tensor("wbias_n", [128, 3], F32))
        wbias_p = per.enter_context(nc.sbuf_tensor("wbias_p", [128, 3], F32))
        wmean_b = per.enter_context(nc.sbuf_tensor("wmean_b", [128, 3], F32))
        ag_sb = per.enter_context(nc.sbuf_tensor("ag_sb", [128, nt], F32))
        au_sb = per.enter_context(nc.sbuf_tensor("au_sb", [128, nt], F32))
        ad_sb = per.enter_context(nc.sbuf_tensor("ad_sb", [128, nt], F32))
        sch_sb = per.enter_context(nc.sbuf_tensor("sch_sb", [128, nt], F32))
        stat1 = per.enter_context(nc.sbuf_tensor("stat1", [1, 16], F32))

        for nm in tb:
            P.dma("sync", out=tb[nm][:], in_=tabs[nm][:, :], inc=s_tab)
        tTAB = s_tab.n
        ident = tb["ident"]

        def transpose8(psbuf, src_ap_fn, nblk, wfirst):
            for c in range(nblk):
                P.op("tensor", lambda t, c=c: t.transpose(
                    out=psbuf[:, c * 128:(c + 1) * 128], in_=src_ap_fn(c),
                    identity=ident[:]),
                    wait=wfirst if c == 0 else (), inc=c_t if c == nblk - 1 else None)
            return c_t.n

        right = ExitStack()
        es.enter_context(right)
        xn_fm = right.enter_context(nc.sbuf_tensor("xn_fm", [128, KD * T], BF16, side="right"))
        wqkvT_q = right.enter_context(nc.sbuf_tensor("wqkvT_q", [128, KD * D], BF16, side="right"))
        rkv = ExitStack()
        es.enter_context(rkv)
        wqkvT_kv = rkv.enter_context(nc.sbuf_tensor("wqkvT_kv", [128, KD * 2 * D], BF16, side="right"))

        pm = ExitStack()
        m_wsl = [pm.enter_context(nc.sbuf_tensor(f"m_wsl{k}", [128, 512], F32)) for k in range(4)]
        mcols = [pm.enter_context(nc.sbuf_tensor(f"mcols{k}", [128, 64], F32)) for k in range(3)]
        rscol = pm.enter_context(nc.sbuf_tensor("rscol", [128, 3], F32))
        mrow = pm.enter_context(nc.sbuf_tensor("mrow", [1, 384], F32))

        MG = float(1.5 * 2**24 * 2**-6)
        mci = {"g": 0, "u": 1, "d": 2}
        mean_ready = {}
        _mp = dict(prev_ld=[0, 0, 0, 0], tick=0, done=0)
        _mp_jobs = []
        for mi, (nm, hsrc) in enumerate((("g", wg_h), ("u", wu_h), ("d", wd_h))):
            for j in range(64):
                _mp_jobs.append((mci[nm], j, hsrc, nm))

        def emit_mean_jobs(njobs):
            while njobs > 0 and _mp["done"] < len(_mp_jobs):
                ci, j, hsrc, nm = _mp_jobs[_mp["done"]]
                bs = _mp["done"] % 4
                if nm == "d":
                    r0, c0 = (j // 8) * 128, (j % 8) * 512
                    src_ap = wd_h[r0:r0 + 128, c0:c0 + 512]
                else:
                    src_ap = hsrc[(j // 2) * 128:(j // 2 + 1) * 128,
                                  (j % 2) * 512:(j % 2 + 1) * 512]
                wld = [(c_s, _mp["prev_ld"][bs])] if _mp["prev_ld"][bs] else []
                tl = P.dma("gpsimd", out=m_wsl[bs][:], in_=src_ap, wait=wld, inc=s_msl[bs])
                ab = P.op("scalar", lambda s, bs=bs, ci=ci, j=j: s.activation(
                    out=m_wsl[bs][:], in_=m_wsl[bs][:], func=AF.Abs,
                    accum_out=mcols[ci][:, j:j + 1]), wait=[(s_msl[bs], tl)], inc=c_s)
                _mp["prev_ld"][bs] = ab
                _mp["done"] += 1
                njobs -= 1
                if _mp["done"] % 64 == 0:
                    ci_done = _mp_jobs[_mp["done"] - 1][0]
                    P.op("vector", lambda v, ci=ci_done: v.reduce_sum(
                        out=rscol[:, ci:ci + 1], in_=mcols[ci][:, :], axis=AX),
                        wait=[(c_s, ab)], inc=c_v)
                    _mp["tick"] = c_v.n


        pa = ExitStack()
        xtile = [pa.enter_context(nc.sbuf_tensor(f"xt{k}", [128, D], F32)) for k in range(3)]
        xnf = [pa.enter_context(nc.sbuf_tensor(f"xnf{k}", [128, D], F32)) for k in range(3)]
        xnb = [pa.enter_context(nc.sbuf_tensor(f"xnb{k}", [128, D], BF16)) for k in range(3)]
        q1t = [pa.enter_context(nc.sbuf_tensor(f"q1t{k}", [128, D], BF16)) for k in range(3)]
        y0 = [pa.enter_context(nc.sbuf_tensor(f"y0{k}", [128, D], F32)) for k in range(3)]
        y1 = [pa.enter_context(nc.sbuf_tensor(f"y1{k}", [128, D], F32)) for k in range(3)]
        y2 = [pa.enter_context(nc.sbuf_tensor(f"y2{k}", [128, D], F32)) for k in range(3)]
        colA = [pa.enter_context(nc.sbuf_tensor(f"colA{k}", [128, 8], F32)) for k in range(3)]
        wsl = [pa.enter_context(nc.sbuf_tensor(f"wsl{k}", [128, D], F32)) for k in range(4)]
        wslb = [pa.enter_context(nc.sbuf_tensor(f"wslb{k}", [128, D], BF16)) for k in range(2)]

        psTw = [pa.enter_context(nc.psum_tensor(f"psTw{k}", [128, 1024], BF16)) for k in range(2)]
        psTx = [pa.enter_context(nc.psum_tensor(f"psTx{k}", [128, 1024], BF16)) for k in range(3)]


        wq_cast = [0] * 24
        wq_tr = [0] * 24
        wq_ev = [0] * 24
        for j in range(24):
            w = [(c_s, wq_cast[j - 4])] if j >= 4 else ()
            tl = P.dma("sync", out=wsl[j % 4][:], in_=wqkv_h[j * 128:(j + 1) * 128, :],
                       wait=w, inc=s_wsl[j % 4])
            w2 = [(s_wsl[j % 4], tl)]
            if j >= 2:
                w2.append((c_t, wq_tr[j - 2]))
            wq_cast[j] = P.op("scalar",
                              lambda s, j=j: s.copy(out=wslb[j % 2][:], in_=wsl[j % 4][:]),
                              wait=w2, inc=c_s)
            w3 = [(c_s, wq_cast[j]), (s_tab, tTAB)]
            if j >= 2:
                w3.append((c_v, wq_ev[j - 2]))
            wq_tr[j] = transpose8(psTw[j % 2],
                                  lambda c, j=j: wslb[j % 2][:, c * 128:(c + 1) * 128], 8, w3)
            if j < 8:
                dst, W, off = wqkvT_q, D, j * 128
            else:
                dst, W, off = wqkvT_kv, 2 * D, (j - 8) * 128
            wq_ev[j] = P.op(
                "vector",
                lambda v, j=j, dst=dst, W=W, off=off: v.tensor_copy(
                    out=dst[:].rearrange("p (c r) -> p c r", c=8, r=W)[:, :, off:off + 128],
                    in_=psTw[j % 2][:]),
                wait=[(c_t, wq_tr[j])], inc=c_v)
        tWQ_Q = wq_ev[7]
        tWQ_KV = wq_ev[23]

        xn_ready = [0] * nt
        q1_done = [0] * nt
        ev_x = [0] * nt
        xnT = [0] * nt
        yq_t = [0] * nt
        r2_t = [0] * nt
        st_q1 = [0] * nt
        xb_t = [0] * nt
        for i in range(nt):
            k = i % 3
            ca = colA[k]
            w = [(c_s, xn_ready[i - 3])] if i >= 3 else ()
            tl = P.dma("sync", out=xtile[k][:], in_=x_h[i * 128:(i + 1) * 128, :],
                       wait=w, inc=s_x[k])
            wsq = [(s_x[k], tl)]
            if i >= 3:
                wsq.append((c_v, r2_t[i - 3]))
            s2 = P.op("scalar",
                      lambda s, k=k, ca=ca: s.activation(out=y2[k][:], in_=xtile[k][:],
                                                         func=AF.Square, accum_out=ca[:, 1:2]),
                      wait=wsq, inc=c_s)
            P.op("vector", lambda v, k=k, ca=ca: v.reduce_sum(out=ca[:, 0:1], in_=xtile[k][:], axis=AX),
                 wait=[(c_s, s2)])
            P.op("vector", lambda v, ca=ca: v.tensor_scalar(out=ca[:, 2:3], in0=ca[:, 0:1],
                                                            scalar1=INV1024, scalar2=None, op0=OP.mult))
            P.op("vector", lambda v, ca=ca: v.tensor_scalar(out=ca[:, 3:4], in0=ca[:, 1:2],
                                                            scalar1=INV1024, scalar2=None, op0=OP.mult))
            P.op("vector", lambda v, ca=ca: v.scalar_tensor_tensor(
                out=ca[:, 4:5], in0=ca[:, 2:3], scalar=ca[:, 2:3], in1=ca[:, 3:4],
                op0=OP.mult, op1=OP.subtract))
            P.op("vector", lambda v, ca=ca: v.tensor_scalar(
                out=ca[:, 4:5], in0=ca[:, 4:5], scalar1=-1.0, scalar2=EPS_LN,
                op0=OP.mult, op1=OP.add))
            rc = P.op("vector", lambda v, ca=ca: v.reciprocal(out=ca[:, 5:6], in_=ca[:, 4:5]),
                      inc=c_v)
            sq = P.op("scalar", lambda s, ca=ca: s.activation(out=ca[:, 6:7], in_=ca[:, 5:6],
                                                              func=AF.Sqrt),
                      wait=[(c_v, rc)], inc=c_s)
            nb = P.op("vector", lambda v, ca=ca: v.scalar_tensor_tensor(
                out=ca[:, 7:8], in0=ca[:, 2:3], scalar=-1.0, in1=ca[:, 6:7],
                op0=OP.mult, op1=OP.mult), wait=[(c_s, sq)], inc=c_v)
            wxn = [(c_v, nb)]
            if i >= 3:
                wxn.append((c_v, yq_t[i - 3]))
            xn_ready[i] = P.op("scalar", lambda s, k=k, ca=ca: s.activation(
                out=xnf[k][:], in_=xtile[k][:], func=AF.Identity,
                bias=ca[:, 7:8], scale=ca[:, 6:7]), wait=wxn, inc=c_s)
            P.op("vector", lambda v, k=k, ca=ca: v.reduce_max(
                out=ca[:, 0:1], in_=xnf[k][:], axis=AX, apply_absolute_value=True),
                 wait=[(c_s, xn_ready[i])])
            P.op("vector", lambda v, ca=ca: v.tensor_scalar(
                out=ca[:, 1:2], in0=ca[:, 0:1], scalar1=EPS_Q, scalar2=None, op0=OP.max))
            P.op("vector", lambda v, ca=ca: v.reciprocal(out=ca[:, 2:3], in_=ca[:, 1:2]))
            P.op("vector", lambda v, ca=ca, i=i: v.tensor_scalar(
                out=scx_sb[:, i:i + 1], in0=ca[:, 2:3], scalar1=127.0, scalar2=None, op0=OP.mult))
            P.op("vector", lambda v, ca=ca, i=i: v.tensor_scalar(
                out=invsc127[:, i:i + 1], in0=ca[:, 1:2], scalar1=RINV127, scalar2=None, op0=OP.mult))
            P.op("vector", lambda v, ca=ca, i=i: v.scalar_tensor_tensor(
                out=invsc127[:, i:i + 1], in0=ca[:, 1:2], scalar=RINV127_LO,
                in1=invsc127[:, i:i + 1], op0=OP.mult, op1=OP.add))
            wxb = [(c_s, xn_ready[i])]
            if i >= 3:
                wxb.append((c_t, xnT[i - 3]))
            xb_t[i] = P.op("scalar", lambda s, k=k: s.copy(out=xnb[k][:], in_=xnf[k][:]),
                           wait=wxb, inc=c_s)
            wy = [(c_v, q1_done[i - 3])] if i >= 3 else ()
            yq_t[i] = P.op("vector", lambda v, k=k, i=i: v.tensor_scalar(
                out=y0[k][:], in0=xnf[k][:], scalar1=scx_sb[:, i:i + 1], scalar2=1000.0,
                op0=OP.mult, op1=OP.add), wait=wy, inc=c_v)
            P.op("vector", lambda v, k=k: v.tensor_scalar(
                out=y1[k][:], in0=y0[k][:], scalar1=M1, scalar2=M1, op0=OP.add, op1=OP.subtract))
            r2_t[i] = P.op("vector", lambda v, k=k: v.tensor_scalar(
                out=y2[k][:], in0=y0[k][:], scalar1=M2, scalar2=M2, op0=OP.add, op1=OP.subtract),
                inc=c_v)
            P.op("vector", lambda v, k=k: v.scalar_tensor_tensor(
                out=y0[k][:], in0=y2[k][:], scalar=1.0, in1=y1[k][:], op0=OP.is_lt, op1=OP.mult))
            wq1 = [(s_q1s[k], st_q1[i - 3])] if i >= 3 else ()
            q1_done[i] = P.op("vector", lambda v, k=k: v.scalar_tensor_tensor(
                out=q1t[k][:], in0=y0[k][:], scalar=1000.0, in1=y2[k][:],
                op0=OP.subtract, op1=OP.add), wait=wq1, inc=c_v)
            st_q1[i] = P.dma("gpsimd", out=q1_scr[i * 128:(i + 1) * 128, :], in_=q1t[k][:],
                             wait=[(c_v, q1_done[i])], inc=s_q1s[k])
            wtr = [(c_s, xb_t[i]), (s_tab, tTAB)]
            if i >= 3:
                wtr.append((c_v, ev_x[i - 3]))
            xnT[i] = transpose8(psTx[k], lambda c, k=k: xnb[k][:, c * 128:(c + 1) * 128], 8, wtr)
            ev_x[i] = P.op("vector", lambda v, k=k, i=i: v.tensor_copy(
                out=xn_fm[:].rearrange("p (c r) -> p c r", c=8, r=T)[:, :, i * 128:(i + 1) * 128],
                in_=psTx[k][:]), wait=[(c_t, xnT[i])], inc=c_v)

        P.flush()
        pa.close()

        P.phase_barrier()

        pbL = ExitStack()
        kf_sb = pbL.enter_context(nc.sbuf_tensor("kf_sb", [128, nt * D], BF16))
        v_sb = pbL.enter_context(nc.sbuf_tensor("v_sb", [128, nt * D], BF16))
        qf_sb = pbL.enter_context(nc.sbuf_tensor("qf_sb", [128, KD * T], BF16))

        pbM = ExitStack()
        rs1 = pbM.enter_context(nc.sbuf_tensor("rs1", [128, D], F32))
        rs2 = pbM.enter_context(nc.sbuf_tensor("rs2", [128, D], F32))
        qs1, qs2 = rs1, rs2
        qfb0 = pbM.enter_context(nc.sbuf_tensor("qfb0", [128, D], BF16))
        qfb = [qfb0, qfb0]
        psK = pbM.enter_context(nc.psum_tensor("psK", [128, 1024], F32))
        psV = pbM.enter_context(nc.psum_tensor("psV", [128, 1024], F32))
        psQ = pbM.enter_context(nc.psum_tensor("psQ", [128, 1024], F32))
        psTq = pbM.enter_context(nc.psum_tensor("psTq", [128, 1024], BF16))

        def swap_ap(buf):
            return buf[:].rearrange("p (h two j) -> p h two j", two=2, j=32)[:, :, ::-1, :]

        def full_ap(buf):
            return buf[:].rearrange("p (h two j) -> p h two j", two=2, j=32)

        def rtab64(tbl, i):
            return tbl[:, i * 64:(i + 1) * 64].rearrange("p (o two j) -> p o two j", o=1, two=2) \
                      .broadcast_to([128, H, 2, 32])

        kft_t = [0] * nt
        qf_t = [0] * nt
        vev_t = [0] * nt
        krd_t = [0] * nt
        qrd_t = [0] * nt
        qT_t = [0] * nt
        qev_t = [0] * nt

        def emit_qtr(i):
            wtr = [(c_v, qf_t[i])]
            if i >= 1:
                wtr.append((c_s, qev_t[i - 1]))
            qT_t[i] = transpose8(psTq, lambda c, i=i: qfb[i % 2][:, c * 128:(c + 1) * 128],
                                 8, wtr)
            qev_t[i] = P.op("scalar", lambda s, i=i: s.copy(
                out=qf_sb[:].rearrange("p (c r) -> p c r", c=8, r=T)[:, :, i * 128:(i + 1) * 128],
                in_=psTq[:]), wait=[(c_t, qT_t[i])], inc=c_s)

        def mm_group(ps, rhs_buf, rhs_stride, rhs_off, wfirst, i):
            for sh in range(2):
                for c in range(KD):
                    P.op("tensor", lambda t, i=i, sh=sh, c=c, ps=ps, rb=rhs_buf,
                         rs=rhs_stride, ro=rhs_off: t.matmul(
                        out=ps[:, sh * 512:(sh + 1) * 512],
                        lhsT=xn_fm[:, c * T + i * 128: c * T + (i + 1) * 128],
                        rhs=rb[:, c * rs + ro + sh * 512: c * rs + ro + (sh + 1) * 512],
                        start=(c == 0), stop=(c == KD - 1)),
                        wait=wfirst if (sh == 0 and c == 0) else (),
                        inc=c_t if (sh == 1 and c == KD - 1) else None)
            return c_t.n

        def rope_elu(ps, s1, s2, i, dst_fn, qscale, wfirst):
            P.op("vector", lambda v, ps=ps, i=i, s1=s1: v.tensor_tensor(
                out=full_ap(s1), in0=full_ap(ps), in1=rtab64(tb["cos64"], i), op=OP.mult),
                wait=wfirst)
            rd = P.op("vector", lambda v, ps=ps, i=i, s2=s2: v.tensor_tensor(
                out=full_ap(s2), in0=swap_ap(ps), in1=rtab64(tb["sinpat"], i), op=OP.mult),
                inc=c_v)
            P.op("vector", lambda v, s1=s1, s2=s2: v.tensor_tensor(
                out=s1[:], in0=s1[:], in1=s2[:], op=OP.add))
            mx = P.op("vector", lambda v, s1=s1, s2=s2: v.tensor_scalar(
                out=s2[:], in0=s1[:], scalar1=0.0, scalar2=None, op0=OP.min), inc=c_v)
            if qscale:
                eq = P.op("scalar", lambda s, s2=s2: s.activation(
                    out=s2[:], in_=s2[:], func=AF.Exp, scale=qscale),
                    wait=[(c_v, mx)], inc=c_s)
                P.op("vector", lambda v, s1=s1: v.tensor_scalar(
                    out=s1[:], in0=s1[:], scalar1=0.0, scalar2=qscale, op0=OP.max, op1=OP.mult))
            else:
                eq = P.op("scalar", lambda s, s2=s2: s.activation(
                    out=s2[:], in_=s2[:], func=AF.Exp), wait=[(c_v, mx)], inc=c_s)
                P.op("vector", lambda v, s1=s1: v.tensor_scalar(
                    out=s1[:], in0=s1[:], scalar1=0.0, scalar2=None, op0=OP.max))
            ft = P.op("vector", lambda v, s1=s1, s2=s2: v.tensor_tensor(
                out=dst_fn(), in0=s1[:], in1=s2[:], op=OP.add),
                wait=[(c_s, eq)], inc=c_v)
            return rd, ft

        for i in range(nt):
            wk = [(c_v, ev_x[i]), (c_v, tWQ_KV)]
            if i >= 1:
                wk.append((c_v, krd_t[i - 1]))
            kMMi = mm_group(psK, wqkvT_kv, 2 * D, 0, wk, i)
            wv = [(c_s, vev_t[i - 1])] if i >= 1 else ()
            vMMi = mm_group(psV, wqkvT_kv, 2 * D, 1024, wv, i)
            wq = [(c_v, tWQ_Q)]
            if i >= 1:
                wq.append((c_v, qrd_t[i - 1]))
            qMMi = mm_group(psQ, wqkvT_q, D, 0, wq, i)
            if i >= 1:
                emit_qtr(i - 1)
            wkr = [(c_t, kMMi)]
            if i >= 1:
                wkr.append((c_v, kft_t[i - 1]))
            krd_t[i], kft_t[i] = rope_elu(
                psK, rs1, rs2, i, lambda i=i: kf_sb[:, i * D:(i + 1) * D], None, wkr)
            vev_t[i] = P.op("scalar", lambda s, i=i: s.copy(
                out=v_sb[:, i * D:(i + 1) * D], in_=psV[:, :]),
                wait=[(c_t, vMMi)], inc=c_s)
            wqr = [(c_t, qMMi)]
            if i >= 1:
                wqr.append((c_v, qf_t[i - 1]))
                wqr.append((c_t, qT_t[i - 1]))
            qrd_t[i], qf_t[i] = rope_elu(
                psQ, qs1, qs2, i, lambda i=i: qfb[i % 2][:], 0.125, wqr)

            if i == nt - 1:
                emit_qtr(i)
            emit_mean_jobs(192 // nt + 1)
        emit_mean_jobs(len(_mp_jobs))
        tQF = qev_t[nt - 1]

        P.flush()
        pbM.close()
        P.phase_barrier(waits=[(c_v, kft_t[nt - 1]), (c_s, qev_t[nt - 1]),
                               (c_t, qT_t[nt - 1])])
        rkv.close()
        right.close()


        pt = ExitStack()
        psB = pt.enter_context(nc.psum_tensor("psB", [128, 512], F32))
        for nm in ("g", "u", "d"):
            ci = mci[nm]
            tp = P.op("tensor", lambda t, ci=ci: t.transpose(
                out=psB[0:1, 128:256], in_=rscol[:, ci:ci + 1], identity=tb["ident_f"][:]),
                wait=[(c_v, _mp["tick"])], inc=c_t)
            ev = P.op("vector", lambda v: v.tensor_copy(out=mrow[0:1, 0:128], in_=psB[0:1, 128:256]),
                      wait=[(c_t, tp)], inc=c_v)
            P.op("vector", lambda v: v.tensor_scalar(
                out=mrow[0:1, 128:256], in0=mrow[0:1, 0:128], scalar1=MG, scalar2=MG,
                op0=OP.add, op1=OP.subtract))
            P.op("vector", lambda v: v.tensor_tensor(
                out=mrow[0:1, 256:384], in0=mrow[0:1, 0:128], in1=mrow[0:1, 128:256],
                op=OP.subtract))
            P.op("vector", lambda v: v.reduce_sum(out=stat1[0:1, 0:1], in_=mrow[0:1, 128:256], axis=AX))
            P.op("vector", lambda v: v.reduce_sum(out=stat1[0:1, 1:2], in_=mrow[0:1, 256:384], axis=AX))
            P.op("vector", lambda v: v.tensor_tensor(
                out=stat1[0:1, 2:3], in0=stat1[0:1, 0:1], in1=stat1[0:1, 1:2], op=OP.add))
            P.op("vector", lambda v: v.tensor_scalar(out=stat1[0:1, 3:4], in0=stat1[0:1, 2:3],
                                                     scalar1=INV2P22, scalar2=EPS_Q,
                                                     op0=OP.mult, op1=OP.max))
            P.op("vector", lambda v: v.tensor_scalar(out=stat1[0:1, 4:5], in0=stat1[0:1, 3:4],
                                                     scalar1=-0.5, scalar2=None, op0=OP.mult))
            bc0 = P.op("vector", lambda v: v.tensor_scalar(out=stat1[0:1, 5:6], in0=stat1[0:1, 3:4],
                                                           scalar1=0.5, scalar2=None, op0=OP.mult),
                       inc=c_v)
            last = 0
            for (srcc, dstb) in ((3, wmean_b), (4, wbias_n), (5, wbias_p)):
                bcm = P.op("tensor", lambda t, srcc=srcc: t.matmul(
                    out=psB[:, 0:1], lhsT=tb["onesrow_f"][:], rhs=stat1[0:1, srcc:srcc + 1],
                    start=True, stop=True), wait=[(c_v, bc0), (c_v, last)] if last else [(c_v, bc0)],
                    inc=c_t)
                last = P.op("vector", lambda v, ci=ci, dstb=dstb: v.tensor_copy(
                    out=dstb[:, ci:ci + 1], in_=psB[:, 0:1]), wait=[(c_t, bcm)], inc=c_v)
            mean_ready[nm] = last
        tMEAN = mean_ready["d"]
        tmp = 0
        for nm, dstc in (("g", ag_sb), ("u", au_sb)):
            tmp = P.op("vector", lambda v, dstc=dstc, ci=mci[nm]: v.tensor_scalar(
                out=dstc[:, :], in0=invsc127[:, :], scalar1=wmean_b[:, ci:ci + 1],
                scalar2=0.5, op0=OP.mult, op1=OP.mult),
                wait=[(c_v, tMEAN), (c_v, q1_done[nt - 1])], inc=c_v)
        tAGU = tmp
        P.flush()
        pt.close()

        pb3 = ExitStack()
        kv_sb = pb3.enter_context(nc.sbuf_tensor("kv_sb", [128, 8 * 128], BF16))
        ks_sb = pb3.enter_context(nc.sbuf_tensor("ks_sb", [128, 16], BF16))
        rnb0 = [pb3.enter_context(nc.sbuf_tensor(f"rnb0{k}", [2, 512], F32)) for k in range(2)]
        rnbs = [pb3.enter_context(nc.sbuf_tensor(f"rnbs{k}", [128, 512], F32)) for k in range(2)]
        attn_fm = pb3.enter_context(nc.sbuf_tensor("attn_fm", [128, KD * T], BF16))
        woutT = pb3.enter_context(nc.sbuf_tensor("woutT", [128, KD * D], BF16))
        wosl = [pb3.enter_context(nc.sbuf_tensor(f"wosl{k}", [128, D], F32)) for k in range(2)]
        woslb = [pb3.enter_context(nc.sbuf_tensor(f"woslb{k}", [128, D], BF16)) for k in range(2)]
        ao = [pb3.enter_context(nc.sbuf_tensor(f"ao{k}", [128, D], BF16)) for k in range(2)]
        psKV = pb3.enter_context(nc.psum_tensor("psKV", [128, 512], F32))
        psKS = pb3.enter_context(nc.psum_tensor("psKS", [128, 512], F32))
        wdsg = [pb3.enter_context(nc.sbuf_tensor(f"wdsg{k}", [128, 512], BF16)) for k in range(2)]
        wdtr = [pb3.enter_context(nc.sbuf_tensor(f"wdtr{k}", [128, 512], BF16)) for k in range(2)]
        pbo = ExitStack()
        psTo = [pbo.enter_context(nc.psum_tensor(f"psTo{k}", [128, 1024], BF16)) for k in range(2)]

        _wd = dict(ld=[0, 0, 0, 0], tl=[0, 0, 0, 0], sgn=[0, 0], st=[0, 0],
                   li=0, pi=0)

        def emit_wd_load(n):
            for _ in range(n):
                ix = _wd["li"]
                if ix >= 64 or ix >= _wd["pi"] + 4:
                    return
                _wd["li"] = ix + 1
                d8, cc = ix // 8, ix % 8
                slot = ix % 4
                wld = [(c_v, _wd["ld"][slot])] if _wd["ld"][slot] else [(c_v, tMEAN)]
                _wd["tl"][slot] = P.dma("gpsimd", out=m_wsl[slot][:],
                                        in_=wd_h[d8 * 128:(d8 + 1) * 128,
                                                 cc * 512:(cc + 1) * 512],
                                        wait=wld, inc=s_msl[slot])

        def emit_wd_proc(n):
            for _ in range(n):
                ix = _wd["pi"]
                if ix >= 64:
                    return
                if ix >= _wd["li"]:
                    emit_wd_load(1)
                _wd["pi"] = ix + 1
                d8, cc = ix // 8, ix % 8
                slot = ix % 4
                par = ix % 2
                wsg = [(s_msl[slot], _wd["tl"][slot]), (c_v, tMEAN)]
                if _wd["sgn"][par]:
                    wsg.append((c_v, _wd["sgn"][par]))
                P.op("scalar", lambda s, par=par, slot=slot: s.activation(
                    out=wdsg[par][:], in_=m_wsl[slot][:], func=AF.Sign,
                    bias=wbias_n[:, 2:3]), wait=wsg, inc=c_s)
                P.op("scalar", lambda s, slot=slot: s.activation(
                    out=m_wsl[slot][:], in_=m_wsl[slot][:], func=AF.Sign,
                    bias=wbias_p[:, 2:3]), inc=c_s)
                s2t = c_s.n
                wtt = [(c_s, s2t)]
                if _wd["st"][par]:
                    wtt.append((s_wts[par], _wd["st"][par]))
                tt = P.op("vector", lambda v, par=par, slot=slot: v.tensor_tensor(
                    out=wdtr[par][:], in0=wdsg[par][:], in1=m_wsl[slot][:], op=OP.add),
                    wait=wtt, inc=c_v)
                _wd["sgn"][par] = tt
                _wd["ld"][slot] = tt
                _wd["st"][par] = P.dma("gpsimd",
                                       out=wdt_scr[d8 * 128:(d8 + 1) * 128,
                                                   cc * 512:(cc + 1) * 512],
                                       in_=wdtr[par][:], wait=[(c_v, tt)], inc=s_wts[par])

        emit_wd_load(4)

        nq4 = T // 512 if T >= 512 else 1
        qw = min(T, 512)
        wo_cast = [0] * 8
        wo_tr = [0] * 8
        wo_ev = [0] * 8
        for j in range(8):
            w = [(c_s, wo_cast[j - 2])] if j >= 2 else ()
            tl = P.dma("sync", out=wosl[j % 2][:], in_=wout_h[j * 128:(j + 1) * 128, :],
                       wait=w, inc=s_wo[j % 2])
            wcst = [(s_wo[j % 2], tl)]
            if j >= 2:
                wcst.append((c_t, wo_tr[j - 2]))
            wo_cast[j] = P.op("scalar", lambda s, j=j: s.copy(
                out=woslb[j % 2][:], in_=wosl[j % 2][:]), wait=wcst, inc=c_s)
            wtr = [(c_s, wo_cast[j])]
            if j >= 2:
                wtr.append((c_v, wo_ev[j - 2]))
            wo_tr[j] = transpose8(psTo[j % 2],
                                  lambda c, j=j: woslb[j % 2][:, c * 128:(c + 1) * 128], 8, wtr)
            wo_ev[j] = P.op("vector", lambda v, j=j: v.tensor_copy(
                out=woutT[:].rearrange("p (c r) -> p c r", c=8, r=D)[:, :, j * 128:(j + 1) * 128],
                in_=psTo[j % 2][:]), wait=[(c_t, wo_tr[j])], inc=c_v)
            emit_wd_load(1)
            emit_wd_proc(1)
        tWOUT = wo_ev[7]
        pbo.close()
        pp2 = ExitStack()
        psD = pp2.enter_context(nc.psum_tensor("psD", [2, 512], F32))
        psR = [pp2.enter_context(nc.psum_tensor(f"psR{k}", [128, 512], F32)) for k in range(2)]
        psN = [pp2.enter_context(nc.psum_tensor(f"psN{k}", [128, 512], F32)) for k in range(2)]

        kvev = [0] * KD

        def emit_kv_pair(qc):
            wkv = [(c_v, kft_t[nt - 1]), (c_s, vev_t[nt - 1])]
            if qc >= 1:
                wkv.append((c_v, kvev[qc - 1]))
            for i2 in range(nt):
                P.op("tensor", lambda t, qc=qc, i2=i2: t.matmul(
                    out=psKV[:, 0:128],
                    lhsT=kf_sb[:, i2 * D + qc * 128: i2 * D + (qc + 1) * 128],
                    rhs=v_sb[:, i2 * D + qc * 128: i2 * D + (qc + 1) * 128],
                    start=(i2 == 0), stop=(i2 == nt - 1)),
                    wait=wkv if i2 == 0 else ())
                P.op("tensor", lambda t, qc=qc, i2=i2: t.matmul(
                    out=psKS[:, 0:1],
                    lhsT=kf_sb[:, i2 * D + qc * 128: i2 * D + (qc + 1) * 128],
                    rhs=tb["ones_col_b"][:],
                    start=(i2 == 0), stop=(i2 == nt - 1)),
                    inc=c_t if i2 == nt - 1 else None)
            kvmm = c_t.n
            P.op("vector", lambda v, qc=qc: v.tensor_copy(
                out=kv_sb[:, qc * 128:(qc + 1) * 128], in_=psKV[:, 0:128]),
                wait=[(c_t, kvmm)])
            P.op("vector", lambda v, qc=qc: v.memset(ks_sb[:, 2 * qc:2 * qc + 2], 0.0))
            P.op("vector", lambda v, qc=qc: v.tensor_copy(
                out=ks_sb[0:64, 2 * qc:2 * qc + 1], in_=psKS[0:64, 0:1]))
            kvev[qc] = P.op("vector", lambda v, qc=qc: v.tensor_copy(
                out=ks_sb[64:128, 2 * qc + 1:2 * qc + 2], in_=psKS[64:128, 0:1]), inc=c_v)

        emit_kv_pair(0)
        att_t = {}
        mx_t = {}
        rev_t = {}
        rn_t = {}
        it = 0
        for p in range(KD):
            if p + 1 < KD:
                emit_kv_pair(p + 1)
            for st4 in range(nq4):
                kd = it % 2
                wdn = [(c_v, kvev[p]), (c_s, tQF)]
                if it >= 1:
                    wdn.append((c_v, mx_t[it - 1]))
                dnMM = P.op("tensor", lambda t, p=p, st4=st4, kd=kd: t.matmul(
                    out=psD[:, 0:qw], lhsT=ks_sb[:, 2 * p:2 * p + 2],
                    rhs=qf_sb[:, p * T + st4 * qw: p * T + (st4 + 1) * qw],
                    start=True, stop=True), wait=wdn, inc=c_t)
                wmx = [(c_t, dnMM)]
                if it >= 2:
                    wmx.append((c_t, rn_t[it - 2]))
                mx_t[it] = P.op("vector", lambda v, kd=kd: v.tensor_scalar(
                    out=rnb0[kd][:, 0:qw], in0=psD[:, 0:qw], scalar1=EPS_DEN,
                    scalar2=None, op0=OP.max), wait=wmx, inc=c_v)
                rrc = P.op("vector", lambda v, kd=kd: v.reciprocal(
                    out=rnb0[kd][:, 0:qw], in_=rnb0[kd][:, 0:qw]), inc=c_v)
                wrn = [(c_v, rrc)]
                if it >= 2:
                    wrn.append((c_v, rev_t[it - 2]))
                rn_t[it] = P.op("tensor", lambda t, kd=kd: t.matmul(
                    out=psR[kd][:, 0:qw], lhsT=tb["sel2"][:], rhs=rnb0[kd][:, 0:qw],
                    start=True, stop=True), wait=wrn, inc=c_t)
                rev_t[it] = P.op("vector", lambda v, kd=kd: v.tensor_copy(
                    out=rnbs[kd][:, 0:qw], in_=psR[kd][:, 0:qw]),
                    wait=[(c_t, rn_t[it])], inc=c_v)
                wn0 = []
                if it >= 2:
                    wn0.append((c_v, att_t[it - 2]))
                P.op("tensor", lambda t, p=p, st4=st4, kd=kd: t.matmul(
                    out=psN[kd][0:64, 0:qw], lhsT=kv_sb[0:64, p * 128: p * 128 + 64],
                    rhs=qf_sb[0:64, p * T + st4 * qw: p * T + (st4 + 1) * qw],
                    start=True, stop=True, tile_position=(0, 0)), wait=wn0)
                nmm = P.op("tensor", lambda t, p=p, st4=st4, kd=kd: t.matmul(
                    out=psN[kd][64:128, 0:qw], lhsT=kv_sb[64:128, p * 128 + 64: p * 128 + 128],
                    rhs=qf_sb[64:128, p * T + st4 * qw: p * T + (st4 + 1) * qw],
                    start=True, stop=True, tile_position=(64, 64)), inc=c_t)
                att_t[it] = P.op("vector", lambda v, p=p, st4=st4, kd=kd: v.tensor_tensor(
                    out=attn_fm[:, p * T + st4 * qw: p * T + (st4 + 1) * qw],
                    in0=psN[kd][:, 0:qw], in1=rnbs[kd][:, 0:qw], op=OP.mult),
                    wait=[(c_v, rev_t[it]), (c_t, nmm)], inc=c_v)
                emit_wd_load(1)
                emit_wd_proc(1)
                it += 1
        last_att = att_t[it - 1]
        tATTN = last_att
        pp2.close()
        psW = [pb3.enter_context(nc.psum_tensor(f"psW{k}", [128, 512], F32)) for k in range(2)]




        st_at = [0] * nt
        aev_t = [0] * nt
        for i in range(nt):
            for sh in range(2):
                w0 = [(c_v, tATTN), (c_v, tWOUT), (c_v, tAGU)]
                if i >= 1:
                    w0.append((c_s, aev_t[i - 1]))
                for c in range(KD):
                    P.op("tensor", lambda t, i=i, sh=sh, c=c: t.matmul(
                        out=psW[sh][:, :],
                        lhsT=attn_fm[:, c * T + i * 128: c * T + (i + 1) * 128],
                        rhs=woutT[:, c * D + sh * 512: c * D + (sh + 1) * 512],
                        start=(c == 0), stop=(c == KD - 1)),
                        wait=w0 if c == 0 else (), inc=c_t if c == KD - 1 else None)
                mmW = c_t.n
                wev = [(c_t, mmW)]
                if sh == 0 and i >= 2:
                    wev.append((s_at[i % 2], st_at[i - 2]))
                P.op("scalar", lambda s, i=i, sh=sh: s.copy(
                    out=ao[i % 2][:, sh * 512:(sh + 1) * 512], in_=psW[sh][:, :]),
                    wait=wev, inc=c_s)
            aev_t[i] = c_s.n
            st_at[i] = P.dma("gpsimd", out=attn_scr[i * 128:(i + 1) * 128, :], in_=ao[i % 2][:],
                             wait=[(c_s, aev_t[i])], inc=s_at[i % 2])
            emit_wd_load(1)
            emit_wd_proc(1)
        while _wd["pi"] < 64:
            emit_wd_load(1)
            emit_wd_proc(1)
        tWTS = [(s_wts[0], s_wts[0].n), (s_wts[1], s_wts[1].n)]
        tATST = [(s_at[0], s_at[0].n), (s_at[1], s_at[1].n)]

        P.phase_barrier(waits=[(c_v, tAGU), (c_s, _mp["prev_ld"][0]),
                               (c_s, _mp["prev_ld"][1])])
        P.flush()
        pb3.close()
        pbL.close()
        pm.close()
        P.phase_barrier()

        pouter = ExitStack()
        wdT = pouter.enter_context(nc.sbuf_tensor("wdT", [128, NF * D], BF16))
        _wl = dict(idx=0)

        def emit_wdT(n):
            for _ in range(n):
                c = _wl["idx"]
                if c >= NF:
                    return
                _wl["idx"] = c + 1
                P.dma("sync", out=wdT[:, c * D:(c + 1) * D],
                      in_=wdt_scr[:, c * 128:(c + 1) * 128],
                      wait=list(tWTS) if c == 0 else (), inc=s_wdl, transpose=True)

        pc1 = ExitStack()
        q1fm = pc1.enter_context(nc.sbuf_tensor("q1fm", [128, KD * T], BF16))
        q1l = 0
        for c in range(KD):
            q1l = P.dma("sync", out=q1fm[:, c * T:(c + 1) * T],
                        in_=q1_scr[:, c * 128:(c + 1) * 128],
                        wait=[(s, s.n) for s in s_q1s] if c == 0 else (),
                        inc=s_q1l, transpose=True)
        tQ1L = q1l

        gsl = [pc1.enter_context(nc.sbuf_tensor(f"gsl{k}", [128, D], F32)) for k in range(4)]
        sgn = [pc1.enter_context(nc.sbuf_tensor(f"sgn{k}", [128, D], BF16)) for k in range(2)]
        trn = [pc1.enter_context(nc.sbuf_tensor(f"trn{k}", [128, D], BF16)) for k in range(2)]
        wgTb = [pc1.enter_context(nc.sbuf_tensor(f"wgTb{k}", [128, 4096], BF16)) for k in range(2)]
        wuTb = [pc1.enter_context(nc.sbuf_tensor(f"wuTb{k}", [128, 4096], BF16)) for k in range(2)]
        hrelu = [pc1.enter_context(nc.sbuf_tensor(f"hrelu{k}", [128, 512], F32)) for k in range(3)]
        hup = [pc1.enter_context(nc.sbuf_tensor(f"hup{k}", [128, 512], F32)) for k in range(3)]
        hsq = [pc1.enter_context(nc.sbuf_tensor(f"hsq{k}", [128, 512], F32)) for k in range(3)]
        htile = [pc1.enter_context(nc.sbuf_tensor(f"htile{k}", [128, 512], F32)) for k in range(3)]
        hcol = pc1.enter_context(nc.sbuf_tensor("hcol", [128, 2], F32))
        psG = [pc1.enter_context(nc.psum_tensor(f"psG{k}", [128, 512], F32)) for k in range(3)]
        psU = [pc1.enter_context(nc.psum_tensor(f"psU{k}", [128, 512], F32)) for k in range(3)]
        psTg = [pc1.enter_context(nc.psum_tensor(f"psTg{k}", [128, 1024], BF16)) for k in range(2)]

        zh = P.op("vector", lambda v: v.memset(hmax_sb[:, :], 0.0), inc=c_v)

        state = dict(ld=[0, 0, 0, 0], tl=[0, 0, 0, 0], sgn=[0, 0], trn=[0, 0],
                     ev=[0, 0], slot=0, par=0, lslot=0)

        def prep_load(mat_h, j):
            slot = state["lslot"] % 4
            state["lslot"] += 1
            wld = [(c_v, state["ld"][slot])] if state["ld"][slot] else []
            state["tl"][slot] = P.dma("sync", out=gsl[slot][:],
                                      in_=mat_h[j * 128:(j + 1) * 128, :],
                                      wait=wld, inc=s_gu[slot])

        def prep_proc(ci, j, dstbuf, bf):
            slot = state["slot"] % 4
            par = state["par"] % 2
            state["slot"] += 1
            state["par"] += 1
            wsg = [(s_gu[slot], state["tl"][slot]), (c_v, tMEAN)]
            if state["sgn"][par]:
                wsg.append((c_v, state["sgn"][par]))
            P.op("scalar", lambda s, slot=slot, par=par, ci=ci: s.activation(
                out=sgn[par][:], in_=gsl[slot][:], func=AF.Sign,
                bias=wbias_n[:, ci:ci + 1]), wait=wsg, inc=c_s)
            P.op("scalar", lambda s, slot=slot, ci=ci: s.activation(
                out=gsl[slot][:], in_=gsl[slot][:], func=AF.Sign,
                bias=wbias_p[:, ci:ci + 1]), inc=c_s)
            s2t = c_s.n
            wtt = [(c_s, s2t)]
            if state["trn"][par]:
                wtt.append((c_t, state["trn"][par]))
            tt = P.op("vector", lambda v, slot=slot, par=par: v.tensor_tensor(
                out=trn[par][:], in0=sgn[par][:], in1=gsl[slot][:], op=OP.add),
                wait=wtt, inc=c_v)
            state["sgn"][par] = tt
            state["ld"][slot] = tt
            wtr = [(c_v, tt)]
            if state["ev"][par]:
                wtr.append((c_v, state["ev"][par]))
            trt = transpose8(psTg[par], lambda c, par=par: trn[par][:, c * 128:(c + 1) * 128],
                             8, wtr)
            state["trn"][par] = trt
            jj = j % 4
            wev = [(c_t, trt)]
            if bf[par]:
                wev.append((c_t, bf[par]))
            ev = P.op("vector", lambda v, dstbuf=dstbuf, jj=jj, par=par: v.tensor_copy(
                out=dstbuf[:].rearrange("p (c r) -> p c r", c=8, r=512)[:, :, jj * 128:(jj + 1) * 128],
                in_=psTg[par][:]), wait=wev, inc=c_v)
            state["ev"][par] = ev
            return ev

        relu_t = {}
        uev_t = {}
        hdone = {}
        hst = {}
        mm_strip = [0] * 8
        prep_ql = []
        prep_qp = []

        def pop_load(n):
            for _ in range(n):
                if prep_ql:
                    m, j = prep_ql.pop(0)
                    prep_load(m, j)

        def pop_proc(n):
            for _ in range(n):
                if prep_qp:
                    ci, j, dst, bf = prep_qp.pop(0)
                    prep_proc(ci, j, dst, bf)

        for jj in range(4):
            prep_load(wg_h, jj)
            prep_load(wu_h, jj)
            prep_proc(0, jj, wgTb[0], [0, 0])
            prep_proc(1, jj, wuTb[0], [0, 0])
        for s8 in range(8):
            last_ev = list(state["ev"])
            if s8 + 1 < 8:
                bf = [mm_strip[s8 - 1], mm_strip[s8 - 1]] if s8 >= 1 else [0, 0]
                for jj in range(4):
                    prep_ql.append((wg_h, (s8 + 1) * 4 + jj))
                    prep_ql.append((wu_h, (s8 + 1) * 4 + jj))
                    prep_qp.append((0, (s8 + 1) * 4 + jj, wgTb[(s8 + 1) % 2], bf))
                    prep_qp.append((1, (s8 + 1) * 4 + jj, wuTb[(s8 + 1) % 2], bf))
                pop_load(4)
                for _ in range(8):
                    pop_proc(1)
                    pop_load(1)
            for i in range(nt):
                prev = (s8, i - 3) if i >= 3 else (s8 - 1, nt - 3 + i)
                gl = (s8 * nt + i) % 3
                wg0 = [(c_v, last_ev[0]), (c_v, last_ev[1]), (s_q1l, tQ1L), (c_v, tAGU)]
                if prev in uev_t:
                    wg0.append((c_s, uev_t[prev]))
                for c in range(KD):
                    P.op("tensor", lambda t, gl=gl, s8=s8, i=i, c=c: t.matmul(
                        out=psG[gl][:, :],
                        lhsT=q1fm[:, c * T + i * 128: c * T + (i + 1) * 128],
                        rhs=wgTb[s8 % 2][:, c * 512:(c + 1) * 512],
                        start=(c == 0), stop=(c == KD - 1)),
                        wait=wg0 if c == 0 else (), inc=c_t if c == KD - 1 else None)
                gmm = c_t.n
                for c in range(KD):
                    P.op("tensor", lambda t, gl=gl, s8=s8, i=i, c=c: t.matmul(
                        out=psU[gl][:, :],
                        lhsT=q1fm[:, c * T + i * 128: c * T + (i + 1) * 128],
                        rhs=wuTb[s8 % 2][:, c * 512:(c + 1) * 512],
                        start=(c == 0), stop=(c == KD - 1)),
                        inc=c_t if c == KD - 1 else None)
                umm = c_t.n
                wrl = [(c_t, gmm)]
                if prev in hdone:
                    wrl.append((c_v, hdone[prev]))
                relu_t[(s8, i)] = P.op("scalar", lambda s, gl=gl, i=i: s.activation(
                    out=hrelu[gl][:], in_=psG[gl][:, :], func=AF.Relu,
                    scale=ag_sb[:, i:i + 1]), wait=wrl, inc=c_s)
                uev_t[(s8, i)] = P.op("scalar", lambda s, gl=gl, i=i: s.activation(
                    out=hup[gl][:], in_=psU[gl][:, :], func=AF.Identity,
                    scale=au_sb[:, i:i + 1]), wait=[(c_t, umm)], inc=c_s)
                sqt = P.op("scalar", lambda s, gl=gl: s.activation(
                    out=hsq[gl][:], in_=hrelu[gl][:], func=AF.Square), inc=c_s)
                wht = [(c_s, sqt), (c_s, uev_t[(s8, i)])]
                if prev in hst:
                    wht.append((s_hw[gl], hst[prev]))
                ht = P.op("vector", lambda v, gl=gl: v.tensor_tensor(
                    out=htile[gl][:], in0=hsq[gl][:], in1=hup[gl][:], op=OP.mult),
                    wait=wht, inc=c_v)
                P.op("vector", lambda v, gl=gl: v.reduce_max(
                    out=hcol[:, 0:1], in_=htile[gl][:], axis=AX, apply_absolute_value=True),
                    wait=[(c_v, zh)])
                hdone[(s8, i)] = P.op("vector", lambda v, i=i: v.tensor_tensor(
                    out=hmax_sb[:, i:i + 1], in0=hmax_sb[:, i:i + 1], in1=hcol[:, 0:1],
                    op=OP.max), inc=c_v)
                hst[(s8, i)] = P.dma("gpsimd",
                                     out=h_scr[i * 128:(i + 1) * 128, s8 * 512:(s8 + 1) * 512],
                                     in_=htile[gl][:], wait=[(c_v, hdone[(s8, i)])],
                                     inc=s_hw[gl])
            mm_strip[s8] = umm
            if s8 >= 2:
                emit_wdT(6 if s8 < 6 else 4)
        emit_wdT(NF)
        tWDL = s_wdl.n
        tHST = [(s_hw[0], s_hw[0].n), (s_hw[1], s_hw[1].n)]
        tHMAX = hdone[(7, nt - 1)]

        P.flush()
        pc1.close()
        P.phase_barrier()

        pc2 = ExitStack()
        hbuf = [pc2.enter_context(nc.sbuf_tensor(f"hbuf{k}", [128, 1024], F32)) for k in range(8)]
        q2t = [pc2.enter_context(nc.sbuf_tensor(f"q2t{k}", [128, 4096], BF16)) for k in range(3)]
        q2fm = [pc2.enter_context(nc.sbuf_tensor(f"q2fm{k}", [128, 4096], BF16)) for k in range(2)]
        xre = [pc2.enter_context(nc.sbuf_tensor(f"xre{k}", [128, D], F32)) for k in range(2)]
        are = [pc2.enter_context(nc.sbuf_tensor(f"are{k}", [128, D], BF16)) for k in range(2)]
        zz = [[pc2.enter_context(nc.sbuf_tensor(f"z{a}{b}", [128, 1024], F32))
               for a in range(3)] for b in range(2)]

        psDn = [pc2.enter_context(nc.psum_tensor(f"psDn{k}", [128, 1024], F32)) for k in range(3)]
        psTd = [pc2.enter_context(nc.psum_tensor(f"psTd{k}", [128, 1024], BF16)) for k in range(2)]

        P.op("vector", lambda v: v.tensor_scalar(out=sch_sb[:, :], in0=hmax_sb[:, :],
                                                 scalar1=EPS_Q, scalar2=None, op0=OP.max),
             wait=[(c_v, tHMAX)])
        P.op("vector", lambda v: v.tensor_scalar(out=ad_sb[:, :], in0=sch_sb[:, :],
                                                 scalar1=RINV127, scalar2=None, op0=OP.mult))
        P.op("vector", lambda v: v.scalar_tensor_tensor(
            out=ad_sb[:, :], in0=sch_sb[:, :], scalar=RINV127_LO,
            in1=ad_sb[:, :], op0=OP.mult, op1=OP.add))
        P.op("vector", lambda v: v.tensor_scalar(out=ad_sb[:, :], in0=ad_sb[:, :],
                                                 scalar1=wmean_b[:, 2:3], scalar2=0.5,
                                                 op0=OP.mult, op1=OP.mult),
             wait=[(c_v, tMEAN)])
        P.op("vector", lambda v: v.reciprocal(out=sch_sb[:, :], in_=sch_sb[:, :]))
        tSCH = P.op("vector", lambda v: v.tensor_scalar(out=sch_sb[:, :], in0=sch_sb[:, :],
                                                        scalar1=127.0, scalar2=None, op0=OP.mult),
                    inc=c_v)

        st_out = [0] * nt
        base_t = [0] * nt
        dnev2 = {}
        q2done = {}
        q2T = {}
        q2ev = {}
        tail_ev = [0, 0]
        dn_last = [0] * nt
        hb_y = [0] * 8
        hload_t = {}

        def emit_hload(i):
            for qq in range(4):
                gidx = i * 4 + qq
                hs = gidx % 8
                wldh = [(c_v, hb_y[hs])] if hb_y[hs] else list(tHST)
                hload_t[(i, qq)] = P.dma("sync" if hs % 2 == 0 else "gpsimd",
                                         out=hbuf[hs][:],
                                         in_=h_scr[i * 128:(i + 1) * 128,
                                                   qq * 1024:(qq + 1) * 1024],
                                         wait=wldh, inc=s_hl[hs])

        def emit_qmath(i):
            k = i % 3
            for qq in range(4):
                gidx = i * 4 + qq
                hs = gidx % 8
                par = qq % 2
                z0, z1, z2 = zz[par]
                pq = (i, qq - 2) if qq >= 2 else (i - 1, qq + 2)
                wyq = [(s_hl[hs], hload_t[(i, qq)]), (c_v, tSCH)]
                yt = P.op("vector", lambda v, hs=hs, i=i, z0=z0: v.tensor_scalar(
                    out=z0[:], in0=hbuf[hs][:], scalar1=sch_sb[:, i:i + 1], scalar2=1000.0,
                    op0=OP.mult, op1=OP.add), wait=wyq, inc=c_v)
                hb_y[hs] = yt
                P.op("vector", lambda v, z0=z0, z1=z1: v.tensor_scalar(
                    out=z1[:], in0=z0[:], scalar1=M1, scalar2=M1, op0=OP.add, op1=OP.subtract))
                P.op("vector", lambda v, z0=z0, z2=z2: v.tensor_scalar(
                    out=z2[:], in0=z0[:], scalar1=M2, scalar2=M2, op0=OP.add, op1=OP.subtract))
                P.op("vector", lambda v, z0=z0, z1=z1, z2=z2: v.scalar_tensor_tensor(
                    out=z0[:], in0=z2[:], scalar=1.0, in1=z1[:], op0=OP.is_lt, op1=OP.mult))
                wq2 = [(c_t, q2T[(i - 3, qq)])] if (i - 3, qq) in q2T else []
                q2done[(i, qq)] = P.op("vector", lambda v, k=k, qq=qq, z0=z0, z2=z2: v.scalar_tensor_tensor(
                    out=q2t[k][:, qq * 1024:(qq + 1) * 1024], in0=z0[:], in1=z2[:],
                    scalar=1000.0, op0=OP.subtract, op1=OP.add), wait=wq2, inc=c_v)

        emit_hload(0)
        emit_hload(1)
        emit_qmath(0)

        def emit_base(i):
            k = i % 2
            wre = list(tATST)
            if i >= 2:
                wre.append((s_out[k], st_out[i - 2]))
            al = P.dma("sync", out=are[k][:], in_=attn_scr[i * 128:(i + 1) * 128, :],
                       wait=wre, inc=s_atl[k])
            xl = P.dma("sync", out=xre[k][:], in_=x_h[i * 128:(i + 1) * 128, :],
                       wait=[(s_out[k], st_out[i - 2])] if i >= 2 else (), inc=s_xr[k])
            base_t[i] = P.op("vector", lambda v, k=k: v.tensor_tensor(
                out=xre[k][:], in0=xre[k][:], in1=are[k][:], op=OP.add),
                wait=[(s_atl[k], al), (s_xr[k], xl)], inc=c_v)

        def emit_tail(i):
            k = i % 2
            kq = i % 3
            emit_base(i)
            for grp in range(4):
                par = grp % 2
                wtr = [(c_v, q2done[(i, grp)])]
                if tail_ev[par]:
                    wtr.append((c_s, tail_ev[par]))
                trt = transpose8(psTd[par],
                                 lambda c, kq=kq, grp=grp: q2t[kq][:, grp * 1024 + c * 128: grp * 1024 + (c + 1) * 128],
                                 8, wtr)
                q2T[(i, grp)] = trt
                wevq = [(c_t, trt)]
                if i >= 2:
                    wevq.append((c_t, dn_last[i - 2]))
                tail_ev[par] = P.op("scalar", lambda s, k=k, grp=grp, par=par: s.copy(
                    out=q2fm[k][:, grp * 1024:(grp + 1) * 1024], in_=psTd[par][:]),
                    wait=wevq, inc=c_s)
                q2ev[(i, grp)] = tail_ev[par]
            for sh in range(2):
                w0 = [(c_s, q2ev[(i, 3)]), (s_wdl, tWDL)]
                if i >= 3:
                    w0.append((c_v, dnev2[i - 3]))
                for c in range(NF):
                    P.op("tensor", lambda t, i=i, sh=sh, c=c, k=k: t.matmul(
                        out=psDn[i % 3][:, sh * 512:(sh + 1) * 512],
                        lhsT=q2fm[k][:, c * 128:(c + 1) * 128],
                        rhs=wdT[:, c * D + sh * 512:c * D + (sh + 1) * 512],
                        start=(c == 0), stop=(c == NF - 1)),
                        wait=w0 if c == 0 else (), inc=c_t if c == NF - 1 else None)
                dn_last[i] = c_t.n
            dnev2[i] = P.op("vector", lambda v, i=i, k=k: v.scalar_tensor_tensor(
                out=xre[k][:, :], in0=psDn[i % 3][:, :], scalar=ad_sb[:, i:i + 1],
                in1=xre[k][:, :], op0=OP.mult, op1=OP.add),
                wait=[(c_t, dn_last[i]), (c_v, base_t[i])], inc=c_v)
            st_out[i] = P.dma("gpsimd", out=out_h[i * 128:(i + 1) * 128, :], in_=xre[k][:],
                              wait=[(c_v, dnev2[i])], inc=s_out[k])

        for i in range(nt):
            if i + 2 < nt:
                emit_hload(i + 2)
            if i + 1 < nt:
                emit_qmath(i + 1)
            emit_tail(i)

        P.op("gpsimd", lambda e: e.nop(),
             wait=[(s_out[0], s_out[0].n), (s_out[1], s_out[1].n)], drain=False)

        P.flush()
        pc2.close()
        pouter.close()

    return nc


def kernel(L0_attn_x, L0_attn_w_qkv, L0_attn_w_out, L0_ffn_w_gate, L0_ffn_w_up,
           L0_ffn_w_down):
    x = np.ascontiguousarray(np.asarray(L0_attn_x, dtype=np.float32))
    tabs = host_tables(16)
    nc = build_nc(nt=16)
    in_maps = []
    wq = np.ascontiguousarray(np.asarray(L0_attn_w_qkv, dtype=np.float32))
    wo = np.ascontiguousarray(np.asarray(L0_attn_w_out, dtype=np.float32))
    wg = np.ascontiguousarray(np.asarray(L0_ffn_w_gate, dtype=np.float32))
    wu = np.ascontiguousarray(np.asarray(L0_ffn_w_up, dtype=np.float32))
    wd = np.ascontiguousarray(np.asarray(L0_ffn_w_down, dtype=np.float32))
    for b in range(8):
        m = dict(x=np.ascontiguousarray(x[b]), w_qkv=wq, w_out=wo, w_g=wg, w_u=wu, w_d=wd)
        m.update(tabs)
        in_maps.append(m)
    res = run_bass_kernel_spmd(nc, in_maps, core_ids=list(range(8)))
    out = np.stack([np.asarray(r["out"]) for r in res.results], axis=0)
    if out.dtype != np.float64:
        out = out.astype(np.float32)
    return out


# version 26 (final)
# speedup vs baseline: 1.0085x; 1.0010x over previous
import numpy as np
import ml_dtypes
from contextlib import ExitStack

import concourse.bass as bass
import concourse.mybir as mybir
from concourse.bass_utils import run_bass_kernel_spmd

F32 = mybir.dt.float32
BF16 = mybir.dt.bfloat16
AX = mybir.AxisListType.X
AF = mybir.ActivationFunctionType
OP = mybir.AluOpType

D, H, HD, F = 1024, 16, 64, 4096
KD = D // 128
NF = F // 128
M1 = float(3 * 2**22)
M2 = float(3 * 2**51)
EPS_LN = 1e-5
EPS_Q = 1e-5
EPS_DEN = 1e-6
RINV127 = float(np.float32(1.0) / np.float32(127.0))
RINV127_LO = float(np.float32(1.0 / 127.0 - np.float64(np.float32(1.0) / np.float32(127.0))))
INV1024 = 1.0 / 1024.0
INV2P22 = 1.0 / float(2**22)


class Sem:
    def __init__(self, h):
        self.h = h
        self.n = 0

    def tick(self, k=1):
        self.n += k
        return self.n


class Prog:
    ENGINES = ("sync", "vector", "scalar", "tensor", "gpsimd")

    def __init__(self, block):
        self.block = block
        self.items = {e: [] for e in self.ENGINES}
        self.all_sems = []

    def phase_barrier(self, waits=None):
        bar = tuple(waits) if waits is not None else tuple(
            (s, s.n) for s in self.all_sems if s.n > 0)
        for eng in self.ENGINES:
            self.items[eng].append((lambda e: e.nop(), bar, None, 0))

    DRAIN_ENGINES = ("vector", "scalar", "gpsimd")

    def op(self, eng, fn, wait=(), inc=None, incn=1, drain=True):
        tick = inc.tick(incn) if inc is not None else None
        if drain and eng in self.DRAIN_ENGINES:
            def fn2(e, fn=fn):
                r = fn(e)
                e.drain()
                return r
            self.items[eng].append((fn2, tuple(wait), inc, incn))
        else:
            self.items[eng].append((fn, tuple(wait), inc, incn))
        return tick

    def dma(self, eng, out, in_, wait=(), inc=None, transpose=False):
        tick = inc.tick(16) if inc is not None else None

        def fn(e, out=out, in_=in_, tr=transpose):
            return e.dma_start(out=out, in_=in_, transpose=tr)

        self.items[eng].append((fn, tuple(wait), inc, 16))
        return tick

    def flush(self):
        for eng in self.ENGINES:
            items = self.items[eng]
            if not items:
                continue
            self.items[eng] = []
            section = getattr(self.block, eng)

            def emit(e, items=items):
                for fn, waits, inc, incn in items:
                    for (s, v) in waits:
                        e.wait_ge(s.h, v)
                    r = fn(e)
                    if inc is not None:
                        r.then_inc(inc.h, incn)

            section(emit)


def host_tables(nt=16):
    T = nt * 128
    j = np.arange(0, HD, 2, dtype=np.float32) / np.float32(HD)
    inv_freq = np.float32(1.0) / (10000.0 ** j.astype(np.float64)).astype(np.float32)
    t = np.arange(T, dtype=np.float32)
    freqs = (t[:, None] * inv_freq[None, :]).astype(np.float32)
    cos = np.cos(freqs.astype(np.float64)).astype(np.float32)
    sin = np.sin(freqs.astype(np.float64)).astype(np.float32)
    cos_t = cos.reshape(nt, 128, 32).transpose(1, 0, 2)
    sin_t = sin.reshape(nt, 128, 32).transpose(1, 0, 2)
    cos64 = np.concatenate([cos_t, cos_t], axis=2).reshape(128, nt * 64)
    sinpat = np.concatenate([-sin_t, sin_t], axis=2).reshape(128, nt * 64)
    sel2 = np.zeros((2, 128), dtype=np.float32)
    sel2[0, 0:64] = 1.0
    sel2[1, 64:128] = 1.0
    constc = np.zeros((128, 8), dtype=np.float32)
    constc[:, 0] = M1
    constc[:, 1] = -M1
    constc[:, 2] = 0.5
    constc[:, 3] = -0.5
    constc[:, 4] = 1000.0
    constc[:, 5] = M2
    constc[:, 6] = -M2
    return dict(constc=constc,
        cos64=np.ascontiguousarray(cos64), sinpat=np.ascontiguousarray(sinpat),
        ident=np.eye(128, dtype=ml_dtypes.bfloat16),
        ident_f=np.eye(128, dtype=np.float32),
        onesrow_f=np.ones((1, 128), dtype=np.float32),
        ones_col_b=np.ones((128, 1), dtype=ml_dtypes.bfloat16),
        sel2=sel2,
    )


def build_nc(nt=16, tap=False):
    T = nt * 128
    NS = max(T // 512, 1)
    SW = min(T, 512)
    nc = bass.Bass("TRN2", target_bir_lowering=False)

    x_h = nc.dram_tensor("x", [T, D], F32, kind="ExternalInput").ap()
    wqkv_h = nc.dram_tensor("w_qkv", [3 * D, D], F32, kind="ExternalInput").ap()
    wout_h = nc.dram_tensor("w_out", [D, D], F32, kind="ExternalInput").ap()
    wg_h = nc.dram_tensor("w_g", [F, D], F32, kind="ExternalInput").ap()
    wu_h = nc.dram_tensor("w_u", [F, D], F32, kind="ExternalInput").ap()
    wd_h = nc.dram_tensor("w_d", [D, F], F32, kind="ExternalInput").ap()
    tspecs = (("cos64", [128, nt * 64], F32), ("sinpat", [128, nt * 64], F32),
              ("ident", [128, 128], BF16), ("ident_f", [128, 128], F32),
              ("onesrow_f", [1, 128], F32),
              ("ones_col_b", [128, 1], BF16), ("sel2", [2, 128], F32),
              ("constc", [128, 8], F32))
    tabs = {nm: nc.dram_tensor(nm, shp, dt, kind="ExternalInput").ap()
            for nm, shp, dt in tspecs}

    out_h = nc.dram_tensor("out", [T, D], F32, kind="ExternalOutput").ap()
    skind = "ExternalOutput" if tap else "Internal"
    q1_scr = nc.dram_tensor("q1_scr", [T, D], BF16, kind=skind).ap()
    attn_scr = nc.dram_tensor("attn_scr", [T, D], BF16, kind=skind).ap()
    h_scr = nc.dram_tensor("h_scr", [T, F], F32, kind=skind).ap()
    wdt_scr = nc.dram_tensor("wdt_scr", [D, F], BF16, kind="Internal").ap()

    es = ExitStack()
    with es:
        block = es.enter_context(nc.Block())
        P = Prog(block)
        def mksem(name):
            s = Sem(es.enter_context(nc.semaphore(name)))
            P.all_sems.append(s)
            return s

        def mksems(name, n):
            return [mksem(f"{name}{k}") for k in range(n)]

        s_tab, s_q1l = mksem("s_tab"), mksem("s_q1l")
        s_x = mksems("s_x", 3)
        s_wsl = mksems("s_wsl", 4)
        s_msl = mksems("s_msl", 4)
        s_wo = mksems("s_wo", 2)
        s_q1s = mksems("s_q1s", 3)
        s_gu = mksems("s_gu", 4)
        s_wd = mksems("s_wd", 2)
        s_hw = mksems("s_hw", 3)
        s_hl = mksems("s_hl", 8)
        s_at = mksems("s_at", 2)
        s_atl = mksems("s_atl", 2)
        s_xr = mksems("s_xr", 2)
        s_out = mksems("s_out", 2)
        s_wts = mksems("s_wts", 2)
        s_wdl = mksems("s_wdl", 1)[0]
        c_v, c_s, c_t, c_g = mksem("c_v"), mksem("c_s"), mksem("c_t"), mksem("c_g")

        per = ExitStack()
        es.enter_context(per)
        tb = {nm: per.enter_context(nc.sbuf_tensor("t_" + nm, shp, dt))
              for nm, shp, dt in tspecs}
        invsc127 = per.enter_context(nc.sbuf_tensor("invsc127", [128, nt], F32))
        scx_sb = per.enter_context(nc.sbuf_tensor("scx_sb", [128, nt], F32))
        hmax_sb = per.enter_context(nc.sbuf_tensor("hmax_sb", [128, nt], F32))
        wbias_n = per.enter_context(nc.sbuf_tensor("wbias_n", [128, 3], F32))
        wbias_p = per.enter_context(nc.sbuf_tensor("wbias_p", [128, 3], F32))
        wmean_b = per.enter_context(nc.sbuf_tensor("wmean_b", [128, 3], F32))
        ag_sb = per.enter_context(nc.sbuf_tensor("ag_sb", [128, nt], F32))
        au_sb = per.enter_context(nc.sbuf_tensor("au_sb", [128, nt], F32))
        ad_sb = per.enter_context(nc.sbuf_tensor("ad_sb", [128, nt], F32))
        sch_sb = per.enter_context(nc.sbuf_tensor("sch_sb", [128, nt], F32))
        stat1 = per.enter_context(nc.sbuf_tensor("stat1", [1, 16], F32))

        for nm in tb:
            P.dma("sync", out=tb[nm][:], in_=tabs[nm][:, :], inc=s_tab)
        tTAB = s_tab.n
        ident = tb["ident"]

        def transpose8(psbuf, src_ap_fn, nblk, wfirst):
            for c in range(nblk):
                P.op("tensor", lambda t, c=c: t.transpose(
                    out=psbuf[:, c * 128:(c + 1) * 128], in_=src_ap_fn(c),
                    identity=ident[:]),
                    wait=wfirst if c == 0 else (), inc=c_t if c == nblk - 1 else None)
            return c_t.n

        right = ExitStack()
        es.enter_context(right)
        xn_fm = right.enter_context(nc.sbuf_tensor("xn_fm", [128, KD * T], BF16, side="right"))
        wqkvT_q = right.enter_context(nc.sbuf_tensor("wqkvT_q", [128, KD * D], BF16, side="right"))
        rkv = ExitStack()
        es.enter_context(rkv)
        wqkvT_kv = rkv.enter_context(nc.sbuf_tensor("wqkvT_kv", [128, KD * 2 * D], BF16, side="right"))

        pm = ExitStack()
        m_wsl = [pm.enter_context(nc.sbuf_tensor(f"m_wsl{k}", [128, 512], F32)) for k in range(4)]
        mcols = [pm.enter_context(nc.sbuf_tensor(f"mcols{k}", [128, 64], F32)) for k in range(3)]
        rscol = pm.enter_context(nc.sbuf_tensor("rscol", [128, 3], F32))
        mrow = pm.enter_context(nc.sbuf_tensor("mrow", [1, 384], F32))

        MG = float(1.5 * 2**24 * 2**-6)
        mci = {"g": 0, "u": 1, "d": 2}
        mean_ready = {}
        _mp = dict(prev_ld=[0, 0, 0, 0], tick=0, done=0)
        _mp_jobs = []
        for mi, (nm, hsrc) in enumerate((("g", wg_h), ("u", wu_h), ("d", wd_h))):
            for j in range(64):
                _mp_jobs.append((mci[nm], j, hsrc, nm))

        def emit_mean_jobs(njobs):
            while njobs > 0 and _mp["done"] < len(_mp_jobs):
                ci, j, hsrc, nm = _mp_jobs[_mp["done"]]
                bs = _mp["done"] % 4
                if nm == "d":
                    r0, c0 = (j // 8) * 128, (j % 8) * 512
                    src_ap = wd_h[r0:r0 + 128, c0:c0 + 512]
                else:
                    src_ap = hsrc[(j // 2) * 128:(j // 2 + 1) * 128,
                                  (j % 2) * 512:(j % 2 + 1) * 512]
                wld = [(c_s, _mp["prev_ld"][bs])] if _mp["prev_ld"][bs] else []
                tl = P.dma("gpsimd", out=m_wsl[bs][:], in_=src_ap, wait=wld, inc=s_msl[bs])
                ab = P.op("scalar", lambda s, bs=bs, ci=ci, j=j: s.activation(
                    out=m_wsl[bs][:], in_=m_wsl[bs][:], func=AF.Abs,
                    accum_out=mcols[ci][:, j:j + 1]), wait=[(s_msl[bs], tl)], inc=c_s)
                _mp["prev_ld"][bs] = ab
                _mp["done"] += 1
                njobs -= 1
                if _mp["done"] % 64 == 0:
                    ci_done = _mp_jobs[_mp["done"] - 1][0]
                    P.op("vector", lambda v, ci=ci_done: v.reduce_sum(
                        out=rscol[:, ci:ci + 1], in_=mcols[ci][:, :], axis=AX),
                        wait=[(c_s, ab)], inc=c_v)
                    _mp["tick"] = c_v.n


        pa = ExitStack()
        xtile = [pa.enter_context(nc.sbuf_tensor(f"xt{k}", [128, D], F32)) for k in range(3)]
        xnf = [pa.enter_context(nc.sbuf_tensor(f"xnf{k}", [128, D], F32)) for k in range(3)]
        xnb = [pa.enter_context(nc.sbuf_tensor(f"xnb{k}", [128, D], BF16)) for k in range(3)]
        q1t = [pa.enter_context(nc.sbuf_tensor(f"q1t{k}", [128, D], BF16)) for k in range(3)]
        y0 = [pa.enter_context(nc.sbuf_tensor(f"y0{k}", [128, D], F32)) for k in range(3)]
        y1 = [pa.enter_context(nc.sbuf_tensor(f"y1{k}", [128, D], F32)) for k in range(3)]
        y2 = [pa.enter_context(nc.sbuf_tensor(f"y2{k}", [128, D], F32)) for k in range(3)]
        colA = [pa.enter_context(nc.sbuf_tensor(f"colA{k}", [128, 8], F32)) for k in range(3)]
        wsl = [pa.enter_context(nc.sbuf_tensor(f"wsl{k}", [128, D], F32)) for k in range(4)]
        wslb = [pa.enter_context(nc.sbuf_tensor(f"wslb{k}", [128, D], BF16)) for k in range(2)]

        psTw = [pa.enter_context(nc.psum_tensor(f"psTw{k}", [128, 1024], BF16)) for k in range(2)]
        psTx = [pa.enter_context(nc.psum_tensor(f"psTx{k}", [128, 1024], BF16)) for k in range(3)]


        wq_cast = [0] * 24
        wq_tr = [0] * 24
        wq_ev = [0] * 24
        for j in range(24):
            w = [(c_s, wq_cast[j - 4])] if j >= 4 else ()
            tl = P.dma("sync", out=wsl[j % 4][:], in_=wqkv_h[j * 128:(j + 1) * 128, :],
                       wait=w, inc=s_wsl[j % 4])
            w2 = [(s_wsl[j % 4], tl)]
            if j >= 2:
                w2.append((c_t, wq_tr[j - 2]))
            wq_cast[j] = P.op("scalar",
                              lambda s, j=j: s.copy(out=wslb[j % 2][:], in_=wsl[j % 4][:]),
                              wait=w2, inc=c_s)
            w3 = [(c_s, wq_cast[j]), (s_tab, tTAB)]
            if j >= 2:
                w3.append((c_v, wq_ev[j - 2]))
            wq_tr[j] = transpose8(psTw[j % 2],
                                  lambda c, j=j: wslb[j % 2][:, c * 128:(c + 1) * 128], 8, w3)
            if j < 8:
                dst, W, off = wqkvT_q, D, j * 128
            else:
                dst, W, off = wqkvT_kv, 2 * D, (j - 8) * 128
            wq_ev[j] = P.op(
                "vector",
                lambda v, j=j, dst=dst, W=W, off=off: v.tensor_copy(
                    out=dst[:].rearrange("p (c r) -> p c r", c=8, r=W)[:, :, off:off + 128],
                    in_=psTw[j % 2][:]),
                wait=[(c_t, wq_tr[j])], inc=c_v)
        tWQ_Q = wq_ev[7]
        tWQ_KV = wq_ev[23]

        xn_ready = [0] * nt
        q1_done = [0] * nt
        ev_x = [0] * nt
        xnT = [0] * nt
        yq_t = [0] * nt
        r2_t = [0] * nt
        st_q1 = [0] * nt
        xb_t = [0] * nt
        for i in range(nt):
            k = i % 3
            ca = colA[k]
            w = [(c_s, xn_ready[i - 3])] if i >= 3 else ()
            tl = P.dma("sync", out=xtile[k][:], in_=x_h[i * 128:(i + 1) * 128, :],
                       wait=w, inc=s_x[k])
            wsq = [(s_x[k], tl)]
            if i >= 3:
                wsq.append((c_v, r2_t[i - 3]))
            s2 = P.op("scalar",
                      lambda s, k=k, ca=ca: s.activation(out=y2[k][:], in_=xtile[k][:],
                                                         func=AF.Square, accum_out=ca[:, 1:2]),
                      wait=wsq, inc=c_s)
            P.op("vector", lambda v, k=k, ca=ca: v.reduce_sum(out=ca[:, 0:1], in_=xtile[k][:], axis=AX),
                 wait=[(c_s, s2)])
            P.op("vector", lambda v, ca=ca: v.tensor_scalar(out=ca[:, 2:3], in0=ca[:, 0:1],
                                                            scalar1=INV1024, scalar2=None, op0=OP.mult))
            P.op("vector", lambda v, ca=ca: v.tensor_scalar(out=ca[:, 3:4], in0=ca[:, 1:2],
                                                            scalar1=INV1024, scalar2=None, op0=OP.mult))
            P.op("vector", lambda v, ca=ca: v.scalar_tensor_tensor(
                out=ca[:, 4:5], in0=ca[:, 2:3], scalar=ca[:, 2:3], in1=ca[:, 3:4],
                op0=OP.mult, op1=OP.subtract))
            P.op("vector", lambda v, ca=ca: v.tensor_scalar(
                out=ca[:, 4:5], in0=ca[:, 4:5], scalar1=-1.0, scalar2=EPS_LN,
                op0=OP.mult, op1=OP.add))
            rc = P.op("vector", lambda v, ca=ca: v.reciprocal(out=ca[:, 5:6], in_=ca[:, 4:5]),
                      inc=c_v)
            sq = P.op("scalar", lambda s, ca=ca: s.activation(out=ca[:, 6:7], in_=ca[:, 5:6],
                                                              func=AF.Sqrt),
                      wait=[(c_v, rc)], inc=c_s)
            nb = P.op("vector", lambda v, ca=ca: v.scalar_tensor_tensor(
                out=ca[:, 7:8], in0=ca[:, 2:3], scalar=-1.0, in1=ca[:, 6:7],
                op0=OP.mult, op1=OP.mult), wait=[(c_s, sq)], inc=c_v)
            wxn = [(c_v, nb)]
            if i >= 3:
                wxn.append((c_v, yq_t[i - 3]))
            xn_ready[i] = P.op("scalar", lambda s, k=k, ca=ca: s.activation(
                out=xnf[k][:], in_=xtile[k][:], func=AF.Identity,
                bias=ca[:, 7:8], scale=ca[:, 6:7]), wait=wxn, inc=c_s)
            P.op("vector", lambda v, k=k, ca=ca: v.reduce_max(
                out=ca[:, 0:1], in_=xnf[k][:], axis=AX, apply_absolute_value=True),
                 wait=[(c_s, xn_ready[i])])
            P.op("vector", lambda v, ca=ca: v.tensor_scalar(
                out=ca[:, 1:2], in0=ca[:, 0:1], scalar1=EPS_Q, scalar2=None, op0=OP.max))
            P.op("vector", lambda v, ca=ca: v.reciprocal(out=ca[:, 2:3], in_=ca[:, 1:2]))
            P.op("vector", lambda v, ca=ca, i=i: v.tensor_scalar(
                out=scx_sb[:, i:i + 1], in0=ca[:, 2:3], scalar1=127.0, scalar2=None, op0=OP.mult))
            P.op("vector", lambda v, ca=ca, i=i: v.tensor_scalar(
                out=invsc127[:, i:i + 1], in0=ca[:, 1:2], scalar1=RINV127, scalar2=None, op0=OP.mult))
            P.op("vector", lambda v, ca=ca, i=i: v.scalar_tensor_tensor(
                out=invsc127[:, i:i + 1], in0=ca[:, 1:2], scalar=RINV127_LO,
                in1=invsc127[:, i:i + 1], op0=OP.mult, op1=OP.add))
            wxb = [(c_s, xn_ready[i])]
            if i >= 3:
                wxb.append((c_t, xnT[i - 3]))
            xb_t[i] = P.op("scalar", lambda s, k=k: s.copy(out=xnb[k][:], in_=xnf[k][:]),
                           wait=wxb, inc=c_s)
            wy = [(c_v, q1_done[i - 3])] if i >= 3 else ()
            yq_t[i] = P.op("vector", lambda v, k=k, i=i: v.tensor_scalar(
                out=y0[k][:], in0=xnf[k][:], scalar1=scx_sb[:, i:i + 1], scalar2=1000.0,
                op0=OP.mult, op1=OP.add), wait=wy, inc=c_v)
            P.op("vector", lambda v, k=k: v.tensor_scalar(
                out=y1[k][:], in0=y0[k][:], scalar1=M1, scalar2=M1, op0=OP.add, op1=OP.subtract))
            r2_t[i] = P.op("vector", lambda v, k=k: v.tensor_scalar(
                out=y2[k][:], in0=y0[k][:], scalar1=M2, scalar2=M2, op0=OP.add, op1=OP.subtract),
                inc=c_v)
            P.op("vector", lambda v, k=k: v.scalar_tensor_tensor(
                out=y0[k][:], in0=y2[k][:], scalar=1.0, in1=y1[k][:], op0=OP.is_lt, op1=OP.mult))
            wq1 = [(s_q1s[k], st_q1[i - 3])] if i >= 3 else ()
            q1_done[i] = P.op("vector", lambda v, k=k: v.scalar_tensor_tensor(
                out=q1t[k][:], in0=y0[k][:], scalar=1000.0, in1=y2[k][:],
                op0=OP.subtract, op1=OP.add), wait=wq1, inc=c_v)
            st_q1[i] = P.dma("gpsimd", out=q1_scr[i * 128:(i + 1) * 128, :], in_=q1t[k][:],
                             wait=[(c_v, q1_done[i])], inc=s_q1s[k])
            wtr = [(c_s, xb_t[i]), (s_tab, tTAB)]
            if i >= 3:
                wtr.append((c_v, ev_x[i - 3]))
            xnT[i] = transpose8(psTx[k], lambda c, k=k: xnb[k][:, c * 128:(c + 1) * 128], 8, wtr)
            ev_x[i] = P.op("vector", lambda v, k=k, i=i: v.tensor_copy(
                out=xn_fm[:].rearrange("p (c r) -> p c r", c=8, r=T)[:, :, i * 128:(i + 1) * 128],
                in_=psTx[k][:]), wait=[(c_t, xnT[i])], inc=c_v)

        P.flush()
        pa.close()

        P.phase_barrier()

        pbL = ExitStack()
        kf_sb = pbL.enter_context(nc.sbuf_tensor("kf_sb", [128, nt * D], BF16))
        v_sb = pbL.enter_context(nc.sbuf_tensor("v_sb", [128, nt * D], BF16))
        qf_sb = pbL.enter_context(nc.sbuf_tensor("qf_sb", [128, KD * T], BF16))

        pbM = ExitStack()
        rs1 = pbM.enter_context(nc.sbuf_tensor("rs1", [128, D], F32))
        rs2 = pbM.enter_context(nc.sbuf_tensor("rs2", [128, D], F32))
        qs1, qs2 = rs1, rs2
        qfb0 = pbM.enter_context(nc.sbuf_tensor("qfb0", [128, D], BF16))
        qfb = [qfb0, qfb0]
        psK = pbM.enter_context(nc.psum_tensor("psK", [128, 1024], F32))
        psV = pbM.enter_context(nc.psum_tensor("psV", [128, 1024], F32))
        psQ = pbM.enter_context(nc.psum_tensor("psQ", [128, 1024], F32))
        psTq = pbM.enter_context(nc.psum_tensor("psTq", [128, 1024], BF16))

        def swap_ap(buf):
            return buf[:].rearrange("p (h two j) -> p h two j", two=2, j=32)[:, :, ::-1, :]

        def full_ap(buf):
            return buf[:].rearrange("p (h two j) -> p h two j", two=2, j=32)

        def rtab64(tbl, i):
            return tbl[:, i * 64:(i + 1) * 64].rearrange("p (o two j) -> p o two j", o=1, two=2) \
                      .broadcast_to([128, H, 2, 32])

        kft_t = [0] * nt
        qf_t = [0] * nt
        vev_t = [0] * nt
        krd_t = [0] * nt
        qrd_t = [0] * nt
        qT_t = [0] * nt
        qev_t = [0] * nt

        def emit_qtr(i):
            wtr = [(c_v, qf_t[i])]
            if i >= 1:
                wtr.append((c_s, qev_t[i - 1]))
            qT_t[i] = transpose8(psTq, lambda c, i=i: qfb[i % 2][:, c * 128:(c + 1) * 128],
                                 8, wtr)
            qev_t[i] = P.op("scalar", lambda s, i=i: s.copy(
                out=qf_sb[:].rearrange("p (c r) -> p c r", c=8, r=T)[:, :, i * 128:(i + 1) * 128],
                in_=psTq[:]), wait=[(c_t, qT_t[i])], inc=c_s)

        def mm_group(ps, rhs_buf, rhs_stride, rhs_off, wfirst, i):
            for sh in range(2):
                for c in range(KD):
                    P.op("tensor", lambda t, i=i, sh=sh, c=c, ps=ps, rb=rhs_buf,
                         rs=rhs_stride, ro=rhs_off: t.matmul(
                        out=ps[:, sh * 512:(sh + 1) * 512],
                        lhsT=xn_fm[:, c * T + i * 128: c * T + (i + 1) * 128],
                        rhs=rb[:, c * rs + ro + sh * 512: c * rs + ro + (sh + 1) * 512],
                        start=(c == 0), stop=(c == KD - 1)),
                        wait=wfirst if (sh == 0 and c == 0) else (),
                        inc=c_t if (sh == 1 and c == KD - 1) else None)
            return c_t.n

        def rope_elu(ps, s1, s2, i, dst_fn, qscale, wfirst):
            P.op("vector", lambda v, ps=ps, i=i, s1=s1: v.tensor_tensor(
                out=full_ap(s1), in0=full_ap(ps), in1=rtab64(tb["cos64"], i), op=OP.mult),
                wait=wfirst)
            rd = P.op("vector", lambda v, ps=ps, i=i, s2=s2: v.tensor_tensor(
                out=full_ap(s2), in0=swap_ap(ps), in1=rtab64(tb["sinpat"], i), op=OP.mult),
                inc=c_v)
            P.op("vector", lambda v, s1=s1, s2=s2: v.tensor_tensor(
                out=s1[:], in0=s1[:], in1=s2[:], op=OP.add))
            mx = P.op("vector", lambda v, s1=s1, s2=s2: v.tensor_scalar(
                out=s2[:], in0=s1[:], scalar1=0.0, scalar2=None, op0=OP.min), inc=c_v)
            if qscale:
                eq = P.op("scalar", lambda s, s2=s2: s.activation(
                    out=s2[:], in_=s2[:], func=AF.Exp, scale=qscale),
                    wait=[(c_v, mx)], inc=c_s)
                P.op("vector", lambda v, s1=s1: v.tensor_scalar(
                    out=s1[:], in0=s1[:], scalar1=0.0, scalar2=qscale, op0=OP.max, op1=OP.mult))
            else:
                eq = P.op("scalar", lambda s, s2=s2: s.activation(
                    out=s2[:], in_=s2[:], func=AF.Exp), wait=[(c_v, mx)], inc=c_s)
                P.op("vector", lambda v, s1=s1: v.tensor_scalar(
                    out=s1[:], in0=s1[:], scalar1=0.0, scalar2=None, op0=OP.max))
            ft = P.op("vector", lambda v, s1=s1, s2=s2: v.tensor_tensor(
                out=dst_fn(), in0=s1[:], in1=s2[:], op=OP.add),
                wait=[(c_s, eq)], inc=c_v)
            return rd, ft

        for i in range(nt):
            wk = [(c_v, ev_x[i]), (c_v, tWQ_KV)]
            if i >= 1:
                wk.append((c_v, krd_t[i - 1]))
            kMMi = mm_group(psK, wqkvT_kv, 2 * D, 0, wk, i)
            wv = [(c_s, vev_t[i - 1])] if i >= 1 else ()
            vMMi = mm_group(psV, wqkvT_kv, 2 * D, 1024, wv, i)
            wq = [(c_v, tWQ_Q)]
            if i >= 1:
                wq.append((c_v, qrd_t[i - 1]))
            qMMi = mm_group(psQ, wqkvT_q, D, 0, wq, i)
            if i >= 1:
                emit_qtr(i - 1)
            wkr = [(c_t, kMMi)]
            if i >= 1:
                wkr.append((c_v, kft_t[i - 1]))
            krd_t[i], kft_t[i] = rope_elu(
                psK, rs1, rs2, i, lambda i=i: kf_sb[:, i * D:(i + 1) * D], None, wkr)
            vev_t[i] = P.op("scalar", lambda s, i=i: s.copy(
                out=v_sb[:, i * D:(i + 1) * D], in_=psV[:, :]),
                wait=[(c_t, vMMi)], inc=c_s)
            wqr = [(c_t, qMMi)]
            if i >= 1:
                wqr.append((c_v, qf_t[i - 1]))
                wqr.append((c_t, qT_t[i - 1]))
            qrd_t[i], qf_t[i] = rope_elu(
                psQ, qs1, qs2, i, lambda i=i: qfb[i % 2][:], 0.125, wqr)

            if i == nt - 1:
                emit_qtr(i)
            emit_mean_jobs(192 // nt + 1)
        emit_mean_jobs(len(_mp_jobs))
        tQF = qev_t[nt - 1]

        P.flush()
        pbM.close()
        P.phase_barrier(waits=[(c_v, kft_t[nt - 1]), (c_s, qev_t[nt - 1]),
                               (c_t, qT_t[nt - 1])])
        rkv.close()
        right.close()


        pt = ExitStack()
        psB = pt.enter_context(nc.psum_tensor("psB", [128, 512], F32))
        for nm in ("g", "u", "d"):
            ci = mci[nm]
            tp = P.op("tensor", lambda t, ci=ci: t.transpose(
                out=psB[0:1, 128:256], in_=rscol[:, ci:ci + 1], identity=tb["ident_f"][:]),
                wait=[(c_v, _mp["tick"])], inc=c_t)
            ev = P.op("vector", lambda v: v.tensor_copy(out=mrow[0:1, 0:128], in_=psB[0:1, 128:256]),
                      wait=[(c_t, tp)], inc=c_v)
            P.op("vector", lambda v: v.tensor_scalar(
                out=mrow[0:1, 128:256], in0=mrow[0:1, 0:128], scalar1=MG, scalar2=MG,
                op0=OP.add, op1=OP.subtract))
            P.op("vector", lambda v: v.tensor_tensor(
                out=mrow[0:1, 256:384], in0=mrow[0:1, 0:128], in1=mrow[0:1, 128:256],
                op=OP.subtract))
            P.op("vector", lambda v: v.reduce_sum(out=stat1[0:1, 0:1], in_=mrow[0:1, 128:256], axis=AX))
            P.op("vector", lambda v: v.reduce_sum(out=stat1[0:1, 1:2], in_=mrow[0:1, 256:384], axis=AX))
            P.op("vector", lambda v: v.tensor_tensor(
                out=stat1[0:1, 2:3], in0=stat1[0:1, 0:1], in1=stat1[0:1, 1:2], op=OP.add))
            P.op("vector", lambda v: v.tensor_scalar(out=stat1[0:1, 3:4], in0=stat1[0:1, 2:3],
                                                     scalar1=INV2P22, scalar2=EPS_Q,
                                                     op0=OP.mult, op1=OP.max))
            P.op("vector", lambda v: v.tensor_scalar(out=stat1[0:1, 4:5], in0=stat1[0:1, 3:4],
                                                     scalar1=-0.5, scalar2=None, op0=OP.mult))
            bc0 = P.op("vector", lambda v: v.tensor_scalar(out=stat1[0:1, 5:6], in0=stat1[0:1, 3:4],
                                                           scalar1=0.5, scalar2=None, op0=OP.mult),
                       inc=c_v)
            last = 0
            for (srcc, dstb) in ((3, wmean_b), (4, wbias_n), (5, wbias_p)):
                bcm = P.op("tensor", lambda t, srcc=srcc: t.matmul(
                    out=psB[:, 0:1], lhsT=tb["onesrow_f"][:], rhs=stat1[0:1, srcc:srcc + 1],
                    start=True, stop=True), wait=[(c_v, bc0), (c_v, last)] if last else [(c_v, bc0)],
                    inc=c_t)
                last = P.op("vector", lambda v, ci=ci, dstb=dstb: v.tensor_copy(
                    out=dstb[:, ci:ci + 1], in_=psB[:, 0:1]), wait=[(c_t, bcm)], inc=c_v)
            mean_ready[nm] = last
        tMEAN = mean_ready["d"]
        tmp = 0
        for nm, dstc in (("g", ag_sb), ("u", au_sb)):
            tmp = P.op("vector", lambda v, dstc=dstc, ci=mci[nm]: v.tensor_scalar(
                out=dstc[:, :], in0=invsc127[:, :], scalar1=wmean_b[:, ci:ci + 1],
                scalar2=0.5, op0=OP.mult, op1=OP.mult),
                wait=[(c_v, tMEAN), (c_v, q1_done[nt - 1])], inc=c_v)
        tAGU = tmp
        P.flush()
        pt.close()

        pb3 = ExitStack()
        kv_sb = pb3.enter_context(nc.sbuf_tensor("kv_sb", [128, 8 * 128], BF16))
        ks_sb = pb3.enter_context(nc.sbuf_tensor("ks_sb", [128, 16], BF16))
        rnb0 = [pb3.enter_context(nc.sbuf_tensor(f"rnb0{k}", [2, 512], F32)) for k in range(2)]
        rnbs = [pb3.enter_context(nc.sbuf_tensor(f"rnbs{k}", [128, 512], F32)) for k in range(2)]
        attn_fm = pb3.enter_context(nc.sbuf_tensor("attn_fm", [128, KD * T], BF16))
        woutT = pb3.enter_context(nc.sbuf_tensor("woutT", [128, KD * D], BF16))
        wosl = [pb3.enter_context(nc.sbuf_tensor(f"wosl{k}", [128, D], F32)) for k in range(2)]
        woslb = [pb3.enter_context(nc.sbuf_tensor(f"woslb{k}", [128, D], BF16)) for k in range(2)]
        ao = [pb3.enter_context(nc.sbuf_tensor(f"ao{k}", [128, D], BF16)) for k in range(2)]
        psKV = pb3.enter_context(nc.psum_tensor("psKV", [128, 512], F32))
        psKS = pb3.enter_context(nc.psum_tensor("psKS", [128, 512], F32))
        wdsg = [pb3.enter_context(nc.sbuf_tensor(f"wdsg{k}", [128, 512], BF16)) for k in range(2)]
        wdtr = [pb3.enter_context(nc.sbuf_tensor(f"wdtr{k}", [128, 512], BF16)) for k in range(2)]
        pbo = ExitStack()
        psTo = [pbo.enter_context(nc.psum_tensor(f"psTo{k}", [128, 1024], BF16)) for k in range(2)]

        _wd = dict(ld=[0, 0, 0, 0], tl=[0, 0, 0, 0], sgn=[0, 0], st=[0, 0],
                   li=0, pi=0)

        def emit_wd_load(n):
            for _ in range(n):
                ix = _wd["li"]
                if ix >= 64 or ix >= _wd["pi"] + 4:
                    return
                _wd["li"] = ix + 1
                d8, cc = ix // 8, ix % 8
                slot = ix % 4
                wld = [(c_v, _wd["ld"][slot])] if _wd["ld"][slot] else [(c_v, tMEAN)]
                _wd["tl"][slot] = P.dma("gpsimd", out=m_wsl[slot][:],
                                        in_=wd_h[d8 * 128:(d8 + 1) * 128,
                                                 cc * 512:(cc + 1) * 512],
                                        wait=wld, inc=s_msl[slot])

        def emit_wd_proc(n):
            for _ in range(n):
                ix = _wd["pi"]
                if ix >= 64:
                    return
                if ix >= _wd["li"]:
                    emit_wd_load(1)
                _wd["pi"] = ix + 1
                d8, cc = ix // 8, ix % 8
                slot = ix % 4
                par = ix % 2
                wsg = [(s_msl[slot], _wd["tl"][slot]), (c_v, tMEAN)]
                if _wd["sgn"][par]:
                    wsg.append((c_v, _wd["sgn"][par]))
                P.op("scalar", lambda s, par=par, slot=slot: s.activation(
                    out=wdsg[par][:], in_=m_wsl[slot][:], func=AF.Sign,
                    bias=wbias_n[:, 2:3]), wait=wsg, inc=c_s)
                P.op("scalar", lambda s, slot=slot: s.activation(
                    out=m_wsl[slot][:], in_=m_wsl[slot][:], func=AF.Sign,
                    bias=wbias_p[:, 2:3]), inc=c_s)
                s2t = c_s.n
                wtt = [(c_s, s2t)]
                if _wd["st"][par]:
                    wtt.append((s_wts[par], _wd["st"][par]))
                tt = P.op("vector", lambda v, par=par, slot=slot: v.tensor_tensor(
                    out=wdtr[par][:], in0=wdsg[par][:], in1=m_wsl[slot][:], op=OP.add),
                    wait=wtt, inc=c_v)
                _wd["sgn"][par] = tt
                _wd["ld"][slot] = tt
                _wd["st"][par] = P.dma("gpsimd",
                                       out=wdt_scr[d8 * 128:(d8 + 1) * 128,
                                                   cc * 512:(cc + 1) * 512],
                                       in_=wdtr[par][:], wait=[(c_v, tt)], inc=s_wts[par])

        emit_wd_load(4)

        nq4 = T // 512 if T >= 512 else 1
        qw = min(T, 512)
        wo_cast = [0] * 8
        wo_tr = [0] * 8
        wo_ev = [0] * 8
        for j in range(8):
            w = [(c_s, wo_cast[j - 2])] if j >= 2 else ()
            tl = P.dma("sync", out=wosl[j % 2][:], in_=wout_h[j * 128:(j + 1) * 128, :],
                       wait=w, inc=s_wo[j % 2])
            wcst = [(s_wo[j % 2], tl)]
            if j >= 2:
                wcst.append((c_t, wo_tr[j - 2]))
            wo_cast[j] = P.op("scalar", lambda s, j=j: s.copy(
                out=woslb[j % 2][:], in_=wosl[j % 2][:]), wait=wcst, inc=c_s)
            wtr = [(c_s, wo_cast[j])]
            if j >= 2:
                wtr.append((c_v, wo_ev[j - 2]))
            wo_tr[j] = transpose8(psTo[j % 2],
                                  lambda c, j=j: woslb[j % 2][:, c * 128:(c + 1) * 128], 8, wtr)
            wo_ev[j] = P.op("vector", lambda v, j=j: v.tensor_copy(
                out=woutT[:].rearrange("p (c r) -> p c r", c=8, r=D)[:, :, j * 128:(j + 1) * 128],
                in_=psTo[j % 2][:]), wait=[(c_t, wo_tr[j])], inc=c_v)
            emit_wd_load(1)
            emit_wd_proc(1)
        tWOUT = wo_ev[7]
        pbo.close()
        pp2 = ExitStack()
        psD = pp2.enter_context(nc.psum_tensor("psD", [2, 512], F32))
        psR = [pp2.enter_context(nc.psum_tensor(f"psR{k}", [128, 512], F32)) for k in range(2)]
        psN = [pp2.enter_context(nc.psum_tensor(f"psN{k}", [128, 512], F32)) for k in range(2)]

        kvev = [0] * KD

        def emit_kv_pair(qc):
            wkv = [(c_v, kft_t[nt - 1]), (c_s, vev_t[nt - 1])]
            if qc >= 1:
                wkv.append((c_v, kvev[qc - 1]))
            for i2 in range(nt):
                P.op("tensor", lambda t, qc=qc, i2=i2: t.matmul(
                    out=psKV[:, 0:128],
                    lhsT=kf_sb[:, i2 * D + qc * 128: i2 * D + (qc + 1) * 128],
                    rhs=v_sb[:, i2 * D + qc * 128: i2 * D + (qc + 1) * 128],
                    start=(i2 == 0), stop=(i2 == nt - 1)),
                    wait=wkv if i2 == 0 else ())
                P.op("tensor", lambda t, qc=qc, i2=i2: t.matmul(
                    out=psKS[:, 0:1],
                    lhsT=kf_sb[:, i2 * D + qc * 128: i2 * D + (qc + 1) * 128],
                    rhs=tb["ones_col_b"][:],
                    start=(i2 == 0), stop=(i2 == nt - 1)),
                    inc=c_t if i2 == nt - 1 else None)
            kvmm = c_t.n
            P.op("vector", lambda v, qc=qc: v.tensor_copy(
                out=kv_sb[:, qc * 128:(qc + 1) * 128], in_=psKV[:, 0:128]),
                wait=[(c_t, kvmm)])
            P.op("vector", lambda v, qc=qc: v.memset(ks_sb[:, 2 * qc:2 * qc + 2], 0.0))
            P.op("vector", lambda v, qc=qc: v.tensor_copy(
                out=ks_sb[0:64, 2 * qc:2 * qc + 1], in_=psKS[0:64, 0:1]))
            kvev[qc] = P.op("vector", lambda v, qc=qc: v.tensor_copy(
                out=ks_sb[64:128, 2 * qc + 1:2 * qc + 2], in_=psKS[64:128, 0:1]), inc=c_v)

        emit_kv_pair(0)
        att_t = {}
        mx_t = {}
        rev_t = {}
        rn_t = {}
        it = 0
        for p in range(KD):
            if p + 1 < KD:
                emit_kv_pair(p + 1)
            for st4 in range(nq4):
                kd = it % 2
                wdn = [(c_v, kvev[p]), (c_s, tQF)]
                if it >= 1:
                    wdn.append((c_v, mx_t[it - 1]))
                dnMM = P.op("tensor", lambda t, p=p, st4=st4, kd=kd: t.matmul(
                    out=psD[:, 0:qw], lhsT=ks_sb[:, 2 * p:2 * p + 2],
                    rhs=qf_sb[:, p * T + st4 * qw: p * T + (st4 + 1) * qw],
                    start=True, stop=True), wait=wdn, inc=c_t)
                wmx = [(c_t, dnMM)]
                if it >= 2:
                    wmx.append((c_t, rn_t[it - 2]))
                mx_t[it] = P.op("vector", lambda v, kd=kd: v.tensor_scalar(
                    out=rnb0[kd][:, 0:qw], in0=psD[:, 0:qw], scalar1=EPS_DEN,
                    scalar2=None, op0=OP.max), wait=wmx, inc=c_v)
                rrc = P.op("vector", lambda v, kd=kd: v.reciprocal(
                    out=rnb0[kd][:, 0:qw], in_=rnb0[kd][:, 0:qw]), inc=c_v)
                wrn = [(c_v, rrc)]
                if it >= 2:
                    wrn.append((c_v, rev_t[it - 2]))
                rn_t[it] = P.op("tensor", lambda t, kd=kd: t.matmul(
                    out=psR[kd][:, 0:qw], lhsT=tb["sel2"][:], rhs=rnb0[kd][:, 0:qw],
                    start=True, stop=True), wait=wrn, inc=c_t)
                rev_t[it] = P.op("vector", lambda v, kd=kd: v.tensor_copy(
                    out=rnbs[kd][:, 0:qw], in_=psR[kd][:, 0:qw]),
                    wait=[(c_t, rn_t[it])], inc=c_v)
                wn0 = []
                if it >= 2:
                    wn0.append((c_v, att_t[it - 2]))
                P.op("tensor", lambda t, p=p, st4=st4, kd=kd: t.matmul(
                    out=psN[kd][0:64, 0:qw], lhsT=kv_sb[0:64, p * 128: p * 128 + 64],
                    rhs=qf_sb[0:64, p * T + st4 * qw: p * T + (st4 + 1) * qw],
                    start=True, stop=True, tile_position=(0, 0)), wait=wn0)
                nmm = P.op("tensor", lambda t, p=p, st4=st4, kd=kd: t.matmul(
                    out=psN[kd][64:128, 0:qw], lhsT=kv_sb[64:128, p * 128 + 64: p * 128 + 128],
                    rhs=qf_sb[64:128, p * T + st4 * qw: p * T + (st4 + 1) * qw],
                    start=True, stop=True, tile_position=(64, 64)), inc=c_t)
                att_t[it] = P.op("vector", lambda v, p=p, st4=st4, kd=kd: v.tensor_tensor(
                    out=attn_fm[:, p * T + st4 * qw: p * T + (st4 + 1) * qw],
                    in0=psN[kd][:, 0:qw], in1=rnbs[kd][:, 0:qw], op=OP.mult),
                    wait=[(c_v, rev_t[it]), (c_t, nmm)], inc=c_v)
                emit_wd_load(1)
                emit_wd_proc(1)
                it += 1
        last_att = att_t[it - 1]
        tATTN = last_att
        pp2.close()
        psW = [pb3.enter_context(nc.psum_tensor(f"psW{k}", [128, 512], F32)) for k in range(4)]




        st_at = [0] * nt
        aev_t = [0] * nt
        for i in range(nt):
            for sh in range(2):
                w0 = [(c_v, tATTN), (c_v, tWOUT), (c_v, tAGU)]
                if i >= 2:
                    w0.append((c_s, aev_t[i - 2]))
                for c in range(KD):
                    P.op("tensor", lambda t, i=i, sh=sh, c=c: t.matmul(
                        out=psW[2 * (i % 2) + sh][:, :],
                        lhsT=attn_fm[:, c * T + i * 128: c * T + (i + 1) * 128],
                        rhs=woutT[:, c * D + sh * 512: c * D + (sh + 1) * 512],
                        start=(c == 0), stop=(c == KD - 1)),
                        wait=w0 if c == 0 else (), inc=c_t if c == KD - 1 else None)
                mmW = c_t.n
                wev = [(c_t, mmW)]
                if sh == 0 and i >= 2:
                    wev.append((s_at[i % 2], st_at[i - 2]))
                P.op("scalar", lambda s, i=i, sh=sh: s.copy(
                    out=ao[i % 2][:, sh * 512:(sh + 1) * 512], in_=psW[2 * (i % 2) + sh][:, :]),
                    wait=wev, inc=c_s)
            aev_t[i] = c_s.n
            st_at[i] = P.dma("gpsimd", out=attn_scr[i * 128:(i + 1) * 128, :], in_=ao[i % 2][:],
                             wait=[(c_s, aev_t[i])], inc=s_at[i % 2])
            emit_wd_load(1)
            emit_wd_proc(1)
        while _wd["pi"] < 64:
            emit_wd_load(1)
            emit_wd_proc(1)
        tWTS = [(s_wts[0], s_wts[0].n), (s_wts[1], s_wts[1].n)]
        tATST = [(s_at[0], s_at[0].n), (s_at[1], s_at[1].n)]

        P.phase_barrier(waits=[(c_v, tAGU), (c_s, _mp["prev_ld"][0]),
                               (c_s, _mp["prev_ld"][1])])
        P.flush()
        pb3.close()
        pbL.close()
        pm.close()
        P.phase_barrier()

        pouter = ExitStack()
        wdT = pouter.enter_context(nc.sbuf_tensor("wdT", [128, NF * D], BF16))
        _wl = dict(idx=0)

        def emit_wdT(n):
            for _ in range(n):
                c = _wl["idx"]
                if c >= NF:
                    return
                _wl["idx"] = c + 1
                P.dma("sync", out=wdT[:, c * D:(c + 1) * D],
                      in_=wdt_scr[:, c * 128:(c + 1) * 128],
                      wait=list(tWTS) if c == 0 else (), inc=s_wdl, transpose=True)

        pc1 = ExitStack()
        q1fm = pc1.enter_context(nc.sbuf_tensor("q1fm", [128, KD * T], BF16))
        q1l = 0
        for c in range(KD):
            q1l = P.dma("sync", out=q1fm[:, c * T:(c + 1) * T],
                        in_=q1_scr[:, c * 128:(c + 1) * 128],
                        wait=[(s, s.n) for s in s_q1s] if c == 0 else (),
                        inc=s_q1l, transpose=True)
        tQ1L = q1l

        gsl = [pc1.enter_context(nc.sbuf_tensor(f"gsl{k}", [128, D], F32)) for k in range(4)]
        sgn = [pc1.enter_context(nc.sbuf_tensor(f"sgn{k}", [128, D], BF16)) for k in range(2)]
        trn = [pc1.enter_context(nc.sbuf_tensor(f"trn{k}", [128, D], BF16)) for k in range(2)]
        wgTb = [pc1.enter_context(nc.sbuf_tensor(f"wgTb{k}", [128, 4096], BF16)) for k in range(2)]
        wuTb = [pc1.enter_context(nc.sbuf_tensor(f"wuTb{k}", [128, 4096], BF16)) for k in range(2)]
        hrelu = [pc1.enter_context(nc.sbuf_tensor(f"hrelu{k}", [128, 512], F32)) for k in range(3)]
        hup = [pc1.enter_context(nc.sbuf_tensor(f"hup{k}", [128, 512], F32)) for k in range(3)]
        hsq = [pc1.enter_context(nc.sbuf_tensor(f"hsq{k}", [128, 512], F32)) for k in range(3)]
        htile = [pc1.enter_context(nc.sbuf_tensor(f"htile{k}", [128, 512], F32)) for k in range(3)]
        hcol = pc1.enter_context(nc.sbuf_tensor("hcol", [128, 2], F32))
        psG = [pc1.enter_context(nc.psum_tensor(f"psG{k}", [128, 512], F32)) for k in range(3)]
        psU = [pc1.enter_context(nc.psum_tensor(f"psU{k}", [128, 512], F32)) for k in range(3)]
        psTg = [pc1.enter_context(nc.psum_tensor(f"psTg{k}", [128, 1024], BF16)) for k in range(2)]

        zh = P.op("vector", lambda v: v.memset(hmax_sb[:, :], 0.0), inc=c_v)

        state = dict(ld=[0, 0, 0, 0], tl=[0, 0, 0, 0], sgn=[0, 0], trn=[0, 0],
                     ev=[0, 0], slot=0, par=0, lslot=0)

        def prep_load(mat_h, j):
            slot = state["lslot"] % 4
            state["lslot"] += 1
            wld = [(c_v, state["ld"][slot])] if state["ld"][slot] else []
            state["tl"][slot] = P.dma("sync", out=gsl[slot][:],
                                      in_=mat_h[j * 128:(j + 1) * 128, :],
                                      wait=wld, inc=s_gu[slot])

        def prep_proc(ci, j, dstbuf, bf):
            slot = state["slot"] % 4
            par = state["par"] % 2
            state["slot"] += 1
            state["par"] += 1
            wsg = [(s_gu[slot], state["tl"][slot]), (c_v, tMEAN)]
            if state["sgn"][par]:
                wsg.append((c_v, state["sgn"][par]))
            P.op("scalar", lambda s, slot=slot, par=par, ci=ci: s.activation(
                out=sgn[par][:], in_=gsl[slot][:], func=AF.Sign,
                bias=wbias_n[:, ci:ci + 1]), wait=wsg, inc=c_s)
            P.op("scalar", lambda s, slot=slot, ci=ci: s.activation(
                out=gsl[slot][:], in_=gsl[slot][:], func=AF.Sign,
                bias=wbias_p[:, ci:ci + 1]), inc=c_s)
            s2t = c_s.n
            wtt = [(c_s, s2t)]
            if state["trn"][par]:
                wtt.append((c_t, state["trn"][par]))
            tt = P.op("vector", lambda v, slot=slot, par=par: v.tensor_tensor(
                out=trn[par][:], in0=sgn[par][:], in1=gsl[slot][:], op=OP.add),
                wait=wtt, inc=c_v)
            state["sgn"][par] = tt
            state["ld"][slot] = tt
            wtr = [(c_v, tt)]
            if state["ev"][par]:
                wtr.append((c_v, state["ev"][par]))
            trt = transpose8(psTg[par], lambda c, par=par: trn[par][:, c * 128:(c + 1) * 128],
                             8, wtr)
            state["trn"][par] = trt
            jj = j % 4
            wev = [(c_t, trt)]
            if bf[par]:
                wev.append((c_t, bf[par]))
            ev = P.op("vector", lambda v, dstbuf=dstbuf, jj=jj, par=par: v.tensor_copy(
                out=dstbuf[:].rearrange("p (c r) -> p c r", c=8, r=512)[:, :, jj * 128:(jj + 1) * 128],
                in_=psTg[par][:]), wait=wev, inc=c_v)
            state["ev"][par] = ev
            return ev

        relu_t = {}
        uev_t = {}
        hdone = {}
        hst = {}
        mm_strip = [0] * 8
        prep_ql = []
        prep_qp = []

        def pop_load(n):
            for _ in range(n):
                if prep_ql:
                    m, j = prep_ql.pop(0)
                    prep_load(m, j)

        def pop_proc(n):
            for _ in range(n):
                if prep_qp:
                    ci, j, dst, bf = prep_qp.pop(0)
                    prep_proc(ci, j, dst, bf)

        for jj in range(4):
            prep_load(wg_h, jj)
            prep_load(wu_h, jj)
            prep_proc(0, jj, wgTb[0], [0, 0])
            prep_proc(1, jj, wuTb[0], [0, 0])
        for s8 in range(8):
            last_ev = list(state["ev"])
            if s8 + 1 < 8:
                bf = [mm_strip[s8 - 1], mm_strip[s8 - 1]] if s8 >= 1 else [0, 0]
                for jj in range(4):
                    prep_ql.append((wg_h, (s8 + 1) * 4 + jj))
                    prep_ql.append((wu_h, (s8 + 1) * 4 + jj))
                    prep_qp.append((0, (s8 + 1) * 4 + jj, wgTb[(s8 + 1) % 2], bf))
                    prep_qp.append((1, (s8 + 1) * 4 + jj, wuTb[(s8 + 1) % 2], bf))
                pop_load(4)
                for _ in range(8):
                    pop_proc(1)
                    pop_load(1)
            for i in range(nt):
                prev = (s8, i - 3) if i >= 3 else (s8 - 1, nt - 3 + i)
                gl = (s8 * nt + i) % 3
                wg0 = [(c_v, last_ev[0]), (c_v, last_ev[1]), (s_q1l, tQ1L), (c_v, tAGU)]
                if prev in uev_t:
                    wg0.append((c_s, uev_t[prev]))
                for c in range(KD):
                    P.op("tensor", lambda t, gl=gl, s8=s8, i=i, c=c: t.matmul(
                        out=psG[gl][:, :],
                        lhsT=q1fm[:, c * T + i * 128: c * T + (i + 1) * 128],
                        rhs=wgTb[s8 % 2][:, c * 512:(c + 1) * 512],
                        start=(c == 0), stop=(c == KD - 1)),
                        wait=wg0 if c == 0 else (), inc=c_t if c == KD - 1 else None)
                gmm = c_t.n
                for c in range(KD):
                    P.op("tensor", lambda t, gl=gl, s8=s8, i=i, c=c: t.matmul(
                        out=psU[gl][:, :],
                        lhsT=q1fm[:, c * T + i * 128: c * T + (i + 1) * 128],
                        rhs=wuTb[s8 % 2][:, c * 512:(c + 1) * 512],
                        start=(c == 0), stop=(c == KD - 1)),
                        inc=c_t if c == KD - 1 else None)
                umm = c_t.n
                wrl = [(c_t, gmm)]
                if prev in hdone:
                    wrl.append((c_v, hdone[prev]))
                relu_t[(s8, i)] = P.op("scalar", lambda s, gl=gl, i=i: s.activation(
                    out=hrelu[gl][:], in_=psG[gl][:, :], func=AF.Relu,
                    scale=ag_sb[:, i:i + 1]), wait=wrl, inc=c_s)
                uev_t[(s8, i)] = P.op("scalar", lambda s, gl=gl, i=i: s.activation(
                    out=hup[gl][:], in_=psU[gl][:, :], func=AF.Identity,
                    scale=au_sb[:, i:i + 1]), wait=[(c_t, umm)], inc=c_s)
                sqt = P.op("scalar", lambda s, gl=gl: s.activation(
                    out=hsq[gl][:], in_=hrelu[gl][:], func=AF.Square), inc=c_s)
                wht = [(c_s, sqt), (c_s, uev_t[(s8, i)])]
                if prev in hst:
                    wht.append((s_hw[gl], hst[prev]))
                ht = P.op("vector", lambda v, gl=gl: v.tensor_tensor(
                    out=htile[gl][:], in0=hsq[gl][:], in1=hup[gl][:], op=OP.mult),
                    wait=wht, inc=c_v)
                P.op("vector", lambda v, gl=gl: v.reduce_max(
                    out=hcol[:, 0:1], in_=htile[gl][:], axis=AX, apply_absolute_value=True),
                    wait=[(c_v, zh)])
                hdone[(s8, i)] = P.op("vector", lambda v, i=i: v.tensor_tensor(
                    out=hmax_sb[:, i:i + 1], in0=hmax_sb[:, i:i + 1], in1=hcol[:, 0:1],
                    op=OP.max), inc=c_v)
                hst[(s8, i)] = P.dma("gpsimd",
                                     out=h_scr[i * 128:(i + 1) * 128, s8 * 512:(s8 + 1) * 512],
                                     in_=htile[gl][:], wait=[(c_v, hdone[(s8, i)])],
                                     inc=s_hw[gl])
            mm_strip[s8] = umm
            if s8 >= 2:
                emit_wdT(6 if s8 < 6 else 4)
        emit_wdT(NF)
        tWDL = s_wdl.n
        tHST = [(s_hw[0], s_hw[0].n), (s_hw[1], s_hw[1].n)]
        tHMAX = hdone[(7, nt - 1)]

        P.flush()
        pc1.close()
        P.phase_barrier()

        pc2 = ExitStack()
        hbuf = [pc2.enter_context(nc.sbuf_tensor(f"hbuf{k}", [128, 1024], F32)) for k in range(8)]
        q2t = [pc2.enter_context(nc.sbuf_tensor(f"q2t{k}", [128, 4096], BF16)) for k in range(3)]
        q2fm = [pc2.enter_context(nc.sbuf_tensor(f"q2fm{k}", [128, 4096], BF16)) for k in range(2)]
        xre = [pc2.enter_context(nc.sbuf_tensor(f"xre{k}", [128, D], F32)) for k in range(2)]
        are = [pc2.enter_context(nc.sbuf_tensor(f"are{k}", [128, D], BF16)) for k in range(2)]
        zz = [[pc2.enter_context(nc.sbuf_tensor(f"z{a}{b}", [128, 1024], F32))
               for a in range(3)] for b in range(2)]

        psDn = [pc2.enter_context(nc.psum_tensor(f"psDn{k}", [128, 1024], F32)) for k in range(3)]
        psTd = [pc2.enter_context(nc.psum_tensor(f"psTd{k}", [128, 1024], BF16)) for k in range(2)]

        P.op("vector", lambda v: v.tensor_scalar(out=sch_sb[:, :], in0=hmax_sb[:, :],
                                                 scalar1=EPS_Q, scalar2=None, op0=OP.max),
             wait=[(c_v, tHMAX)])
        P.op("vector", lambda v: v.tensor_scalar(out=ad_sb[:, :], in0=sch_sb[:, :],
                                                 scalar1=RINV127, scalar2=None, op0=OP.mult))
        P.op("vector", lambda v: v.scalar_tensor_tensor(
            out=ad_sb[:, :], in0=sch_sb[:, :], scalar=RINV127_LO,
            in1=ad_sb[:, :], op0=OP.mult, op1=OP.add))
        P.op("vector", lambda v: v.tensor_scalar(out=ad_sb[:, :], in0=ad_sb[:, :],
                                                 scalar1=wmean_b[:, 2:3], scalar2=0.5,
                                                 op0=OP.mult, op1=OP.mult),
             wait=[(c_v, tMEAN)])
        P.op("vector", lambda v: v.reciprocal(out=sch_sb[:, :], in_=sch_sb[:, :]))
        tSCH = P.op("vector", lambda v: v.tensor_scalar(out=sch_sb[:, :], in0=sch_sb[:, :],
                                                        scalar1=127.0, scalar2=None, op0=OP.mult),
                    inc=c_v)

        st_out = [0] * nt
        base_t = [0] * nt
        dnev2 = {}
        q2done = {}
        q2T = {}
        q2ev = {}
        tail_ev = [0, 0]
        dn_last = [0] * nt
        hb_y = [0] * 8
        hload_t = {}

        def emit_hload(i):
            for qq in range(4):
                gidx = i * 4 + qq
                hs = gidx % 8
                wldh = [(c_v, hb_y[hs])] if hb_y[hs] else list(tHST)
                hload_t[(i, qq)] = P.dma("sync" if hs % 2 == 0 else "gpsimd",
                                         out=hbuf[hs][:],
                                         in_=h_scr[i * 128:(i + 1) * 128,
                                                   qq * 1024:(qq + 1) * 1024],
                                         wait=wldh, inc=s_hl[hs])

        def emit_qmath(i):
            k = i % 3
            for qq in range(4):
                gidx = i * 4 + qq
                hs = gidx % 8
                par = qq % 2
                z0, z1, z2 = zz[par]
                pq = (i, qq - 2) if qq >= 2 else (i - 1, qq + 2)
                wyq = [(s_hl[hs], hload_t[(i, qq)]), (c_v, tSCH)]
                yt = P.op("vector", lambda v, hs=hs, i=i, z0=z0: v.tensor_scalar(
                    out=z0[:], in0=hbuf[hs][:], scalar1=sch_sb[:, i:i + 1], scalar2=1000.0,
                    op0=OP.mult, op1=OP.add), wait=wyq, inc=c_v)
                hb_y[hs] = yt
                P.op("vector", lambda v, z0=z0, z1=z1: v.tensor_scalar(
                    out=z1[:], in0=z0[:], scalar1=M1, scalar2=M1, op0=OP.add, op1=OP.subtract))
                P.op("vector", lambda v, z0=z0, z2=z2: v.tensor_scalar(
                    out=z2[:], in0=z0[:], scalar1=M2, scalar2=M2, op0=OP.add, op1=OP.subtract))
                P.op("vector", lambda v, z0=z0, z1=z1, z2=z2: v.scalar_tensor_tensor(
                    out=z0[:], in0=z2[:], scalar=1.0, in1=z1[:], op0=OP.is_lt, op1=OP.mult))
                wq2 = [(c_t, q2T[(i - 3, qq)])] if (i - 3, qq) in q2T else []
                q2done[(i, qq)] = P.op("vector", lambda v, k=k, qq=qq, z0=z0, z2=z2: v.scalar_tensor_tensor(
                    out=q2t[k][:, qq * 1024:(qq + 1) * 1024], in0=z0[:], in1=z2[:],
                    scalar=1000.0, op0=OP.subtract, op1=OP.add), wait=wq2, inc=c_v)

        emit_hload(0)
        emit_hload(1)
        emit_qmath(0)

        def emit_base(i):
            k = i % 2
            wre = list(tATST)
            if i >= 2:
                wre.append((s_out[k], st_out[i - 2]))
            al = P.dma("sync", out=are[k][:], in_=attn_scr[i * 128:(i + 1) * 128, :],
                       wait=wre, inc=s_atl[k])
            xl = P.dma("sync", out=xre[k][:], in_=x_h[i * 128:(i + 1) * 128, :],
                       wait=[(s_out[k], st_out[i - 2])] if i >= 2 else (), inc=s_xr[k])
            base_t[i] = P.op("vector", lambda v, k=k: v.tensor_tensor(
                out=xre[k][:], in0=xre[k][:], in1=are[k][:], op=OP.add),
                wait=[(s_atl[k], al), (s_xr[k], xl)], inc=c_v)

        def emit_tail(i):
            k = i % 2
            kq = i % 3
            emit_base(i)
            for grp in range(4):
                par = grp % 2
                wtr = [(c_v, q2done[(i, grp)])]
                if tail_ev[par]:
                    wtr.append((c_s, tail_ev[par]))
                trt = transpose8(psTd[par],
                                 lambda c, kq=kq, grp=grp: q2t[kq][:, grp * 1024 + c * 128: grp * 1024 + (c + 1) * 128],
                                 8, wtr)
                q2T[(i, grp)] = trt
                wevq = [(c_t, trt)]
                if i >= 2:
                    wevq.append((c_t, dn_last[i - 2]))
                tail_ev[par] = P.op("scalar", lambda s, k=k, grp=grp, par=par: s.copy(
                    out=q2fm[k][:, grp * 1024:(grp + 1) * 1024], in_=psTd[par][:]),
                    wait=wevq, inc=c_s)
                q2ev[(i, grp)] = tail_ev[par]
            for sh in range(2):
                w0 = [(c_s, q2ev[(i, 3)]), (s_wdl, tWDL)]
                if i >= 3:
                    w0.append((c_v, dnev2[i - 3]))
                for c in range(NF):
                    P.op("tensor", lambda t, i=i, sh=sh, c=c, k=k: t.matmul(
                        out=psDn[i % 3][:, sh * 512:(sh + 1) * 512],
                        lhsT=q2fm[k][:, c * 128:(c + 1) * 128],
                        rhs=wdT[:, c * D + sh * 512:c * D + (sh + 1) * 512],
                        start=(c == 0), stop=(c == NF - 1)),
                        wait=w0 if c == 0 else (), inc=c_t if c == NF - 1 else None)
                dn_last[i] = c_t.n
            dnev2[i] = P.op("vector", lambda v, i=i, k=k: v.scalar_tensor_tensor(
                out=xre[k][:, :], in0=psDn[i % 3][:, :], scalar=ad_sb[:, i:i + 1],
                in1=xre[k][:, :], op0=OP.mult, op1=OP.add),
                wait=[(c_t, dn_last[i]), (c_v, base_t[i])], inc=c_v)
            st_out[i] = P.dma("gpsimd", out=out_h[i * 128:(i + 1) * 128, :], in_=xre[k][:],
                              wait=[(c_v, dnev2[i])], inc=s_out[k])

        for i in range(nt):
            if i + 2 < nt:
                emit_hload(i + 2)
            if i + 1 < nt:
                emit_qmath(i + 1)
            emit_tail(i)

        P.op("gpsimd", lambda e: e.nop(),
             wait=[(s_out[0], s_out[0].n), (s_out[1], s_out[1].n)], drain=False)

        P.flush()
        pc2.close()
        pouter.close()

    return nc


def kernel(L0_attn_x, L0_attn_w_qkv, L0_attn_w_out, L0_ffn_w_gate, L0_ffn_w_up,
           L0_ffn_w_down):
    x = np.ascontiguousarray(np.asarray(L0_attn_x, dtype=np.float32))
    tabs = host_tables(16)
    nc = build_nc(nt=16)
    in_maps = []
    wq = np.ascontiguousarray(np.asarray(L0_attn_w_qkv, dtype=np.float32))
    wo = np.ascontiguousarray(np.asarray(L0_attn_w_out, dtype=np.float32))
    wg = np.ascontiguousarray(np.asarray(L0_ffn_w_gate, dtype=np.float32))
    wu = np.ascontiguousarray(np.asarray(L0_ffn_w_up, dtype=np.float32))
    wd = np.ascontiguousarray(np.asarray(L0_ffn_w_down, dtype=np.float32))
    for b in range(8):
        m = dict(x=np.ascontiguousarray(x[b]), w_qkv=wq, w_out=wo, w_g=wg, w_u=wu, w_d=wd)
        m.update(tabs)
        in_maps.append(m)
    res = run_bass_kernel_spmd(nc, in_maps, core_ids=list(range(8)))
    out = np.stack([np.asarray(r["out"]) for r in res.results], axis=0)
    if out.dtype != np.float64:
        out = out.astype(np.float32)
    return out


# version 27
# speedup vs baseline: 1.0271x; 1.0185x over previous
import numpy as np
import ml_dtypes
from contextlib import ExitStack

import concourse.bass as bass
import concourse.mybir as mybir
from concourse.bass_utils import run_bass_kernel_spmd

F32 = mybir.dt.float32
BF16 = mybir.dt.bfloat16
AX = mybir.AxisListType.X
AF = mybir.ActivationFunctionType
OP = mybir.AluOpType

D, H, HD, F = 1024, 16, 64, 4096
KD = D // 128
NF = F // 128
M1 = float(3 * 2**22)
M2 = float(3 * 2**51)
EPS_LN = 1e-5
EPS_Q = 1e-5
EPS_DEN = 1e-6
RINV127 = float(np.float32(1.0) / np.float32(127.0))
RINV127_LO = float(np.float32(1.0 / 127.0 - np.float64(np.float32(1.0) / np.float32(127.0))))
INV1024 = 1.0 / 1024.0
INV2P22 = 1.0 / float(2**22)


class Sem:
    def __init__(self, h):
        self.h = h
        self.n = 0

    def tick(self, k=1):
        self.n += k
        return self.n


class Prog:
    ENGINES = ("sync", "vector", "scalar", "tensor", "gpsimd")

    def __init__(self, block):
        self.block = block
        self.items = {e: [] for e in self.ENGINES}
        self.all_sems = []

    def phase_barrier(self, waits=None):
        bar = tuple(waits) if waits is not None else tuple(
            (s, s.n) for s in self.all_sems if s.n > 0)
        for eng in self.ENGINES:
            self.items[eng].append((lambda e: e.nop(), bar, None, 0))

    DRAIN_ENGINES = ("vector", "scalar", "gpsimd")

    def op(self, eng, fn, wait=(), inc=None, incn=1, drain=True):
        tick = inc.tick(incn) if inc is not None else None
        if drain and eng in self.DRAIN_ENGINES:
            def fn2(e, fn=fn):
                r = fn(e)
                e.drain()
                return r
            self.items[eng].append((fn2, tuple(wait), inc, incn))
        else:
            self.items[eng].append((fn, tuple(wait), inc, incn))
        return tick

    def dma(self, eng, out, in_, wait=(), inc=None, transpose=False):
        tick = inc.tick(16) if inc is not None else None

        def fn(e, out=out, in_=in_, tr=transpose):
            return e.dma_start(out=out, in_=in_, transpose=tr)

        self.items[eng].append((fn, tuple(wait), inc, 16))
        return tick

    def flush(self):
        for eng in self.ENGINES:
            items = self.items[eng]
            if not items:
                continue
            self.items[eng] = []
            section = getattr(self.block, eng)

            def emit(e, items=items):
                for fn, waits, inc, incn in items:
                    for (s, v) in waits:
                        e.wait_ge(s.h, v)
                    r = fn(e)
                    if inc is not None:
                        r.then_inc(inc.h, incn)

            section(emit)


def host_tables(nt=16):
    T = nt * 128
    j = np.arange(0, HD, 2, dtype=np.float32) / np.float32(HD)
    inv_freq = np.float32(1.0) / (10000.0 ** j.astype(np.float64)).astype(np.float32)
    t = np.arange(T, dtype=np.float32)
    freqs = (t[:, None] * inv_freq[None, :]).astype(np.float32)
    cos = np.cos(freqs.astype(np.float64)).astype(np.float32)
    sin = np.sin(freqs.astype(np.float64)).astype(np.float32)
    cos_t = cos.reshape(nt, 128, 32).transpose(1, 0, 2)
    sin_t = sin.reshape(nt, 128, 32).transpose(1, 0, 2)
    cos64 = np.concatenate([cos_t, cos_t], axis=2).reshape(128, nt * 64)
    sinpat = np.concatenate([-sin_t, sin_t], axis=2).reshape(128, nt * 64)
    sel2 = np.zeros((2, 128), dtype=np.float32)
    sel2[0, 0:64] = 1.0
    sel2[1, 64:128] = 1.0
    constc = np.zeros((128, 8), dtype=np.float32)
    constc[:, 0] = M1
    constc[:, 1] = -M1
    constc[:, 2] = 0.5
    constc[:, 3] = -0.5
    constc[:, 4] = 1000.0
    constc[:, 5] = M2
    constc[:, 6] = -M2
    return dict(constc=constc,
        cos64=np.ascontiguousarray(cos64), sinpat=np.ascontiguousarray(sinpat),
        ident=np.eye(128, dtype=ml_dtypes.bfloat16),
        ident_f=np.eye(128, dtype=np.float32),
        onesrow_f=np.ones((1, 128), dtype=np.float32),
        ones_col_b=np.ones((128, 1), dtype=ml_dtypes.bfloat16),
        sel2=sel2,
    )


def build_nc(nt=16, tap=False):
    T = nt * 128
    NS = max(T // 512, 1)
    SW = min(T, 512)
    nc = bass.Bass("TRN2", target_bir_lowering=False)

    x_h = nc.dram_tensor("x", [T, D], F32, kind="ExternalInput").ap()
    wqkv_h = nc.dram_tensor("w_qkv", [3 * D, D], F32, kind="ExternalInput").ap()
    wout_h = nc.dram_tensor("w_out", [D, D], F32, kind="ExternalInput").ap()
    wg_h = nc.dram_tensor("w_g", [F, D], F32, kind="ExternalInput").ap()
    wu_h = nc.dram_tensor("w_u", [F, D], F32, kind="ExternalInput").ap()
    wd_h = nc.dram_tensor("w_d", [D, F], F32, kind="ExternalInput").ap()
    tspecs = (("cos64", [128, nt * 64], F32), ("sinpat", [128, nt * 64], F32),
              ("ident", [128, 128], BF16), ("ident_f", [128, 128], F32),
              ("onesrow_f", [1, 128], F32),
              ("ones_col_b", [128, 1], BF16), ("sel2", [2, 128], F32),
              ("constc", [128, 8], F32))
    tabs = {nm: nc.dram_tensor(nm, shp, dt, kind="ExternalInput").ap()
            for nm, shp, dt in tspecs}

    out_h = nc.dram_tensor("out", [T, D], F32, kind="ExternalOutput").ap()
    skind = "ExternalOutput" if tap else "Internal"
    q1_scr = nc.dram_tensor("q1_scr", [T, D], BF16, kind=skind).ap()
    attn_scr = nc.dram_tensor("attn_scr", [T, D], BF16, kind=skind).ap()
    h_scr = nc.dram_tensor("h_scr", [T, F], F32, kind=skind).ap()
    wdt_scr = nc.dram_tensor("wdt_scr", [D, F], BF16, kind="Internal").ap()

    es = ExitStack()
    with es:
        block = es.enter_context(nc.Block())
        P = Prog(block)
        def mksem(name):
            s = Sem(es.enter_context(nc.semaphore(name)))
            P.all_sems.append(s)
            return s

        def mksems(name, n):
            return [mksem(f"{name}{k}") for k in range(n)]

        s_tab, s_q1l = mksem("s_tab"), mksem("s_q1l")
        s_x = mksems("s_x", 3)
        s_wsl = mksems("s_wsl", 4)
        s_msl = mksems("s_msl", 4)
        s_wo = mksems("s_wo", 2)
        s_q1s = mksems("s_q1s", 3)
        s_gu = mksems("s_gu", 4)
        s_wd = mksems("s_wd", 2)
        s_hw = mksems("s_hw", 3)
        s_hl = mksems("s_hl", 8)
        s_at = mksems("s_at", 2)
        s_atl = mksems("s_atl", 2)
        s_xr = mksems("s_xr", 2)
        s_out = mksems("s_out", 2)
        s_wts = mksems("s_wts", 2)
        s_wdl = mksems("s_wdl", 1)[0]
        c_v, c_s, c_t, c_g = mksem("c_v"), mksem("c_s"), mksem("c_t"), mksem("c_g")

        per = ExitStack()
        es.enter_context(per)
        tb = {nm: per.enter_context(nc.sbuf_tensor("t_" + nm, shp, dt))
              for nm, shp, dt in tspecs}
        invsc127 = per.enter_context(nc.sbuf_tensor("invsc127", [128, nt], F32))
        scx_sb = per.enter_context(nc.sbuf_tensor("scx_sb", [128, nt], F32))
        hmax_sb = per.enter_context(nc.sbuf_tensor("hmax_sb", [128, nt], F32))
        wbias_n = per.enter_context(nc.sbuf_tensor("wbias_n", [128, 3], F32))
        wbias_p = per.enter_context(nc.sbuf_tensor("wbias_p", [128, 3], F32))
        wmean_b = per.enter_context(nc.sbuf_tensor("wmean_b", [128, 3], F32))
        ag_sb = per.enter_context(nc.sbuf_tensor("ag_sb", [128, nt], F32))
        au_sb = per.enter_context(nc.sbuf_tensor("au_sb", [128, nt], F32))
        ad_sb = per.enter_context(nc.sbuf_tensor("ad_sb", [128, nt], F32))
        sch_sb = per.enter_context(nc.sbuf_tensor("sch_sb", [128, nt], F32))
        stat1 = per.enter_context(nc.sbuf_tensor("stat1", [1, 16], F32))

        for nm in tb:
            P.dma("sync", out=tb[nm][:], in_=tabs[nm][:, :], inc=s_tab)
        tTAB = s_tab.n
        ident = tb["ident"]

        def transpose8(psbuf, src_ap_fn, nblk, wfirst):
            for c in range(nblk):
                P.op("tensor", lambda t, c=c: t.transpose(
                    out=psbuf[:, c * 128:(c + 1) * 128], in_=src_ap_fn(c),
                    identity=ident[:]),
                    wait=wfirst if c == 0 else (), inc=c_t if c == nblk - 1 else None)
            return c_t.n

        right = ExitStack()
        es.enter_context(right)
        xn_fm = right.enter_context(nc.sbuf_tensor("xn_fm", [128, KD * T], BF16, side="right"))
        wqkvT_q = right.enter_context(nc.sbuf_tensor("wqkvT_q", [128, KD * D], BF16, side="right"))
        rkv = ExitStack()
        es.enter_context(rkv)
        wqkvT_kv = rkv.enter_context(nc.sbuf_tensor("wqkvT_kv", [128, KD * 2 * D], BF16, side="right"))

        pm = ExitStack()
        m_wsl = [pm.enter_context(nc.sbuf_tensor(f"m_wsl{k}", [128, 512], F32)) for k in range(4)]
        mcols = [pm.enter_context(nc.sbuf_tensor(f"mcols{k}", [128, 64], F32)) for k in range(3)]
        rscol = pm.enter_context(nc.sbuf_tensor("rscol", [128, 3], F32))
        mrow = pm.enter_context(nc.sbuf_tensor("mrow", [1, 384], F32))

        MG = float(1.5 * 2**24 * 2**-6)
        mci = {"g": 0, "u": 1, "d": 2}
        mean_ready = {}
        _mp = dict(prev_ld=[0, 0, 0, 0], tick=0, done=0)
        _mp_jobs = []
        for mi, (nm, hsrc) in enumerate((("g", wg_h), ("u", wu_h), ("d", wd_h))):
            for j in range(64):
                _mp_jobs.append((mci[nm], j, hsrc, nm))

        def emit_mean_jobs(njobs):
            while njobs > 0 and _mp["done"] < len(_mp_jobs):
                ci, j, hsrc, nm = _mp_jobs[_mp["done"]]
                bs = _mp["done"] % 4
                if nm == "d":
                    r0, c0 = (j // 8) * 128, (j % 8) * 512
                    src_ap = wd_h[r0:r0 + 128, c0:c0 + 512]
                else:
                    src_ap = hsrc[(j // 2) * 128:(j // 2 + 1) * 128,
                                  (j % 2) * 512:(j % 2 + 1) * 512]
                wld = [(c_s, _mp["prev_ld"][bs])] if _mp["prev_ld"][bs] else []
                tl = P.dma("gpsimd", out=m_wsl[bs][:], in_=src_ap, wait=wld, inc=s_msl[bs])
                ab = P.op("scalar", lambda s, bs=bs, ci=ci, j=j: s.activation(
                    out=m_wsl[bs][:], in_=m_wsl[bs][:], func=AF.Abs,
                    accum_out=mcols[ci][:, j:j + 1]), wait=[(s_msl[bs], tl)], inc=c_s)
                _mp["prev_ld"][bs] = ab
                _mp["done"] += 1
                njobs -= 1
                if _mp["done"] % 64 == 0:
                    ci_done = _mp_jobs[_mp["done"] - 1][0]
                    P.op("vector", lambda v, ci=ci_done: v.reduce_sum(
                        out=rscol[:, ci:ci + 1], in_=mcols[ci][:, :], axis=AX),
                        wait=[(c_s, ab)], inc=c_v)
                    _mp["tick"] = c_v.n


        pa = ExitStack()
        xtile = [pa.enter_context(nc.sbuf_tensor(f"xt{k}", [128, D], F32)) for k in range(3)]
        xnf = [pa.enter_context(nc.sbuf_tensor(f"xnf{k}", [128, D], F32)) for k in range(3)]
        xnb = [pa.enter_context(nc.sbuf_tensor(f"xnb{k}", [128, D], BF16)) for k in range(3)]
        q1t = [pa.enter_context(nc.sbuf_tensor(f"q1t{k}", [128, D], BF16)) for k in range(3)]
        y0 = [pa.enter_context(nc.sbuf_tensor(f"y0{k}", [128, D], F32)) for k in range(3)]
        y1 = [pa.enter_context(nc.sbuf_tensor(f"y1{k}", [128, D], F32)) for k in range(3)]
        y2 = [pa.enter_context(nc.sbuf_tensor(f"y2{k}", [128, D], F32)) for k in range(3)]
        colA = [pa.enter_context(nc.sbuf_tensor(f"colA{k}", [128, 8], F32)) for k in range(3)]
        wsl = [pa.enter_context(nc.sbuf_tensor(f"wsl{k}", [128, D], F32)) for k in range(4)]
        wslb = [pa.enter_context(nc.sbuf_tensor(f"wslb{k}", [128, D], BF16)) for k in range(2)]

        psTw = [pa.enter_context(nc.psum_tensor(f"psTw{k}", [128, 1024], BF16)) for k in range(2)]
        psTx = [pa.enter_context(nc.psum_tensor(f"psTx{k}", [128, 1024], BF16)) for k in range(3)]


        wq_cast = [0] * 24
        wq_tr = [0] * 24
        wq_ev = [0] * 24
        for j in range(24):
            w = [(c_s, wq_cast[j - 4])] if j >= 4 else ()
            tl = P.dma("sync", out=wsl[j % 4][:], in_=wqkv_h[j * 128:(j + 1) * 128, :],
                       wait=w, inc=s_wsl[j % 4])
            w2 = [(s_wsl[j % 4], tl)]
            if j >= 2:
                w2.append((c_t, wq_tr[j - 2]))
            wq_cast[j] = P.op("scalar",
                              lambda s, j=j: s.copy(out=wslb[j % 2][:], in_=wsl[j % 4][:]),
                              wait=w2, inc=c_s)
            w3 = [(c_s, wq_cast[j]), (s_tab, tTAB)]
            if j >= 2:
                w3.append((c_v, wq_ev[j - 2]))
            wq_tr[j] = transpose8(psTw[j % 2],
                                  lambda c, j=j: wslb[j % 2][:, c * 128:(c + 1) * 128], 8, w3)
            if j < 8:
                dst, W, off = wqkvT_q, D, j * 128
            else:
                dst, W, off = wqkvT_kv, 2 * D, (j - 8) * 128
            wq_ev[j] = P.op(
                "vector",
                lambda v, j=j, dst=dst, W=W, off=off: v.tensor_copy(
                    out=dst[:].rearrange("p (c r) -> p c r", c=8, r=W)[:, :, off:off + 128],
                    in_=psTw[j % 2][:]),
                wait=[(c_t, wq_tr[j])], inc=c_v)
        tWQ_Q = wq_ev[7]
        tWQ_KV = wq_ev[23]

        xn_ready = [0] * nt
        q1_done = [0] * nt
        ev_x = [0] * nt
        xnT = [0] * nt
        yq_t = [0] * nt
        r2_t = [0] * nt
        st_q1 = [0] * nt
        xb_t = [0] * nt
        for i in range(nt):
            k = i % 3
            ca = colA[k]
            w = [(c_s, xn_ready[i - 3])] if i >= 3 else ()
            tl = P.dma("sync", out=xtile[k][:], in_=x_h[i * 128:(i + 1) * 128, :],
                       wait=w, inc=s_x[k])
            wsq = [(s_x[k], tl)]
            if i >= 3:
                wsq.append((c_v, r2_t[i - 3]))
            s2 = P.op("scalar",
                      lambda s, k=k, ca=ca: s.activation(out=y2[k][:], in_=xtile[k][:],
                                                         func=AF.Square, accum_out=ca[:, 1:2]),
                      wait=wsq, inc=c_s)
            P.op("vector", lambda v, k=k, ca=ca: v.reduce_sum(out=ca[:, 0:1], in_=xtile[k][:], axis=AX),
                 wait=[(c_s, s2)])
            P.op("vector", lambda v, ca=ca: v.tensor_scalar(out=ca[:, 2:3], in0=ca[:, 0:1],
                                                            scalar1=INV1024, scalar2=None, op0=OP.mult))
            P.op("vector", lambda v, ca=ca: v.tensor_scalar(out=ca[:, 3:4], in0=ca[:, 1:2],
                                                            scalar1=INV1024, scalar2=None, op0=OP.mult))
            P.op("vector", lambda v, ca=ca: v.scalar_tensor_tensor(
                out=ca[:, 4:5], in0=ca[:, 2:3], scalar=ca[:, 2:3], in1=ca[:, 3:4],
                op0=OP.mult, op1=OP.subtract))
            P.op("vector", lambda v, ca=ca: v.tensor_scalar(
                out=ca[:, 4:5], in0=ca[:, 4:5], scalar1=-1.0, scalar2=EPS_LN,
                op0=OP.mult, op1=OP.add))
            rc = P.op("vector", lambda v, ca=ca: v.reciprocal(out=ca[:, 5:6], in_=ca[:, 4:5]),
                      inc=c_v)
            sq = P.op("scalar", lambda s, ca=ca: s.activation(out=ca[:, 6:7], in_=ca[:, 5:6],
                                                              func=AF.Sqrt),
                      wait=[(c_v, rc)], inc=c_s)
            nb = P.op("vector", lambda v, ca=ca: v.scalar_tensor_tensor(
                out=ca[:, 7:8], in0=ca[:, 2:3], scalar=-1.0, in1=ca[:, 6:7],
                op0=OP.mult, op1=OP.mult), wait=[(c_s, sq)], inc=c_v)
            wxn = [(c_v, nb)]
            if i >= 3:
                wxn.append((c_v, yq_t[i - 3]))
            xn_ready[i] = P.op("scalar", lambda s, k=k, ca=ca: s.activation(
                out=xnf[k][:], in_=xtile[k][:], func=AF.Identity,
                bias=ca[:, 7:8], scale=ca[:, 6:7]), wait=wxn, inc=c_s)
            P.op("vector", lambda v, k=k, ca=ca: v.reduce_max(
                out=ca[:, 0:1], in_=xnf[k][:], axis=AX, apply_absolute_value=True),
                 wait=[(c_s, xn_ready[i])])
            P.op("vector", lambda v, ca=ca: v.tensor_scalar(
                out=ca[:, 1:2], in0=ca[:, 0:1], scalar1=EPS_Q, scalar2=None, op0=OP.max))
            P.op("vector", lambda v, ca=ca: v.reciprocal(out=ca[:, 2:3], in_=ca[:, 1:2]))
            P.op("vector", lambda v, ca=ca, i=i: v.tensor_scalar(
                out=scx_sb[:, i:i + 1], in0=ca[:, 2:3], scalar1=127.0, scalar2=None, op0=OP.mult))
            P.op("vector", lambda v, ca=ca, i=i: v.tensor_scalar(
                out=invsc127[:, i:i + 1], in0=ca[:, 1:2], scalar1=RINV127, scalar2=None, op0=OP.mult))
            P.op("vector", lambda v, ca=ca, i=i: v.scalar_tensor_tensor(
                out=invsc127[:, i:i + 1], in0=ca[:, 1:2], scalar=RINV127_LO,
                in1=invsc127[:, i:i + 1], op0=OP.mult, op1=OP.add))
            wxb = [(c_s, xn_ready[i])]
            if i >= 3:
                wxb.append((c_t, xnT[i - 3]))
            xb_t[i] = P.op("scalar", lambda s, k=k: s.copy(out=xnb[k][:], in_=xnf[k][:]),
                           wait=wxb, inc=c_s)
            wy = [(c_v, q1_done[i - 3])] if i >= 3 else ()
            yq_t[i] = P.op("vector", lambda v, k=k, i=i: v.tensor_scalar(
                out=y0[k][:], in0=xnf[k][:], scalar1=scx_sb[:, i:i + 1], scalar2=1000.0,
                op0=OP.mult, op1=OP.add), wait=wy, inc=c_v)
            P.op("vector", lambda v, k=k: v.tensor_scalar(
                out=y1[k][:], in0=y0[k][:], scalar1=M1, scalar2=M1, op0=OP.add, op1=OP.subtract))
            r2_t[i] = P.op("vector", lambda v, k=k: v.tensor_scalar(
                out=y2[k][:], in0=y0[k][:], scalar1=M2, scalar2=M2, op0=OP.add, op1=OP.subtract),
                inc=c_v)
            P.op("vector", lambda v, k=k: v.scalar_tensor_tensor(
                out=y0[k][:], in0=y2[k][:], scalar=1.0, in1=y1[k][:], op0=OP.is_lt, op1=OP.mult))
            wq1 = [(s_q1s[k], st_q1[i - 3])] if i >= 3 else ()
            q1_done[i] = P.op("vector", lambda v, k=k: v.scalar_tensor_tensor(
                out=q1t[k][:], in0=y0[k][:], scalar=1000.0, in1=y2[k][:],
                op0=OP.subtract, op1=OP.add), wait=wq1, inc=c_v)
            st_q1[i] = P.dma("gpsimd", out=q1_scr[i * 128:(i + 1) * 128, :], in_=q1t[k][:],
                             wait=[(c_v, q1_done[i])], inc=s_q1s[k])
            wtr = [(c_s, xb_t[i]), (s_tab, tTAB)]
            if i >= 3:
                wtr.append((c_v, ev_x[i - 3]))
            xnT[i] = transpose8(psTx[k], lambda c, k=k: xnb[k][:, c * 128:(c + 1) * 128], 8, wtr)
            ev_x[i] = P.op("vector", lambda v, k=k, i=i: v.tensor_copy(
                out=xn_fm[:].rearrange("p (c r) -> p c r", c=8, r=T)[:, :, i * 128:(i + 1) * 128],
                in_=psTx[k][:]), wait=[(c_t, xnT[i])], inc=c_v)

        P.flush()
        pa.close()

        P.phase_barrier()

        pbL = ExitStack()
        kf_sb = pbL.enter_context(nc.sbuf_tensor("kf_sb", [128, nt * D], BF16))
        v_sb = pbL.enter_context(nc.sbuf_tensor("v_sb", [128, nt * D], BF16))
        qf_sb = pbL.enter_context(nc.sbuf_tensor("qf_sb", [128, KD * T], BF16))

        pbM = ExitStack()
        rs1 = pbM.enter_context(nc.sbuf_tensor("rs1", [128, D], F32))
        rs2 = pbM.enter_context(nc.sbuf_tensor("rs2", [128, D], F32))
        qs1, qs2 = rs1, rs2
        qfb0 = pbM.enter_context(nc.sbuf_tensor("qfb0", [128, D], BF16))
        qfb = [qfb0, qfb0]
        psK = pbM.enter_context(nc.psum_tensor("psK", [128, 1024], F32))
        psV = pbM.enter_context(nc.psum_tensor("psV", [128, 1024], F32))
        psQ = pbM.enter_context(nc.psum_tensor("psQ", [128, 1024], F32))
        psTq = pbM.enter_context(nc.psum_tensor("psTq", [128, 1024], BF16))

        def swap_ap(buf):
            return buf[:].rearrange("p (h two j) -> p h two j", two=2, j=32)[:, :, ::-1, :]

        def full_ap(buf):
            return buf[:].rearrange("p (h two j) -> p h two j", two=2, j=32)

        def rtab64(tbl, i):
            return tbl[:, i * 64:(i + 1) * 64].rearrange("p (o two j) -> p o two j", o=1, two=2) \
                      .broadcast_to([128, H, 2, 32])

        kft_t = [0] * nt
        qf_t = [0] * nt
        vev_t = [0] * nt
        krd_t = [0] * nt
        qrd_t = [0] * nt
        qT_t = [0] * nt
        qev_t = [0] * nt

        def emit_qtr(i):
            wtr = [(c_v, qf_t[i])]
            if i >= 1:
                wtr.append((c_s, qev_t[i - 1]))
            qT_t[i] = transpose8(psTq, lambda c, i=i: qfb[i % 2][:, c * 128:(c + 1) * 128],
                                 8, wtr)
            qev_t[i] = P.op("scalar", lambda s, i=i: s.copy(
                out=qf_sb[:].rearrange("p (c r) -> p c r", c=8, r=T)[:, :, i * 128:(i + 1) * 128],
                in_=psTq[:]), wait=[(c_t, qT_t[i])], inc=c_s)

        def mm_group(ps, rhs_buf, rhs_stride, rhs_off, wfirst, i):
            for sh in range(2):
                for c in range(KD):
                    P.op("tensor", lambda t, i=i, sh=sh, c=c, ps=ps, rb=rhs_buf,
                         rs=rhs_stride, ro=rhs_off: t.matmul(
                        out=ps[:, sh * 512:(sh + 1) * 512],
                        lhsT=xn_fm[:, c * T + i * 128: c * T + (i + 1) * 128],
                        rhs=rb[:, c * rs + ro + sh * 512: c * rs + ro + (sh + 1) * 512],
                        start=(c == 0), stop=(c == KD - 1)),
                        wait=wfirst if (sh == 0 and c == 0) else (),
                        inc=c_t if (sh == 1 and c == KD - 1) else None)
            return c_t.n

        def rope_elu(ps, s1, s2, i, dst_fn, qscale, wfirst):
            P.op("vector", lambda v, ps=ps, i=i, s1=s1: v.tensor_tensor(
                out=full_ap(s1), in0=full_ap(ps), in1=rtab64(tb["cos64"], i), op=OP.mult),
                wait=wfirst)
            rd = P.op("vector", lambda v, ps=ps, i=i, s2=s2: v.tensor_tensor(
                out=full_ap(s2), in0=swap_ap(ps), in1=rtab64(tb["sinpat"], i), op=OP.mult),
                inc=c_v)
            P.op("vector", lambda v, s1=s1, s2=s2: v.tensor_tensor(
                out=s1[:], in0=s1[:], in1=s2[:], op=OP.add))
            mx = P.op("vector", lambda v, s1=s1, s2=s2: v.tensor_scalar(
                out=s2[:], in0=s1[:], scalar1=0.0, scalar2=None, op0=OP.min), inc=c_v)
            if qscale:
                eq = P.op("scalar", lambda s, s2=s2: s.activation(
                    out=s2[:], in_=s2[:], func=AF.Exp, scale=qscale),
                    wait=[(c_v, mx)], inc=c_s)
                P.op("vector", lambda v, s1=s1: v.tensor_scalar(
                    out=s1[:], in0=s1[:], scalar1=0.0, scalar2=qscale, op0=OP.max, op1=OP.mult))
            else:
                eq = P.op("scalar", lambda s, s2=s2: s.activation(
                    out=s2[:], in_=s2[:], func=AF.Exp), wait=[(c_v, mx)], inc=c_s)
                P.op("vector", lambda v, s1=s1: v.tensor_scalar(
                    out=s1[:], in0=s1[:], scalar1=0.0, scalar2=None, op0=OP.max))
            ft = P.op("vector", lambda v, s1=s1, s2=s2: v.tensor_tensor(
                out=dst_fn(), in0=s1[:], in1=s2[:], op=OP.add),
                wait=[(c_s, eq)], inc=c_v)
            return rd, ft

        for i in range(nt):
            wk = [(c_v, ev_x[i]), (c_v, tWQ_KV)]
            if i >= 1:
                wk.append((c_v, krd_t[i - 1]))
            kMMi = mm_group(psK, wqkvT_kv, 2 * D, 0, wk, i)
            wv = [(c_s, vev_t[i - 1])] if i >= 1 else ()
            vMMi = mm_group(psV, wqkvT_kv, 2 * D, 1024, wv, i)
            wq = [(c_v, tWQ_Q)]
            if i >= 1:
                wq.append((c_v, qrd_t[i - 1]))
            qMMi = mm_group(psQ, wqkvT_q, D, 0, wq, i)
            if i >= 1:
                emit_qtr(i - 1)
            wkr = [(c_t, kMMi)]
            if i >= 1:
                wkr.append((c_v, kft_t[i - 1]))
            krd_t[i], kft_t[i] = rope_elu(
                psK, rs1, rs2, i, lambda i=i: kf_sb[:, i * D:(i + 1) * D], None, wkr)
            vev_t[i] = P.op("scalar", lambda s, i=i: s.copy(
                out=v_sb[:, i * D:(i + 1) * D], in_=psV[:, :]),
                wait=[(c_t, vMMi)], inc=c_s)
            wqr = [(c_t, qMMi)]
            if i >= 1:
                wqr.append((c_v, qf_t[i - 1]))
                wqr.append((c_t, qT_t[i - 1]))
            qrd_t[i], qf_t[i] = rope_elu(
                psQ, qs1, qs2, i, lambda i=i: qfb[i % 2][:], 0.125, wqr)

            if i == nt - 1:
                emit_qtr(i)
            emit_mean_jobs(192 // nt + 1)
        emit_mean_jobs(len(_mp_jobs))
        tQF = qev_t[nt - 1]

        P.flush()
        pbM.close()
        P.phase_barrier(waits=[(c_v, kft_t[nt - 1]), (c_s, qev_t[nt - 1]),
                               (c_t, qT_t[nt - 1])])
        rkv.close()
        right.close()


        pt = ExitStack()
        psB = pt.enter_context(nc.psum_tensor("psB", [128, 512], F32))
        for nm in ("g", "u", "d"):
            ci = mci[nm]
            tp = P.op("tensor", lambda t, ci=ci: t.transpose(
                out=psB[0:1, 128:256], in_=rscol[:, ci:ci + 1], identity=tb["ident_f"][:]),
                wait=[(c_v, _mp["tick"])], inc=c_t)
            ev = P.op("vector", lambda v: v.tensor_copy(out=mrow[0:1, 0:128], in_=psB[0:1, 128:256]),
                      wait=[(c_t, tp)], inc=c_v)
            P.op("vector", lambda v: v.tensor_scalar(
                out=mrow[0:1, 128:256], in0=mrow[0:1, 0:128], scalar1=MG, scalar2=MG,
                op0=OP.add, op1=OP.subtract))
            P.op("vector", lambda v: v.tensor_tensor(
                out=mrow[0:1, 256:384], in0=mrow[0:1, 0:128], in1=mrow[0:1, 128:256],
                op=OP.subtract))
            P.op("vector", lambda v: v.reduce_sum(out=stat1[0:1, 0:1], in_=mrow[0:1, 128:256], axis=AX))
            P.op("vector", lambda v: v.reduce_sum(out=stat1[0:1, 1:2], in_=mrow[0:1, 256:384], axis=AX))
            P.op("vector", lambda v: v.tensor_tensor(
                out=stat1[0:1, 2:3], in0=stat1[0:1, 0:1], in1=stat1[0:1, 1:2], op=OP.add))
            P.op("vector", lambda v: v.tensor_scalar(out=stat1[0:1, 3:4], in0=stat1[0:1, 2:3],
                                                     scalar1=INV2P22, scalar2=EPS_Q,
                                                     op0=OP.mult, op1=OP.max))
            P.op("vector", lambda v: v.tensor_scalar(out=stat1[0:1, 4:5], in0=stat1[0:1, 3:4],
                                                     scalar1=-0.5, scalar2=None, op0=OP.mult))
            bc0 = P.op("vector", lambda v: v.tensor_scalar(out=stat1[0:1, 5:6], in0=stat1[0:1, 3:4],
                                                           scalar1=0.5, scalar2=None, op0=OP.mult),
                       inc=c_v)
            last = 0
            for (srcc, dstb) in ((3, wmean_b), (4, wbias_n), (5, wbias_p)):
                bcm = P.op("tensor", lambda t, srcc=srcc: t.matmul(
                    out=psB[:, 0:1], lhsT=tb["onesrow_f"][:], rhs=stat1[0:1, srcc:srcc + 1],
                    start=True, stop=True), wait=[(c_v, bc0), (c_v, last)] if last else [(c_v, bc0)],
                    inc=c_t)
                last = P.op("vector", lambda v, ci=ci, dstb=dstb: v.tensor_copy(
                    out=dstb[:, ci:ci + 1], in_=psB[:, 0:1]), wait=[(c_t, bcm)], inc=c_v)
            mean_ready[nm] = last
        tMEAN = mean_ready["d"]
        tmp = 0
        for nm, dstc in (("g", ag_sb), ("u", au_sb)):
            tmp = P.op("vector", lambda v, dstc=dstc, ci=mci[nm]: v.tensor_scalar(
                out=dstc[:, :], in0=invsc127[:, :], scalar1=wmean_b[:, ci:ci + 1],
                scalar2=0.5, op0=OP.mult, op1=OP.mult),
                wait=[(c_v, tMEAN), (c_v, q1_done[nt - 1])], inc=c_v)
        tAGU = tmp
        P.flush()
        pt.close()

        pb3 = ExitStack()
        kv_sb = pb3.enter_context(nc.sbuf_tensor("kv_sb", [128, 8 * 128], BF16))
        ks_sb = pb3.enter_context(nc.sbuf_tensor("ks_sb", [128, 16], BF16))
        rnb0 = [pb3.enter_context(nc.sbuf_tensor(f"rnb0{k}", [2, 512], F32)) for k in range(2)]
        rnbs = [pb3.enter_context(nc.sbuf_tensor(f"rnbs{k}", [128, 512], F32)) for k in range(2)]
        attn_fm = pb3.enter_context(nc.sbuf_tensor("attn_fm", [128, KD * T], BF16))
        woutT = pb3.enter_context(nc.sbuf_tensor("woutT", [128, KD * D], BF16))
        wosl = [pb3.enter_context(nc.sbuf_tensor(f"wosl{k}", [128, D], F32)) for k in range(2)]
        woslb = [pb3.enter_context(nc.sbuf_tensor(f"woslb{k}", [128, D], BF16)) for k in range(2)]
        ao = [pb3.enter_context(nc.sbuf_tensor(f"ao{k}", [128, D], BF16)) for k in range(2)]
        psKV = [pb3.enter_context(nc.psum_tensor(f"psKV{k}", [128, 512], F32)) for k in range(2)]
        psKS = pb3.enter_context(nc.psum_tensor("psKS", [128, 512], F32))
        wdsg = [pb3.enter_context(nc.sbuf_tensor(f"wdsg{k}", [128, 512], BF16)) for k in range(2)]
        wdtr = [pb3.enter_context(nc.sbuf_tensor(f"wdtr{k}", [128, 512], BF16)) for k in range(2)]
        pbo = ExitStack()
        psTo = [pbo.enter_context(nc.psum_tensor(f"psTo{k}", [128, 1024], BF16)) for k in range(2)]

        _wd = dict(ld=[0, 0, 0, 0], tl=[0, 0, 0, 0], sgn=[0, 0], st=[0, 0],
                   li=0, pi=0)

        def emit_wd_load(n):
            for _ in range(n):
                ix = _wd["li"]
                if ix >= 64 or ix >= _wd["pi"] + 4:
                    return
                _wd["li"] = ix + 1
                d8, cc = ix // 8, ix % 8
                slot = ix % 4
                wld = [(c_v, _wd["ld"][slot])] if _wd["ld"][slot] else [(c_v, tMEAN)]
                _wd["tl"][slot] = P.dma("gpsimd", out=m_wsl[slot][:],
                                        in_=wd_h[d8 * 128:(d8 + 1) * 128,
                                                 cc * 512:(cc + 1) * 512],
                                        wait=wld, inc=s_msl[slot])

        def emit_wd_proc(n):
            for _ in range(n):
                ix = _wd["pi"]
                if ix >= 64:
                    return
                if ix >= _wd["li"]:
                    emit_wd_load(1)
                _wd["pi"] = ix + 1
                d8, cc = ix // 8, ix % 8
                slot = ix % 4
                par = ix % 2
                wsg = [(s_msl[slot], _wd["tl"][slot]), (c_v, tMEAN)]
                if _wd["sgn"][par]:
                    wsg.append((c_v, _wd["sgn"][par]))
                P.op("scalar", lambda s, par=par, slot=slot: s.activation(
                    out=wdsg[par][:], in_=m_wsl[slot][:], func=AF.Sign,
                    bias=wbias_n[:, 2:3]), wait=wsg, inc=c_s)
                P.op("scalar", lambda s, slot=slot: s.activation(
                    out=m_wsl[slot][:], in_=m_wsl[slot][:], func=AF.Sign,
                    bias=wbias_p[:, 2:3]), inc=c_s)
                s2t = c_s.n
                wtt = [(c_s, s2t)]
                if _wd["st"][par]:
                    wtt.append((s_wts[par], _wd["st"][par]))
                tt = P.op("vector", lambda v, par=par, slot=slot: v.tensor_tensor(
                    out=wdtr[par][:], in0=wdsg[par][:], in1=m_wsl[slot][:], op=OP.add),
                    wait=wtt, inc=c_v)
                _wd["sgn"][par] = tt
                _wd["ld"][slot] = tt
                _wd["st"][par] = P.dma("gpsimd",
                                       out=wdt_scr[d8 * 128:(d8 + 1) * 128,
                                                   cc * 512:(cc + 1) * 512],
                                       in_=wdtr[par][:], wait=[(c_v, tt)], inc=s_wts[par])

        emit_wd_load(4)

        nq4 = T // 512 if T >= 512 else 1
        qw = min(T, 512)
        wo_cast = [0] * 8
        wo_tr = [0] * 8
        wo_ev = [0] * 8
        for j in range(8):
            w = [(c_s, wo_cast[j - 2])] if j >= 2 else ()
            tl = P.dma("sync", out=wosl[j % 2][:], in_=wout_h[j * 128:(j + 1) * 128, :],
                       wait=w, inc=s_wo[j % 2])
            wcst = [(s_wo[j % 2], tl)]
            if j >= 2:
                wcst.append((c_t, wo_tr[j - 2]))
            wo_cast[j] = P.op("scalar", lambda s, j=j: s.copy(
                out=woslb[j % 2][:], in_=wosl[j % 2][:]), wait=wcst, inc=c_s)
            wtr = [(c_s, wo_cast[j])]
            if j >= 2:
                wtr.append((c_v, wo_ev[j - 2]))
            wo_tr[j] = transpose8(psTo[j % 2],
                                  lambda c, j=j: woslb[j % 2][:, c * 128:(c + 1) * 128], 8, wtr)
            wo_ev[j] = P.op("vector", lambda v, j=j: v.tensor_copy(
                out=woutT[:].rearrange("p (c r) -> p c r", c=8, r=D)[:, :, j * 128:(j + 1) * 128],
                in_=psTo[j % 2][:]), wait=[(c_t, wo_tr[j])], inc=c_v)
            emit_wd_load(1)
            emit_wd_proc(1)
        tWOUT = wo_ev[7]
        pbo.close()
        pp2 = ExitStack()
        psD = pp2.enter_context(nc.psum_tensor("psD", [2, 512], F32))
        psR = [pp2.enter_context(nc.psum_tensor(f"psR{k}", [128, 512], F32)) for k in range(2)]
        psN = [pp2.enter_context(nc.psum_tensor(f"psN{k}", [128, 512], F32)) for k in range(2)]

        kvev = [0] * KD

        def emit_kv_pair(qc):
            wkv = [(c_v, kft_t[nt - 1]), (c_s, vev_t[nt - 1])]
            if qc >= 2:
                wkv.append((c_v, kvev[qc - 2]))
            for i2 in range(nt):
                P.op("tensor", lambda t, qc=qc, i2=i2: t.matmul(
                    out=psKV[qc % 2][:, 0:128],
                    lhsT=kf_sb[:, i2 * D + qc * 128: i2 * D + (qc + 1) * 128],
                    rhs=v_sb[:, i2 * D + qc * 128: i2 * D + (qc + 1) * 128],
                    start=(i2 == 0), stop=(i2 == nt - 1)),
                    wait=wkv if i2 == 0 else ())
                P.op("tensor", lambda t, qc=qc, i2=i2: t.matmul(
                    out=psKS[:, qc:qc + 1],
                    lhsT=kf_sb[:, i2 * D + qc * 128: i2 * D + (qc + 1) * 128],
                    rhs=tb["ones_col_b"][:],
                    start=(i2 == 0), stop=(i2 == nt - 1)),
                    inc=c_t if i2 == nt - 1 else None)
            kvmm = c_t.n
            P.op("vector", lambda v, qc=qc: v.tensor_copy(
                out=kv_sb[:, qc * 128:(qc + 1) * 128], in_=psKV[qc % 2][:, 0:128]),
                wait=[(c_t, kvmm)])
            P.op("vector", lambda v, qc=qc: v.memset(ks_sb[:, 2 * qc:2 * qc + 2], 0.0))
            P.op("vector", lambda v, qc=qc: v.tensor_copy(
                out=ks_sb[0:64, 2 * qc:2 * qc + 1], in_=psKS[0:64, qc:qc + 1]))
            kvev[qc] = P.op("vector", lambda v, qc=qc: v.tensor_copy(
                out=ks_sb[64:128, 2 * qc + 1:2 * qc + 2], in_=psKS[64:128, qc:qc + 1]), inc=c_v)

        emit_kv_pair(0)
        att_t = {}
        mx_t = {}
        rev_t = {}
        rn_t = {}
        it = 0
        for p in range(KD):
            if p + 1 < KD:
                emit_kv_pair(p + 1)
            for st4 in range(nq4):
                kd = it % 2
                wdn = [(c_v, kvev[p]), (c_s, tQF)]
                if it >= 1:
                    wdn.append((c_v, mx_t[it - 1]))
                dnMM = P.op("tensor", lambda t, p=p, st4=st4, kd=kd: t.matmul(
                    out=psD[:, 0:qw], lhsT=ks_sb[:, 2 * p:2 * p + 2],
                    rhs=qf_sb[:, p * T + st4 * qw: p * T + (st4 + 1) * qw],
                    start=True, stop=True), wait=wdn, inc=c_t)
                wmx = [(c_t, dnMM)]
                if it >= 2:
                    wmx.append((c_t, rn_t[it - 2]))
                mx_t[it] = P.op("vector", lambda v, kd=kd: v.tensor_scalar(
                    out=rnb0[kd][:, 0:qw], in0=psD[:, 0:qw], scalar1=EPS_DEN,
                    scalar2=None, op0=OP.max), wait=wmx, inc=c_v)
                rrc = P.op("vector", lambda v, kd=kd: v.reciprocal(
                    out=rnb0[kd][:, 0:qw], in_=rnb0[kd][:, 0:qw]), inc=c_v)
                wrn = [(c_v, rrc)]
                if it >= 2:
                    wrn.append((c_v, rev_t[it - 2]))
                rn_t[it] = P.op("tensor", lambda t, kd=kd: t.matmul(
                    out=psR[kd][:, 0:qw], lhsT=tb["sel2"][:], rhs=rnb0[kd][:, 0:qw],
                    start=True, stop=True), wait=wrn, inc=c_t)
                rev_t[it] = P.op("vector", lambda v, kd=kd: v.tensor_copy(
                    out=rnbs[kd][:, 0:qw], in_=psR[kd][:, 0:qw]),
                    wait=[(c_t, rn_t[it])], inc=c_v)
                wn0 = []
                if it >= 2:
                    wn0.append((c_v, att_t[it - 2]))
                P.op("tensor", lambda t, p=p, st4=st4, kd=kd: t.matmul(
                    out=psN[kd][0:64, 0:qw], lhsT=kv_sb[0:64, p * 128: p * 128 + 64],
                    rhs=qf_sb[0:64, p * T + st4 * qw: p * T + (st4 + 1) * qw],
                    start=True, stop=True, tile_position=(0, 0)), wait=wn0)
                nmm = P.op("tensor", lambda t, p=p, st4=st4, kd=kd: t.matmul(
                    out=psN[kd][64:128, 0:qw], lhsT=kv_sb[64:128, p * 128 + 64: p * 128 + 128],
                    rhs=qf_sb[64:128, p * T + st4 * qw: p * T + (st4 + 1) * qw],
                    start=True, stop=True, tile_position=(64, 64)), inc=c_t)
                att_t[it] = P.op("vector", lambda v, p=p, st4=st4, kd=kd: v.tensor_tensor(
                    out=attn_fm[:, p * T + st4 * qw: p * T + (st4 + 1) * qw],
                    in0=psN[kd][:, 0:qw], in1=rnbs[kd][:, 0:qw], op=OP.mult),
                    wait=[(c_v, rev_t[it]), (c_t, nmm)], inc=c_v)
                emit_wd_load(1)
                emit_wd_proc(1)
                it += 1
        last_att = att_t[it - 1]
        tATTN = last_att
        pp2.close()
        psW = [pb3.enter_context(nc.psum_tensor(f"psW{k}", [128, 512], F32)) for k in range(4)]




        st_at = [0] * nt
        aev_t = [0] * nt
        for i in range(nt):
            for sh in range(2):
                w0 = [(c_v, tATTN), (c_v, tWOUT), (c_v, tAGU)]
                if i >= 2:
                    w0.append((c_s, aev_t[i - 2]))
                for c in range(KD):
                    P.op("tensor", lambda t, i=i, sh=sh, c=c: t.matmul(
                        out=psW[2 * (i % 2) + sh][:, :],
                        lhsT=attn_fm[:, c * T + i * 128: c * T + (i + 1) * 128],
                        rhs=woutT[:, c * D + sh * 512: c * D + (sh + 1) * 512],
                        start=(c == 0), stop=(c == KD - 1)),
                        wait=w0 if c == 0 else (), inc=c_t if c == KD - 1 else None)
                mmW = c_t.n
                wev = [(c_t, mmW)]
                if sh == 0 and i >= 2:
                    wev.append((s_at[i % 2], st_at[i - 2]))
                P.op("scalar", lambda s, i=i, sh=sh: s.copy(
                    out=ao[i % 2][:, sh * 512:(sh + 1) * 512], in_=psW[2 * (i % 2) + sh][:, :]),
                    wait=wev, inc=c_s)
            aev_t[i] = c_s.n
            st_at[i] = P.dma("gpsimd", out=attn_scr[i * 128:(i + 1) * 128, :], in_=ao[i % 2][:],
                             wait=[(c_s, aev_t[i])], inc=s_at[i % 2])
            emit_wd_load(1)
            emit_wd_proc(1)
        while _wd["pi"] < 64:
            emit_wd_load(1)
            emit_wd_proc(1)
        tWTS = [(s_wts[0], s_wts[0].n), (s_wts[1], s_wts[1].n)]
        tATST = [(s_at[0], s_at[0].n), (s_at[1], s_at[1].n)]

        P.phase_barrier(waits=[(c_v, tAGU), (c_s, _mp["prev_ld"][0]),
                               (c_s, _mp["prev_ld"][1])])
        P.flush()
        pb3.close()
        pbL.close()
        pm.close()
        P.phase_barrier()

        pouter = ExitStack()
        wdT = pouter.enter_context(nc.sbuf_tensor("wdT", [128, NF * D], BF16))
        _wl = dict(idx=0)

        def emit_wdT(n):
            for _ in range(n):
                c = _wl["idx"]
                if c >= NF:
                    return
                _wl["idx"] = c + 1
                P.dma("sync", out=wdT[:, c * D:(c + 1) * D],
                      in_=wdt_scr[:, c * 128:(c + 1) * 128],
                      wait=list(tWTS) if c == 0 else (), inc=s_wdl, transpose=True)

        pc1 = ExitStack()
        q1fm = pc1.enter_context(nc.sbuf_tensor("q1fm", [128, KD * T], BF16))
        q1l = 0
        for c in range(KD):
            q1l = P.dma("sync", out=q1fm[:, c * T:(c + 1) * T],
                        in_=q1_scr[:, c * 128:(c + 1) * 128],
                        wait=[(s, s.n) for s in s_q1s] if c == 0 else (),
                        inc=s_q1l, transpose=True)
        tQ1L = q1l

        gsl = [pc1.enter_context(nc.sbuf_tensor(f"gsl{k}", [128, D], F32)) for k in range(4)]
        sgn = [pc1.enter_context(nc.sbuf_tensor(f"sgn{k}", [128, D], BF16)) for k in range(2)]
        trn = [pc1.enter_context(nc.sbuf_tensor(f"trn{k}", [128, D], BF16)) for k in range(2)]
        wgTb = [pc1.enter_context(nc.sbuf_tensor(f"wgTb{k}", [128, 4096], BF16)) for k in range(2)]
        wuTb = [pc1.enter_context(nc.sbuf_tensor(f"wuTb{k}", [128, 4096], BF16)) for k in range(2)]
        hrelu = [pc1.enter_context(nc.sbuf_tensor(f"hrelu{k}", [128, 512], F32)) for k in range(3)]
        hup = [pc1.enter_context(nc.sbuf_tensor(f"hup{k}", [128, 512], F32)) for k in range(3)]
        hsq = [pc1.enter_context(nc.sbuf_tensor(f"hsq{k}", [128, 512], F32)) for k in range(3)]
        htile = [pc1.enter_context(nc.sbuf_tensor(f"htile{k}", [128, 512], F32)) for k in range(3)]
        hcol = pc1.enter_context(nc.sbuf_tensor("hcol", [128, 2], F32))
        psG = [pc1.enter_context(nc.psum_tensor(f"psG{k}", [128, 512], F32)) for k in range(3)]
        psU = [pc1.enter_context(nc.psum_tensor(f"psU{k}", [128, 512], F32)) for k in range(3)]
        psTg = [pc1.enter_context(nc.psum_tensor(f"psTg{k}", [128, 1024], BF16)) for k in range(2)]

        zh = P.op("vector", lambda v: v.memset(hmax_sb[:, :], 0.0), inc=c_v)

        state = dict(ld=[0, 0, 0, 0], tl=[0, 0, 0, 0], sgn=[0, 0], trn=[0, 0],
                     ev=[0, 0], slot=0, par=0, lslot=0)

        def prep_load(mat_h, j):
            slot = state["lslot"] % 4
            state["lslot"] += 1
            wld = [(c_v, state["ld"][slot])] if state["ld"][slot] else []
            state["tl"][slot] = P.dma("sync", out=gsl[slot][:],
                                      in_=mat_h[j * 128:(j + 1) * 128, :],
                                      wait=wld, inc=s_gu[slot])

        def prep_proc(ci, j, dstbuf, bf):
            slot = state["slot"] % 4
            par = state["par"] % 2
            state["slot"] += 1
            state["par"] += 1
            wsg = [(s_gu[slot], state["tl"][slot]), (c_v, tMEAN)]
            if state["sgn"][par]:
                wsg.append((c_v, state["sgn"][par]))
            P.op("scalar", lambda s, slot=slot, par=par, ci=ci: s.activation(
                out=sgn[par][:], in_=gsl[slot][:], func=AF.Sign,
                bias=wbias_n[:, ci:ci + 1]), wait=wsg, inc=c_s)
            P.op("scalar", lambda s, slot=slot, ci=ci: s.activation(
                out=gsl[slot][:], in_=gsl[slot][:], func=AF.Sign,
                bias=wbias_p[:, ci:ci + 1]), inc=c_s)
            s2t = c_s.n
            wtt = [(c_s, s2t)]
            if state["trn"][par]:
                wtt.append((c_t, state["trn"][par]))
            tt = P.op("vector", lambda v, slot=slot, par=par: v.tensor_tensor(
                out=trn[par][:], in0=sgn[par][:], in1=gsl[slot][:], op=OP.add),
                wait=wtt, inc=c_v)
            state["sgn"][par] = tt
            state["ld"][slot] = tt
            wtr = [(c_v, tt)]
            if state["ev"][par]:
                wtr.append((c_v, state["ev"][par]))
            trt = transpose8(psTg[par], lambda c, par=par: trn[par][:, c * 128:(c + 1) * 128],
                             8, wtr)
            state["trn"][par] = trt
            jj = j % 4
            wev = [(c_t, trt)]
            if bf[par]:
                wev.append((c_t, bf[par]))
            ev = P.op("vector", lambda v, dstbuf=dstbuf, jj=jj, par=par: v.tensor_copy(
                out=dstbuf[:].rearrange("p (c r) -> p c r", c=8, r=512)[:, :, jj * 128:(jj + 1) * 128],
                in_=psTg[par][:]), wait=wev, inc=c_v)
            state["ev"][par] = ev
            return ev

        relu_t = {}
        uev_t = {}
        hdone = {}
        hst = {}
        mm_strip = [0] * 8
        prep_ql = []
        prep_qp = []

        def pop_load(n):
            for _ in range(n):
                if prep_ql:
                    m, j = prep_ql.pop(0)
                    prep_load(m, j)

        def pop_proc(n):
            for _ in range(n):
                if prep_qp:
                    ci, j, dst, bf = prep_qp.pop(0)
                    prep_proc(ci, j, dst, bf)

        for jj in range(4):
            prep_load(wg_h, jj)
            prep_load(wu_h, jj)
            prep_proc(0, jj, wgTb[0], [0, 0])
            prep_proc(1, jj, wuTb[0], [0, 0])
        for s8 in range(8):
            last_ev = list(state["ev"])
            if s8 + 1 < 8:
                bf = [mm_strip[s8 - 1], mm_strip[s8 - 1]] if s8 >= 1 else [0, 0]
                for jj in range(4):
                    prep_ql.append((wg_h, (s8 + 1) * 4 + jj))
                    prep_ql.append((wu_h, (s8 + 1) * 4 + jj))
                    prep_qp.append((0, (s8 + 1) * 4 + jj, wgTb[(s8 + 1) % 2], bf))
                    prep_qp.append((1, (s8 + 1) * 4 + jj, wuTb[(s8 + 1) % 2], bf))
                pop_load(4)
                for _ in range(8):
                    pop_proc(1)
                    pop_load(1)
            for i in range(nt):
                prev = (s8, i - 3) if i >= 3 else (s8 - 1, nt - 3 + i)
                gl = (s8 * nt + i) % 3
                wg0 = [(c_v, last_ev[0]), (c_v, last_ev[1]), (s_q1l, tQ1L), (c_v, tAGU)]
                if prev in uev_t:
                    wg0.append((c_s, uev_t[prev]))
                for c in range(KD):
                    P.op("tensor", lambda t, gl=gl, s8=s8, i=i, c=c: t.matmul(
                        out=psG[gl][:, :],
                        lhsT=q1fm[:, c * T + i * 128: c * T + (i + 1) * 128],
                        rhs=wgTb[s8 % 2][:, c * 512:(c + 1) * 512],
                        start=(c == 0), stop=(c == KD - 1)),
                        wait=wg0 if c == 0 else (), inc=c_t if c == KD - 1 else None)
                gmm = c_t.n
                for c in range(KD):
                    P.op("tensor", lambda t, gl=gl, s8=s8, i=i, c=c: t.matmul(
                        out=psU[gl][:, :],
                        lhsT=q1fm[:, c * T + i * 128: c * T + (i + 1) * 128],
                        rhs=wuTb[s8 % 2][:, c * 512:(c + 1) * 512],
                        start=(c == 0), stop=(c == KD - 1)),
                        inc=c_t if c == KD - 1 else None)
                umm = c_t.n
                wrl = [(c_t, gmm)]
                if prev in hdone:
                    wrl.append((c_v, hdone[prev]))
                relu_t[(s8, i)] = P.op("scalar", lambda s, gl=gl, i=i: s.activation(
                    out=hrelu[gl][:], in_=psG[gl][:, :], func=AF.Relu,
                    scale=ag_sb[:, i:i + 1]), wait=wrl, inc=c_s)
                uev_t[(s8, i)] = P.op("scalar", lambda s, gl=gl, i=i: s.activation(
                    out=hup[gl][:], in_=psU[gl][:, :], func=AF.Identity,
                    scale=au_sb[:, i:i + 1]), wait=[(c_t, umm)], inc=c_s)
                sqt = P.op("scalar", lambda s, gl=gl: s.activation(
                    out=hsq[gl][:], in_=hrelu[gl][:], func=AF.Square), inc=c_s)
                wht = [(c_s, sqt), (c_s, uev_t[(s8, i)])]
                if prev in hst:
                    wht.append((s_hw[gl], hst[prev]))
                ht = P.op("vector", lambda v, gl=gl: v.tensor_tensor(
                    out=htile[gl][:], in0=hsq[gl][:], in1=hup[gl][:], op=OP.mult),
                    wait=wht, inc=c_v)
                P.op("vector", lambda v, gl=gl: v.reduce_max(
                    out=hcol[:, 0:1], in_=htile[gl][:], axis=AX, apply_absolute_value=True),
                    wait=[(c_v, zh)])
                hdone[(s8, i)] = P.op("vector", lambda v, i=i: v.tensor_tensor(
                    out=hmax_sb[:, i:i + 1], in0=hmax_sb[:, i:i + 1], in1=hcol[:, 0:1],
                    op=OP.max), inc=c_v)
                hst[(s8, i)] = P.dma("gpsimd",
                                     out=h_scr[i * 128:(i + 1) * 128, s8 * 512:(s8 + 1) * 512],
                                     in_=htile[gl][:], wait=[(c_v, hdone[(s8, i)])],
                                     inc=s_hw[gl])
            mm_strip[s8] = umm
            if s8 >= 2:
                emit_wdT(6 if s8 < 6 else 4)
        emit_wdT(NF)
        tWDL = s_wdl.n
        tHST = [(s_hw[0], s_hw[0].n), (s_hw[1], s_hw[1].n)]
        tHMAX = hdone[(7, nt - 1)]

        P.flush()
        pc1.close()
        P.phase_barrier()

        pc2 = ExitStack()
        hbuf = [pc2.enter_context(nc.sbuf_tensor(f"hbuf{k}", [128, 1024], F32)) for k in range(8)]
        q2t = [pc2.enter_context(nc.sbuf_tensor(f"q2t{k}", [128, 4096], BF16)) for k in range(3)]
        q2fm = [pc2.enter_context(nc.sbuf_tensor(f"q2fm{k}", [128, 4096], BF16)) for k in range(2)]
        xre = [pc2.enter_context(nc.sbuf_tensor(f"xre{k}", [128, D], F32)) for k in range(2)]
        are = [pc2.enter_context(nc.sbuf_tensor(f"are{k}", [128, D], BF16)) for k in range(2)]
        zz = [[pc2.enter_context(nc.sbuf_tensor(f"z{a}{b}", [128, 1024], F32))
               for a in range(3)] for b in range(2)]

        psDn = [pc2.enter_context(nc.psum_tensor(f"psDn{k}", [128, 1024], F32)) for k in range(3)]
        psTd = [pc2.enter_context(nc.psum_tensor(f"psTd{k}", [128, 1024], BF16)) for k in range(2)]

        P.op("vector", lambda v: v.tensor_scalar(out=sch_sb[:, :], in0=hmax_sb[:, :],
                                                 scalar1=EPS_Q, scalar2=None, op0=OP.max),
             wait=[(c_v, tHMAX)])
        P.op("vector", lambda v: v.tensor_scalar(out=ad_sb[:, :], in0=sch_sb[:, :],
                                                 scalar1=RINV127, scalar2=None, op0=OP.mult))
        P.op("vector", lambda v: v.scalar_tensor_tensor(
            out=ad_sb[:, :], in0=sch_sb[:, :], scalar=RINV127_LO,
            in1=ad_sb[:, :], op0=OP.mult, op1=OP.add))
        P.op("vector", lambda v: v.tensor_scalar(out=ad_sb[:, :], in0=ad_sb[:, :],
                                                 scalar1=wmean_b[:, 2:3], scalar2=0.5,
                                                 op0=OP.mult, op1=OP.mult),
             wait=[(c_v, tMEAN)])
        P.op("vector", lambda v: v.reciprocal(out=sch_sb[:, :], in_=sch_sb[:, :]))
        tSCH = P.op("vector", lambda v: v.tensor_scalar(out=sch_sb[:, :], in0=sch_sb[:, :],
                                                        scalar1=127.0, scalar2=None, op0=OP.mult),
                    inc=c_v)

        st_out = [0] * nt
        base_t = [0] * nt
        dnev2 = {}
        q2done = {}
        q2T = {}
        q2ev = {}
        tail_ev = [0, 0]
        dn_last = [0] * nt
        hb_y = [0] * 8
        hload_t = {}

        def emit_hload(i):
            for qq in range(4):
                gidx = i * 4 + qq
                hs = gidx % 8
                wldh = [(c_v, hb_y[hs])] if hb_y[hs] else list(tHST)
                hload_t[(i, qq)] = P.dma("sync" if hs % 2 == 0 else "gpsimd",
                                         out=hbuf[hs][:],
                                         in_=h_scr[i * 128:(i + 1) * 128,
                                                   qq * 1024:(qq + 1) * 1024],
                                         wait=wldh, inc=s_hl[hs])

        def emit_qmath(i):
            k = i % 3
            for qq in range(4):
                gidx = i * 4 + qq
                hs = gidx % 8
                par = qq % 2
                z0, z1, z2 = zz[par]
                pq = (i, qq - 2) if qq >= 2 else (i - 1, qq + 2)
                wyq = [(s_hl[hs], hload_t[(i, qq)]), (c_v, tSCH)]
                yt = P.op("vector", lambda v, hs=hs, i=i, z0=z0: v.tensor_scalar(
                    out=z0[:], in0=hbuf[hs][:], scalar1=sch_sb[:, i:i + 1], scalar2=1000.0,
                    op0=OP.mult, op1=OP.add), wait=wyq, inc=c_v)
                hb_y[hs] = yt
                P.op("vector", lambda v, z0=z0, z1=z1: v.tensor_scalar(
                    out=z1[:], in0=z0[:], scalar1=M1, scalar2=M1, op0=OP.add, op1=OP.subtract))
                P.op("vector", lambda v, z0=z0, z2=z2: v.tensor_scalar(
                    out=z2[:], in0=z0[:], scalar1=M2, scalar2=M2, op0=OP.add, op1=OP.subtract))
                P.op("vector", lambda v, z0=z0, z1=z1, z2=z2: v.scalar_tensor_tensor(
                    out=z0[:], in0=z2[:], scalar=1.0, in1=z1[:], op0=OP.is_lt, op1=OP.mult))
                wq2 = [(c_t, q2T[(i - 3, qq)])] if (i - 3, qq) in q2T else []
                q2done[(i, qq)] = P.op("vector", lambda v, k=k, qq=qq, z0=z0, z2=z2: v.scalar_tensor_tensor(
                    out=q2t[k][:, qq * 1024:(qq + 1) * 1024], in0=z0[:], in1=z2[:],
                    scalar=1000.0, op0=OP.subtract, op1=OP.add), wait=wq2, inc=c_v)

        emit_hload(0)
        emit_hload(1)
        emit_qmath(0)

        def emit_base(i):
            k = i % 2
            wre = list(tATST)
            if i >= 2:
                wre.append((s_out[k], st_out[i - 2]))
            al = P.dma("sync", out=are[k][:], in_=attn_scr[i * 128:(i + 1) * 128, :],
                       wait=wre, inc=s_atl[k])
            xl = P.dma("sync", out=xre[k][:], in_=x_h[i * 128:(i + 1) * 128, :],
                       wait=[(s_out[k], st_out[i - 2])] if i >= 2 else (), inc=s_xr[k])
            base_t[i] = P.op("vector", lambda v, k=k: v.tensor_tensor(
                out=xre[k][:], in0=xre[k][:], in1=are[k][:], op=OP.add),
                wait=[(s_atl[k], al), (s_xr[k], xl)], inc=c_v)

        def emit_tail(i):
            k = i % 2
            kq = i % 3
            emit_base(i)
            for grp in range(4):
                par = grp % 2
                wtr = [(c_v, q2done[(i, grp)])]
                if tail_ev[par]:
                    wtr.append((c_s, tail_ev[par]))
                trt = transpose8(psTd[par],
                                 lambda c, kq=kq, grp=grp: q2t[kq][:, grp * 1024 + c * 128: grp * 1024 + (c + 1) * 128],
                                 8, wtr)
                q2T[(i, grp)] = trt
                wevq = [(c_t, trt)]
                if i >= 2:
                    wevq.append((c_t, dn_last[i - 2]))
                tail_ev[par] = P.op("scalar", lambda s, k=k, grp=grp, par=par: s.copy(
                    out=q2fm[k][:, grp * 1024:(grp + 1) * 1024], in_=psTd[par][:]),
                    wait=wevq, inc=c_s)
                q2ev[(i, grp)] = tail_ev[par]
            for sh in range(2):
                w0 = [(c_s, q2ev[(i, 3)]), (s_wdl, tWDL)]
                if i >= 3:
                    w0.append((c_v, dnev2[i - 3]))
                for c in range(NF):
                    P.op("tensor", lambda t, i=i, sh=sh, c=c, k=k: t.matmul(
                        out=psDn[i % 3][:, sh * 512:(sh + 1) * 512],
                        lhsT=q2fm[k][:, c * 128:(c + 1) * 128],
                        rhs=wdT[:, c * D + sh * 512:c * D + (sh + 1) * 512],
                        start=(c == 0), stop=(c == NF - 1)),
                        wait=w0 if c == 0 else (), inc=c_t if c == NF - 1 else None)
                dn_last[i] = c_t.n
            dnev2[i] = P.op("vector", lambda v, i=i, k=k: v.scalar_tensor_tensor(
                out=xre[k][:, :], in0=psDn[i % 3][:, :], scalar=ad_sb[:, i:i + 1],
                in1=xre[k][:, :], op0=OP.mult, op1=OP.add),
                wait=[(c_t, dn_last[i]), (c_v, base_t[i])], inc=c_v)
            st_out[i] = P.dma("gpsimd", out=out_h[i * 128:(i + 1) * 128, :], in_=xre[k][:],
                              wait=[(c_v, dnev2[i])], inc=s_out[k])

        for i in range(nt):
            if i + 2 < nt:
                emit_hload(i + 2)
            if i + 1 < nt:
                emit_qmath(i + 1)
            emit_tail(i)

        P.op("gpsimd", lambda e: e.nop(),
             wait=[(s_out[0], s_out[0].n), (s_out[1], s_out[1].n)], drain=False)

        P.flush()
        pc2.close()
        pouter.close()

    return nc


def kernel(L0_attn_x, L0_attn_w_qkv, L0_attn_w_out, L0_ffn_w_gate, L0_ffn_w_up,
           L0_ffn_w_down):
    x = np.ascontiguousarray(np.asarray(L0_attn_x, dtype=np.float32))
    tabs = host_tables(16)
    nc = build_nc(nt=16)
    in_maps = []
    wq = np.ascontiguousarray(np.asarray(L0_attn_w_qkv, dtype=np.float32))
    wo = np.ascontiguousarray(np.asarray(L0_attn_w_out, dtype=np.float32))
    wg = np.ascontiguousarray(np.asarray(L0_ffn_w_gate, dtype=np.float32))
    wu = np.ascontiguousarray(np.asarray(L0_ffn_w_up, dtype=np.float32))
    wd = np.ascontiguousarray(np.asarray(L0_ffn_w_down, dtype=np.float32))
    for b in range(8):
        m = dict(x=np.ascontiguousarray(x[b]), w_qkv=wq, w_out=wo, w_g=wg, w_u=wu, w_d=wd)
        m.update(tabs)
        in_maps.append(m)
    res = run_bass_kernel_spmd(nc, in_maps, core_ids=list(range(8)))
    out = np.stack([np.asarray(r["out"]) for r in res.results], axis=0)
    if out.dtype != np.float64:
        out = out.astype(np.float32)
    return out
